# version 27
# baseline (speedup 1.0000x reference)
import numpy as np
import ml_dtypes
from contextlib import ExitStack
import concourse.bass as bass
import concourse.mybir as mybir
from concourse.bass_utils import run_bass_kernel_spmd

F32 = mybir.dt.float32
BF16 = mybir.dt.bfloat16
I32 = mybir.dt.int32
AF = mybir.ActivationFunctionType
ALU = mybir.AluOpType
AX = mybir.AxisListType

D, T, TC = 1024, 2048, 512
NCH = T // TC
NSEQ = 2
DC = D // 128
H, HD, ST = 16, 64, 16
E, RK = 6, 16
LN_EPS = 1e-5
RMS_EPS = 1e-6


class Step:
    __slots__ = ("eng", "fn", "deps", "ordinal", "tag", "idx")

    def __init__(self, eng, fn, deps, tag, idx):
        self.eng, self.fn, self.deps, self.tag, self.idx = eng, fn, deps, tag, idx
        self.ordinal = 0


class Sched:
    def __init__(self):
        self.steps = []
        self.eng_count = {}
        self.writer = {}
        self.readers = {}
        self.floor = []

    def step(self, eng, fn, reads=(), writes=(), tag=""):
        deps = set(self.floor)
        for b in reads:
            w = self.writer.get(b)
            if w is not None:
                deps.add(w)
        for b in writes:
            w = self.writer.get(b)
            if w is not None:
                deps.add(w)
            for r in self.readers.get(b, ()):
                deps.add(r)
        s = Step(eng, fn, deps, tag, len(self.steps))
        s.ordinal = self.eng_count.get(eng, 0) + 1
        self.eng_count[eng] = s.ordinal
        self.steps.append(s)
        for b in reads:
            self.readers.setdefault(b, []).append(s)
        for b in writes:
            self.writer[b] = s
            self.readers[b] = []
        return s

    def barrier(self):
        engs = ("sync", "tensor", "vector", "scalar", "gpsimd")
        lasts = {}
        for st in self.steps:
            lasts[st.eng] = st
        newfloor = []
        for en in engs:
            deps = set(v for k, v in lasts.items() if k != en)
            stp = Step(en, (lambda e: e.nop()), deps, "barrier", len(self.steps))
            stp.ordinal = self.eng_count.get(en, 0) + 1
            self.eng_count[en] = stp.ordinal
            self.steps.append(stp)
            newfloor.append(stp)
        self.floor = newfloor
        self.writer = {}
        self.readers = {}


    _COST = {
        "tensor": {"qmm": 2300, "kmm": 2300, "vmm": 2300, "dmm": 2300, "gmm": 2300, "omm": 2300,
                   "main": 2600, "ffnpre": 6000, "wo": 2000, "sc": 450, "av": 450,
                   "stmm": 700, "bqm": 700, "bkm": 700, "in_tr": 700, "otr": 1200,
                   "nrep": 1800, "bc": 4000, "stbc": 600, "a_bc": 600, "rb": 600,
                   "ltr": 400, "swtr": 400, "swrep": 500, "bt": 300, "btb": 300},
        "vector": {"sc": 4000, "wm": 2150, "hcm": 2150, "ya": 2150, "cv": 1200,
                   "xn1": 700, "xn2": 700, "a_x1": 700, "a_x2": 1200, "sx1": 700, "sx2": 700,
                   "woe": 1200, "wor": 700, "mainev": 1200, "onrm": 700, "yg": 700,
                   "dp1": 400, "du": 450, "hc": 150, "rcph": 1500, "rcp": 4500, "pz": 300, "pm": 300},
        "scalar": {"dk": 1250, "px": 650, "dex": 650, "dln": 650, "qev": 700, "kev": 700,
                   "vev": 800, "oev": 700, "oevs": 350, "in_ev": 600, "sq": 700,
                   "cvs": 650, "cvx": 650, "gev": 650, "a_x3": 650, "shev": 500,
                   "swev": 400, "swrev": 500, "bre": 550, "cre": 550, "cast": 900},
        "sync": {"sdma": 2500, "wq": 6000, "wk": 6000, "wv": 6000, "swd": 6000, "swg": 6000, "swo": 6000,
                 "in_tm": 3000, "in_fm": 1800, "odma": 1800, "srl": 1800, "xsc": 1800,
                 "sdma": 1500, "rdma": 1500, "bdma": 1500, "cdma": 1500, "wol": 1500,
                 "ld_": 2500, "sld": 2500, "st_": 4000},
        "gpsimd": {"ya": 2400, "wm": 2400, "hcm": 2400},
    }
    _DEFC = {"tensor": 500, "vector": 800, "scalar": 700, "sync": 2000, "gpsimd": 1200}

    def _cost(self, st):
        table = self._COST.get(st.eng, {})
        for pref, c in table.items():
            if st.tag.startswith(pref):
                return c
        return self._DEFC.get(st.eng, 800)

    def reorder(self, only_segments=None, freeze=()):
        segs = []
        cur = []
        i = 0
        n = len(self.steps)
        while i < n:
            st = self.steps[i]
            if st.tag == "barrier":
                segs.append(cur)
                segs.append(self.steps[i:i + 5])
                cur = []
                i += 5
            else:
                cur.append(st)
                i += 1
        segs.append(cur)
        out = []
        segidx = -1
        for seg in segs:
            if not seg or seg[0].tag == "barrier":
                out.extend(seg)
                continue
            segidx += 1
            if only_segments is not None and segidx not in only_segments:
                out.extend(seg)
                continue
            segset = set(id(x) for x in seg)
            cost = {id(x): self._cost(x) for x in seg}
            succ = {id(x): [] for x in seg}
            indeg = {id(x): 0 for x in seg}
            extra = {}
            prev_by_eng = {}
            for x in seg:
                if x.eng in freeze:
                    p = prev_by_eng.get(x.eng)
                    if p is not None:
                        extra[id(x)] = p
                    prev_by_eng[x.eng] = x
            for x in seg:
                for d in x.deps:
                    if id(d) in segset:
                        succ[id(d)].append(x)
                        indeg[id(x)] += 1
                p = extra.get(id(x))
                if p is not None:
                    succ[id(p)].append(x)
                    indeg[id(x)] += 1
            cp = {}
            for x in reversed(seg):
                cp[id(x)] = cost[id(x)] + max((cp[id(t)] for t in succ[id(x)]), default=0)
            import heapq
            ready = []
            seqn = 0
            for x in seg:
                if indeg[id(x)] == 0:
                    heapq.heappush(ready, (-cp[id(x)], seqn, x))
                    seqn += 1
            eng_free = {}
            fin = {}
            order = []
            while ready:
                _, _, x = heapq.heappop(ready)
                dep_t = max((fin.get(id(d), 0) for d in x.deps), default=0)
                stt = max(eng_free.get(x.eng, 0), dep_t)
                fin[id(x)] = stt + cost[id(x)]
                eng_free[x.eng] = fin[id(x)]
                order.append(x)
                for t in succ[id(x)]:
                    indeg[id(t)] -= 1
                    if indeg[id(t)] == 0:
                        heapq.heappush(ready, (-cp[id(t)], seqn, t))
                        seqn += 1
            assert len(order) == len(seg), "scheduler lost steps"
            out.extend(order)
        assert len(out) == n
        self.steps = out
        cnt = {}
        for x in self.steps:
            cnt[x.eng] = cnt.get(x.eng, 0) + 1
            x.ordinal = cnt[x.eng]
        self.eng_count = cnt

    def emit_one(self, ename, eng, sems):
        unit = {"sync": 16}
        seen = {}
        for s in self.steps:
            if s.eng != ename:
                continue
            need = {}
            for d in s.deps:
                need[d.eng] = max(need.get(d.eng, 0), d.ordinal)
            for de, o in sorted(need.items()):
                v = o * unit.get(de, 1)
                if seen.get(de, 0) < v:
                    eng.wait_ge(sems[de], v)
                    seen[de] = v
            last = s.fn(eng)
            assert last is not None, f"step {s.tag} returned no instruction"
            last.then_inc(sems[ename], unit.get(ename, 1))
        if ename == "sync":
            eng.wait_ge(sems["sync"], self.eng_count.get("sync", 0) * 16)


def prep_weights(w):
    t = lambda a: np.ascontiguousarray(np.asarray(a).T)
    out = {
        "WqT": t(w["L0_attn_a_Wq"]), "WkT": t(w["L0_attn_a_Wk"]),
        "WvT": t(w["L0_attn_a_Wv"]), "WoT": t(w["L0_attn_a_Wo"]),
        "bq_row": np.asarray(w["L0_attn_a_bq"]).reshape(1, D).copy(),
        "bk_row": np.asarray(w["L0_attn_a_bk"]).reshape(1, D).copy(),
        "bv_row": np.asarray(w["L0_attn_a_bv"]).reshape(1, D).copy(),
        "bo_row": np.asarray(w["L0_attn_a_bo"]).reshape(1, D).copy(),
        "norm_w": np.asarray(w["L0_attn_a_norm_w"]).reshape(DC, 128).T.copy(),
        "baseWT": t(w["L0_ffn_base_W"]),
        "bb": np.asarray(w["L0_ffn_base_b"]).reshape(DC, 128).T.copy(),
        "sAT": t(w["L0_ffn_shared_A"]),
        "sBT": t(w["L0_ffn_shared_B"]),
        "rAT": t(np.asarray(w["L0_ffn_routed_A"]).reshape(E * RK, D)),
        "rBf": np.ascontiguousarray(
            np.asarray(w["L0_ffn_routed_B"]).transpose(0, 2, 1).reshape(E * RK, D)),
        "rWT": t(w["L0_ffn_router_W"]),
        "rb": np.asarray(w["L0_ffn_router_b"]).reshape(E, 1).copy(),
        "convw": np.asarray(w["L0_attn_b_conv_w"]).reshape(D, 4).copy(),
        "convb": np.asarray(w["L0_attn_b_conv_b"]).reshape(DC, 128).T.copy(),
        "WdT": t(w["L0_attn_b_Wd"]),
        "bd": np.asarray(w["L0_attn_b_bd"]).reshape(DC, 128).T.copy(),
        "WbT": t(w["L0_attn_b_Wb"]),
        "WcT": t(w["L0_attn_b_Wc"]),
        "log_a": np.asarray(w["L0_attn_b_log_a"]).copy(),
        "WgT": t(w["L0_attn_b_Wg"]),
        "bg": np.asarray(w["L0_attn_b_bg"]).reshape(DC, 128).T.copy(),
        "WobT": t(w["L0_attn_b_Wo"]),
        "bob": np.asarray(w["L0_attn_b_bo"]).reshape(DC, 128).T.copy(),
    }
    return {k: np.ascontiguousarray(v.astype(np.float32)) for k, v in out.items()}


WSHAPES = {
    "WqT": [D, D], "WkT": [D, D], "WvT": [D, D], "WoT": [D, D],
    "bq_row": [1, D], "bk_row": [1, D], "bv_row": [1, D], "bo_row": [1, D],
    "norm_w": [128, DC],
    "baseWT": [D, D], "bb": [128, DC], "sAT": [D, RK], "sBT": [RK, D],
    "rAT": [D, E * RK], "rBf": [E * RK, D], "rWT": [D, E], "rb": [E, 1],
    "convw": [D, 4], "convb": [128, DC], "WdT": [D, D], "bd": [128, DC],
    "WbT": [D, ST], "WcT": [D, ST], "log_a": [D, ST], "WgT": [D, D],
    "bg": [128, DC], "WobT": [D, D], "bob": [128, DC],
}


class Arena:
    def __init__(self, mk, bytes_per_part):
        self.cols4 = bytes_per_part // 4
        self.t = mk.sb("arena", [128, self.cols4], F32)
        self.off4 = 0

    def reset(self):
        self.off4 = 0

    def alloc(self, shape, dt=F32):
        rows, cols = shape
        esz = 2 if dt == BF16 else 4
        n4 = (cols * esz + 63) // 64 * 16
        assert self.off4 + n4 <= self.cols4, f"arena overflow at {shape} {dt}: {self.off4*4+n4*4} > {self.cols4*4}"
        self.last_start = self.off4
        base = self.t[0:rows, self.off4:self.off4 + n4]
        self.off4 += n4
        if dt == F32:
            return base[:, 0:cols]
        return base.bitcast(dt)[:, 0:cols]


class MK:
    def __init__(self, stages=("ffn1",), nseq=NSEQ):
        self.stages = stages
        self.nseq = nseq
        self.nc = bass.Bass("TRN2", target_bir_lowering=False)
        self.sched = Sched()
        self.st = ExitStack()
        self.dram = {}

    def sb(self, name, shape, dt=F32):
        return self.st.enter_context(self.nc.sbuf_tensor(name, shape, dt))

    def asb(self, name, shape, dt=F32):
        ap = self.ar.alloc(shape, dt)
        key = (self.ar.last_start, tuple(shape), str(dt))
        if not hasattr(self, "_stable"):
            self._stable = {}
        prev = self._stable.get(name)
        assert prev is None or prev == key, f"arena attr {name} moved: {prev} -> {key}"
        self._stable[name] = key
        return ap

    def ps(self, name, shape, dt=F32):
        return self.st.enter_context(self.nc.psum_tensor(name, shape, dt))

    def din(self, name, shape, dt=F32):
        a = self.nc.dram_tensor(name, shape, dt, kind="ExternalInput").ap()
        self.dram[name] = a
        return a

    def dout(self, name, shape, dt=F32):
        a = self.nc.dram_tensor(name, shape, dt, kind="ExternalOutput").ap()
        self.dram[name] = a
        return a

    def dump(self, name, ap, readbufs, shape, dt=F32):
        d = self.dout("dbg_" + name, shape, dt)
        self.sched.step("sync", lambda e, ap=ap, d=d: e.dma_start(out=d, in_=ap),
                        reads=list(readbufs), tag="dump_" + name)

    def dint(self, name, shape, dt=F32):
        a = self.nc.dram_tensor(name, shape, dt, kind="Internal").ap()
        self.dram[name] = a
        return a

    def build(self):
        nc, S, ns = self.nc, self.sched, self.nseq
        self.din("x", [ns, T, D])
        self.dout("y", [ns, T, D])
        for n, sh in WSHAPES.items():
            self.din(n, sh)
        import os as _os
        n_bounce = len(self.stages) - 1
        for i in range(n_bounce):
            self.dout(f"xs{i}", [ns, DC, 128, T])

        if "ssm" in self.stages and self.stages[0] == "ssm":
            self.dout("xres_sc", [ns, DC, 128, T])

        self.identf = self.sb("identf", [128, 128], F32)
        self.iotab = self.sb("iotab", [128, 128], I32)
        self.onescol = self.sb("onescol", [128, 1], F32)
        self.onesrow = self.sb("onesrow", [1, 128], F32)
        self.onesrow_b = self.sb("onesrow_b", [1, TC], BF16)
        self.csm = self.sb("csm", [1, 8], F32)
        self.c6col = self.sb("c6col", [128, 1], F32)
        self.zcol = self.sb("zcol", [128, 1], F32)
        self.wl = [self.sb(f"wl{i}", [128, 512], F32) for i in range(1)]
        self.emit_consts()

        need_ssm = "ssm" in self.stages
        self.xtm = [self.sb(f"xtm{i}", [128, 1024], F32) for i in range(2)]
        self.xfm = [[self.sb(f"xfm{r}_{dc}", [128, TC], F32) for dc in range(DC)] for r in range(1)]
        t1s = self.sb("t1_s", [128, TC], F32); self.t1 = [t1s, t1s]
        x2s = self.sb("x2t_s", [128, TC], F32); self.x2t = [x2s, x2s]
        statrow = self.sb("statrow", [1, 4 * TC], F32)
        self.rmean = statrow[:, 0:TC]
        self.rvar = statrow[:, TC:2 * TC]
        self.rtmp = statrow[:, 2 * TC:3 * TC]
        self.rstd = statrow[:, 3 * TC:4 * TC]
        self.otm = [self.sb(f"otm{i}", [128, 512], F32) for i in range(1)]

        self.pbT = self.ps("pbT", [128, 512], F32)
        self.pstat = self.ps("pstat", [128, 512], F32)
        self.pbcA = self.ps("pbcA", [128, 512], F32)
        self.pbcB = self.ps("pbcB", [128, 512], F32)
        self.pmA = self.ps("pmA", [128, 512], F32)
        self.pmB = self.ps("pmB", [128, 512], F32)
        self.paux1 = self.ps("paux1", [128, 512], F32)
        self.paux2 = self.ps("paux2", [128, 512], F32)

        if "attn" in self.stages:
            self.dout("wq_bf", [128, 8192], BF16)
            self.dout("wk_bf", [128, 8192], BF16)
            self.dout("wv_bf", [128, 8192], BF16)
            self.dout("wo_bf", [64, 16384], BF16)
        if "ssm" in self.stages:
            self.dout("wd_bf", [128, 8192], BF16)
            self.dout("wg_bf", [128, 8192], BF16)
            self.dout("wob_bf", [128, 8192], BF16)

        rem = self.nc.sbuf_bytes_remaining - 256
        self.ar = Arena(self, rem)

        import os
        srcs = ["x"] + [f"xs{i}" for i in range(n_bounce)]
        dsts = [f"xs{i}" for i in range(n_bounce)] + ["y"]
        first_attn = True
        for si, stname in enumerate(self.stages):
            src, dst = srcs[si], dsts[si]
            if si > 0:
                self.sched.barrier()
            self.ar.reset()
            if stname.startswith("ffn"):
                self.alloc_ffn()
                self.load_ffn_weights()
            elif stname == "attn":
                self.alloc_attn()
                self.load_attn_weights(cast_hbm=first_attn)
                first_attn = False
            elif stname == "ssm":
                self.alloc_ssm()
                self.load_ssm_weights()
            for s in range(ns):
                if stname.startswith("ffn"):
                    self.emit_ffn_seq(s, src, src == "x", dst, dst == "y")
                elif stname == "attn":
                    self.emit_attn_seq(s, src, src == "x", dst, dst == "y")
                elif stname == "ssm":
                    self.emit_ssm_seq(s, src, src == "x", dst, dst == "y")
                else:
                    raise NotImplementedError(stname)

        self.sched.reorder(freeze=("sync",))
        sems = {}
        for en in ("sync", "tensor", "vector", "scalar", "gpsimd"):
            sems[en] = self.st.enter_context(nc.semaphore(f"sem_{en}"))
        block = self.st.enter_context(nc.Block())
        plan = self.sched

        @block.sync
        def _(sync):
            plan.emit_one("sync", sync, sems)

        @block.tensor
        def _(tensor):
            plan.emit_one("tensor", tensor, sems)

        @block.vector
        def _(vector):
            plan.emit_one("vector", vector, sems)

        @block.scalar
        def _(scalar):
            plan.emit_one("scalar", scalar, sems)

        @block.gpsimd
        def _(gpsimd):
            plan.emit_one("gpsimd", gpsimd, sems)

        self.st.close()
        return self.nc

    def emit_consts(self):
        S = self.sched
        idf, iot = self.identf, self.iotab

        S.step("gpsimd", lambda g: g.iota(iot[:], pattern=[[1, 128]], base=0, channel_multiplier=-1),
               writes=["iotab"], tag="iota")
        S.step("gpsimd", lambda g: g.tensor_scalar(out=idf[:], in0=iot[:], scalar1=0.0, scalar2=None, op0=ALU.is_equal),
               reads=["iotab"], writes=["identf"], tag="identf")


        oc, orow, orb, csm = self.onescol, self.onesrow, self.onesrow_b, self.csm

        def f_ones(v):
            v.memset(oc[:], 1.0)
            v.memset(orow[:], 1.0)
            v.memset(orb[:], 1.0)
            v.memset(csm[0:1, 0:1], LN_EPS)
            v.memset(csm[0:1, 1:2], RMS_EPS)
            v.memset(csm[0:1, 2:3], 1e-6)
            v.memset(csm[0:1, 3:4], 0.0)
            v.memset(csm[0:1, 4:5], 1e-11)
            v.memset(self.c6col[:], 1e-6)
            return v.memset(self.zcol[:], 0.0)
        S.step("vector", f_ones, writes=["onescol", "onesrow", "onesrow_b", "csm", "c6col", "zcol"], tag="ones")

    def alloc_ffn(self):
        self.xnb = [[self.asb(f"xnb{r}_{dc}", [128, TC], BF16) for dc in range(DC)] for r in range(1)]
        self.baseWTb = [self.asb(f"baseWTb{k}", [128, 1024], BF16) for k in range(DC)]
        self.sATb = self.asb("sATb", [128, RK * DC], BF16)
        self.sBTb = self.asb("sBTb", [RK, 1024], BF16)
        self.rATb = self.asb("rATb", [128, E * RK * DC], BF16)
        self.rBfb = self.asb("rBfb", [E * RK, 1024], BF16)
        self.rWTb = self.asb("rWTb", [128, E * DC], BF16)
        self.bbcol = self.asb("bbcol", [128, DC], F32)
        self.rbcol = self.asb("rbcol", [E, 1], F32)
        self.rep16 = self.asb("rep16", [E, E * RK], BF16)
        self.shb = self.asb("shb", [RK, TC], BF16)
        self.lrow = self.asb("lrow", [E, TC], F32)
        self.ltm = self.asb("ltm", [128, 32], F32)
        self.t2s = self.asb("t2s", [128, 32], F32)
        self.swtm = self.asb("swtm", [128, 32], F32)
        self.swfb = self.asb("swfb", [E, TC], BF16)
        self.swrb = self.asb("swrb", [E * RK, TC], F32)
        self.hwb = self.asb("hwb", [E * RK, TC], BF16)

    def load_ffn_weights(self):
        S, dram = self.sched, self.dram

        def ld(dst_ap, dstname, src_ap, rows, cols, scratch):
            wname = f"wl{scratch}"
            wl = self.wl[scratch]
            for c0 in range(0, cols, 512):
                cw = min(512, cols - c0)
                S.step("sync", lambda e, wl=wl, src=src_ap, rows=rows, c0=c0, cw=cw:
                       e.dma_start(out=wl[0:rows, 0:cw], in_=src[:, c0:c0 + cw]),
                       writes=[wname], tag=f"ld_{dstname}_{c0}")
                S.step("scalar", lambda e, wl=wl, dst=dst_ap, rows=rows, c0=c0, cw=cw:
                       e.copy(out=dst[:, c0:c0 + cw], in_=wl[0:rows, 0:cw]),
                       reads=[wname], writes=[dstname], tag=f"cast_{dstname}_{c0}")

        for k in range(DC):
            ld(self.baseWTb[k][:], f"baseWTb{k}", dram["baseWT"][k * 128:(k + 1) * 128, :], 128, 1024, 0)
        for k in range(DC):
            ld(self.sATb[:, k * RK:(k + 1) * RK], "sATb", dram["sAT"][k * 128:(k + 1) * 128, :], 128, RK, 0)
            ld(self.rATb[:, k * E * RK:(k + 1) * E * RK], "rATb", dram["rAT"][k * 128:(k + 1) * 128, :], 128, E * RK, 0)
            ld(self.rWTb[:, k * E:(k + 1) * E], "rWTb", dram["rWT"][k * 128:(k + 1) * 128, :], 128, E, 0)
        ld(self.sBTb[:], "sBTb", dram["sBT"], RK, 1024, 0)
        ld(self.rBfb[:], "rBfb", dram["rBf"], E * RK, 1024, 0)
        S.step("sync", lambda e: e.dma_start(out=self.bbcol[:], in_=dram["bb"]), writes=["bbcol"], tag="ld_bb")
        S.step("sync", lambda e: e.dma_start(out=self.rbcol[:], in_=dram["rb"]), writes=["rbcol"], tag="ld_rb")

        rep, iot = self.rep16, self.iotab

        S.step("gpsimd", lambda g: g.iota(iot[0:E, 0:E * RK].rearrange("p (e r) -> p e r", r=RK),
               pattern=[[1, E], [0, RK]], base=0, channel_multiplier=-1),
               writes=["iotab"], tag="rep16i")
        S.step("gpsimd", lambda g: g.tensor_scalar(out=rep[:], in0=iot[0:E, 0:E * RK], scalar1=0.0,
               scalar2=None, op0=ALU.is_equal),
               reads=["iotab"], writes=["rep16"], tag="rep16c")

    def alloc_attn(self):
        self.xnb = [[self.asb(f"xnb{r}_{dc}", [128, TC], BF16) for dc in range(DC)] for r in range(1)]
        self.wpb = self.asb("wpb", [128, 8192], BF16)
        self.wob = [self.asb(f"wob{i}", [64, 1024], BF16) for i in range(2)]
        biasrow = self.asb("biasrow", [1, 4 * D], BF16)
        self.bqrb = biasrow[:, 0:D]
        self.bkrb = biasrow[:, D:2 * D]
        self.bvrb = biasrow[:, 2 * D:3 * D]
        self.borb = biasrow[:, 3 * D:4 * D]
        self.nwcol = self.asb("nwcol", [128, DC], F32)
        self.kcache = [self.asb(f"kcache{j}", [128, T], BF16) for j in range(DC)]
        self.vcache = self.asb("vcache", [128, 16 * H * 65], BF16)
        self.qb = [self.asb(f"qb{j}", [128, TC], BF16) for j in range(DC)]
        self.xn2f = [self.asb(f"xn2f{dc}", [128, TC], F32) for dc in range(DC)]
        self.ptile = [self.asb(f"ptile{i}", [128, TC], BF16) for i in range(3)]
        self.osb = self.asb("osb", [64, 8 * TC], BF16)
        rowpool = self.asb("rowpool", [128, TC], F32)
        self.o64 = [rowpool]
        self.rcpr = [rowpool[0:1, :]]
        self.rcpq = [self.asb(f"rcpq{i}", [1, TC], F32) for i in range(2)]
        self.attnO = self.asb("attnO", [64, H * TC], BF16)
        self.n16a = self.asb("n16a", [16, TC], F32)
        self.n16b = self.asb("n16b", [16, TC], F32)
        self.knmax = self.asb("knmax", [16, 1], F32)
        self.kntmp = self.asb("kntmp", [16, 1], F32)
        self.bstar = self.asb("bstar", [16, 1], F32)
        self.nb8row = self.asb("nb8row", [1, 16], F32)
        self.nbB = self.asb("nbB", [128, 16], F32)

        self.tri01 = self.asb("tri01", [128, 128], BF16)
        self.bdsel = self.asb("bdsel", [128, 16 * DC], F32)

    def load_attn_weights(self, cast_hbm=True):
        S, dram = self.sched, self.dram

        def ld(dst_ap, dstname, src_ap, rows, cols, scratch):
            wname = f"wl{scratch}"
            wl = self.wl[scratch]
            for c0 in range(0, cols, 512):
                cw = min(512, cols - c0)
                S.step("sync", lambda e, wl=wl, src=src_ap, rows=rows, c0=c0, cw=cw:
                       e.dma_start(out=wl[0:rows, 0:cw], in_=src[:, c0:c0 + cw]),
                       writes=[wname], tag=f"ld_{dstname}_{c0}")
                S.step("scalar", lambda e, wl=wl, dst=dst_ap, rows=rows, c0=c0, cw=cw:
                       e.copy(out=dst[:, c0:c0 + cw], in_=wl[0:rows, 0:cw]),
                       reads=[wname], writes=[dstname], tag=f"cast_{dstname}_{c0}")

        if cast_hbm:
            for src, dstd in [("WqT", "wq_bf"), ("WkT", "wk_bf"), ("WvT", "wv_bf")]:
                for k in range(DC):
                    ld(self.wpb[:, k * 1024:(k + 1) * 1024], "wpb",
                       dram[src][k * 128:(k + 1) * 128, :], 128, 1024, 0)
                S.step("sync", lambda e, dstd=dstd: e.dma_start(out=dram[dstd], in_=self.wpb[:]),
                       reads=["wpb"], writes=[dstd], tag=f"st_{dstd}")
            for h in range(H):
                wob = self.wob[h % 2]
                ld(wob[:], f"wob{h % 2}", dram["WoT"][h * 64:(h + 1) * 64, :], 64, 1024, 0)
                S.step("sync", lambda e, h=h, wob=wob: e.dma_start(
                    out=dram["wo_bf"][:, h * 1024:(h + 1) * 1024], in_=wob[:]),
                    reads=[f"wob{h % 2}"], writes=["wo_bf"], tag=f"st_wo{h}")
        for nm, dst, dn in [("bq_row", self.bqrb, "bqrb"), ("bk_row", self.bkrb, "bkrb"),
                            ("bv_row", self.bvrb, "bvrb"), ("bo_row", self.borb, "borb")]:
            ld(dst[:], dn, dram[nm], 1, D, 0)
        S.step("sync", lambda e: e.dma_start(out=self.nwcol[:], in_=dram["norm_w"]),
               writes=["nwcol"], tag="ld_nw")
        iot = self.iotab
        S.step("gpsimd", lambda g: g.iota(iot[:], pattern=[[1, 128]], base=0, channel_multiplier=-1),
               writes=["iotab"], tag="tri_i")
        S.step("gpsimd", lambda g: g.tensor_scalar(out=self.tri01[:], in0=iot[:], scalar1=0.0,
               scalar2=None, op0=ALU.is_ge), reads=["iotab"], writes=["tri01"], tag="tri_c")
        S.step("vector", lambda v: v.memset(self.bdsel[:], 0.0), writes=["bdsel"], tag="bd0")
        for j in range(DC):
            S.step("vector", lambda v, j=j: v.memset(self.bdsel[0:64, j * 16 + 2 * j:j * 16 + 2 * j + 1], 1.0),
                   reads=["bdsel"], writes=["bdsel"], tag=f"bd1_{j}")
            S.step("vector", lambda v, j=j: v.memset(self.bdsel[64:128, j * 16 + 2 * j + 1:j * 16 + 2 * j + 2], 1.0),
                   reads=["bdsel"], writes=["bdsel"], tag=f"bd2_{j}")
        vc = self.vcache[:].rearrange("p (g h e) -> p g h e", g=16, h=H)
        S.step("vector", lambda v: v.memset(vc[:, :, :, 64:65], 1.0), writes=["vcache_ones"], tag="vc1")

    def alloc_ssm(self):
        TH = 1024
        self.ar.alloc([128, 4096], BF16)
        self.wpb = self.asb("wpb", [128, 8192], BF16)
        self.WbTb = self.asb("WbTb", [128, ST * DC], BF16)
        self.WcTb = self.asb("WcTb", [128, ST * DC], BF16)
        self.cwcol = self.asb("cwcol", [128, 4 * DC], F32)
        self.cbcol = self.asb("cbcol", [128, DC], F32)
        self.bdcol = self.asb("bdcol", [128, DC], F32)
        self.bgcol = self.asb("bgcol", [128, DC], F32)
        self.bobcol = self.asb("bobcol", [128, DC], F32)
        self.Acol = self.asb("Acol", [128, ST * DC], F32)
        self.xg = [self.asb(f"xg{dc}", [128, TH + 3], BF16) for dc in range(DC)]
        self.xcb = [self.asb(f"xcb{dc}", [128, TC], BF16) for dc in range(DC)]
        self.dlh = [self.asb(f"dlh{dc}", [128, TH], BF16) for dc in range(DC)]
        self.uh = [self.asb(f"uh{dc}", [128, TH], BF16) for dc in range(DC)]
        self.yh = [self.asb(f"yh{dc}", [128, TH], F32) for dc in range(DC)]
        self.ygb = [self.asb(f"ygb{dc}", [128, TC], BF16) for dc in range(DC)]
        self.Brow = self.asb("Brow", [ST, TH], BF16)
        self.Crow = self.asb("Crow", [ST, TH], BF16)
        self.Brep = [self.asb(f"Brep{i}", [128, TH], BF16) for i in range(2)]
        self.brow0 = self.asb("brow0", [1, TH], BF16)
        self.crow0 = self.asb("crow0", [1, TH], BF16)
        self.Crep = [self.asb(f"Crep{i}", [128, TH], BF16) for i in range(2)]
        self.decay = [self.asb(f"decay{i}", [128, TH], F32) for i in range(2)]
        self.wsc = [self.asb(f"wsc{i}", [128, TH], BF16) for i in range(2)]
        self.hsc = [self.asb(f"hsc{i}", [128, TH], BF16) for i in range(2)]
        self.hcsc = [self.asb(f"hcsc{i}", [128, TH], BF16) for i in range(2)]
        self.hcar = self.asb("hcar", [128, ST * DC], F32)
        self.cacc = [self.asb(f"cacc{i}", [128, TC], F32) for i in range(2)]
        self.sgt = self.asb("sgt", [128, TC], F32)

    def load_ssm_weights(self):
        S, dram = self.sched, self.dram

        def ld(dst_ap, dstname, src_ap, rows, cols, scratch=0):
            wname = f"wl{scratch}"
            wl = self.wl[scratch]
            for c0 in range(0, cols, 512):
                cw = min(512, cols - c0)
                S.step("sync", lambda e, wl=wl, src=src_ap, rows=rows, c0=c0, cw=cw:
                       e.dma_start(out=wl[0:rows, 0:cw], in_=src[:, c0:c0 + cw]),
                       writes=[wname], tag=f"sld_{dstname}_{c0}")
                S.step("scalar", lambda e, wl=wl, dst=dst_ap, rows=rows, c0=c0, cw=cw:
                       e.copy(out=dst[:, c0:c0 + cw], in_=wl[0:rows, 0:cw]),
                       reads=[wname], writes=[dstname], tag=f"scast_{dstname}_{c0}")

        for src, dstd in [("WdT", "wd_bf"), ("WgT", "wg_bf"), ("WobT", "wob_bf")]:
            for k in range(DC):
                ld(self.wpb[:, k * 1024:(k + 1) * 1024], "wpb",
                   dram[src][k * 128:(k + 1) * 128, :], 128, 1024)
            S.step("sync", lambda e, dstd=dstd: e.dma_start(out=dram[dstd], in_=self.wpb[:]),
                   reads=["wpb"], writes=[dstd], tag=f"st_{dstd}")
        for k in range(DC):
            ld(self.WbTb[:, k * ST:(k + 1) * ST], "WbTb", dram["WbT"][k * 128:(k + 1) * 128, :], 128, ST)
            ld(self.WcTb[:, k * ST:(k + 1) * ST], "WcTb", dram["WcT"][k * 128:(k + 1) * 128, :], 128, ST)
        for nm, dst, dn in [("convw", self.cwcol, "cwcol"), ("convb", self.cbcol, "cbcol"),
                            ("bd", self.bdcol, "bdcol"), ("bg", self.bgcol, "bgcol"),
                            ("bob", self.bobcol, "bobcol")]:
            if nm == "convw":
                for dc in range(DC):
                    S.step("sync", lambda e, dc=dc: e.dma_start(
                        out=self.cwcol[:, dc * 4:(dc + 1) * 4],
                        in_=self.dram["convw"][dc * 128:(dc + 1) * 128, :]),
                        writes=["cwcol"], tag=f"ld_cw{dc}")
            else:
                S.step("sync", lambda e, nm=nm, dst=dst: e.dma_start(out=dst[:], in_=self.dram[nm]),
                       writes=[dn], tag=f"ld_{nm}")
        for dc in range(DC):
            S.step("sync", lambda e, dc=dc: e.dma_start(
                out=self.Acol[:, dc * ST:(dc + 1) * ST],
                in_=self.dram["log_a"][dc * 128:(dc + 1) * 128, :]),
                writes=["Acol"], tag=f"ld_A{dc}")
        S.step("scalar", lambda e: e.activation(out=self.Acol[:], in_=self.Acol[:], func=AF.Exp,
               bias=self.zcol[:, 0:1], scale=1.0), reads=["Acol", "zcol"], writes=["Acol"], tag="A_exp")
        S.step("vector", lambda v: v.tensor_scalar(out=self.Acol[:], in0=self.Acol[:],
               scalar1=-1.0, scalar2=None, op0=ALU.mult), reads=["Acol"], writes=["Acol"], tag="A_neg")

    def emit_ssm_seq(self, s, src, src_tm, dst, dst_tm):
        S = self.sched
        TH = 1024
        for dc in range(DC):
            S.step("vector", lambda v, dc=dc: v.memset(self.xg[dc][:, 0:3], 0.0),
                   writes=[f"xg{dc}"], tag=f"zxg{s}_{dc}")
        S.step("vector", lambda v: v.memset(self.hcar[:], 0.0),
               writes=["hcar0", "hcar1"], tag=f"zh{s}")
        for half in range(2):
            for twi in range(2):
                c = half * 2 + twi
                self.emit_ssm_prefix(s, c, half, twi, src, src_tm)
            self.emit_ssm_scan(s, half)
            for twi in range(2):
                c = half * 2 + twi
                self.emit_ssm_suffix(s, c, half, twi, src, dst, dst_tm)
            if half == 0:
                for dc in range(DC):
                    S.step("vector", lambda v, dc=dc: v.tensor_copy(
                        out=self.xg[dc][:, 0:3], in_=self.xg[dc][:, TH:TH + 3]),
                        reads=[f"xg{dc}"], writes=[f"xg{dc}"], tag=f"cxg{s}_{half}_{dc}")

    def emit_ssm_prefix(self, s, c, half, twi, src, src_tm):
        S = self.sched
        dram = self.dram
        TH = 1024
        rot = 0
        self.emit_chunk_in(s, c, src, src_tm, rot)
        xfm = self.xfm[rot]
        if src_tm:
            for dc in range(DC):
                S.step("sync", lambda e, s=s, c=c, dc=dc: e.dma_start(
                    out=dram["xres_sc"][s, dc, :, c * TC:(c + 1) * TC], in_=xfm[dc][:]),
                    reads=[f"xfm{rot}_{dc}"], writes=["xres_sc"], tag=f"xsc{s}_{c}_{dc}")
        self.emit_ln_stats(s, c, rot)
        for dc in range(DC):
            t1 = self.t1[0]
            S.step("vector", lambda v, dc=dc, t1=t1: v.tensor_tensor(
                out=t1[:], in0=xfm[dc][:], in1=self.pbcA[:, 0:TC], op=ALU.subtract),
                reads=[f"xfm{rot}_{dc}", "pbcA"], writes=["t1_s"], tag=f"sx1{s}_{c}_{dc}")
            S.step("vector", lambda v, dc=dc, t1=t1, twi=twi: v.tensor_tensor(
                out=self.xg[dc][:, 3 + twi * TC:3 + (twi + 1) * TC], in0=t1[:],
                in1=self.pbcB[:, 0:TC], op=ALU.mult),
                reads=["t1_s", "pbcB"], writes=[f"xg{dc}"], tag=f"sx2{s}_{c}_{dc}")
        for dc in range(DC):
            ca = self.cacc[dc % 2]
            can = f"cacc_{dc % 2}"
            base = twi * TC
            S.step("vector", lambda v, dc=dc, ca=ca, base=base: v.tensor_scalar(
                out=ca[:], in0=self.xg[dc][:, base:base + TC],
                scalar1=self.cwcol[:, dc * 4:dc * 4 + 1], scalar2=None, op0=ALU.mult),
                reads=[f"xg{dc}", "cwcol"], writes=[can], tag=f"cv0{s}_{c}_{dc}")
            for j in (1, 2, 3):
                S.step("vector", lambda v, dc=dc, ca=ca, base=base, j=j: v.scalar_tensor_tensor(
                    out=ca[:], in0=self.xg[dc][:, base + j:base + j + TC],
                    scalar=self.cwcol[:, dc * 4 + j:dc * 4 + j + 1], in1=ca[:],
                    op0=ALU.mult, op1=ALU.add),
                    reads=[f"xg{dc}", "cwcol", can], writes=[can], tag=f"cv{j}{s}_{c}_{dc}")
            S.step("scalar", lambda e, dc=dc, ca=ca: e.activation(
                out=self.sgt[:], in_=ca[:], func=AF.Sigmoid,
                bias=self.cbcol[:, dc:dc + 1], scale=1.0),
                reads=[can, "cbcol"], writes=["sgt"], tag=f"cvs{s}_{c}_{dc}")
            S.step("vector", lambda v, dc=dc, ca=ca: v.scalar_tensor_tensor(
                out=self.xcb[dc][:], in0=ca[:], scalar=self.cbcol[:, dc:dc + 1],
                in1=self.sgt[:], op0=ALU.add, op1=ALU.mult),
                reads=[can, "cbcol", "sgt"], writes=[f"xcb{dc}"], tag=f"cvx{s}_{c}_{dc}")
        xc_reads = [f"xcb{k}" for k in range(DC)]
        S.step("sync", lambda e: e.dma_start(out=self.wpb[:], in_=dram["wd_bf"]),
               reads=["wd_bf"], writes=["wpb"], tag=f"swd{s}_{c}")
        for oc in range(DC):
            pm, pmname = (self.pmA, "pmA") if oc % 2 == 0 else (self.pmB, "pmB")

            def f_d(e, oc=oc, pm=pm):
                last = None
                for k in range(DC):
                    last = e.matmul(out=pm[:, 0:TC],
                                    lhsT=self.wpb[:, k * 1024 + oc * 128:k * 1024 + (oc + 1) * 128],
                                    rhs=self.xcb[k][:], start=(k == 0), stop=(k == DC - 1))
                return last
            S.step("tensor", f_d, reads=xc_reads + ["wpb"], writes=[pmname], tag=f"dmm{s}_{c}_{oc}")
            S.step("scalar", lambda e, oc=oc, pm=pm: e.activation(
                out=self.sgt[:], in_=pm[:, 0:TC], func=AF.Exp,
                bias=self.bdcol[:, oc:oc + 1], scale=1.0),
                reads=[pmname, "bdcol"], writes=["sgt"], tag=f"dex{s}_{c}_{oc}")
            S.step("vector", lambda v: v.tensor_scalar(
                out=self.sgt[:], in0=self.sgt[:], scalar1=1.0, scalar2=None, op0=ALU.add),
                reads=["sgt"], writes=["sgt"], tag=f"dp1{s}_{c}_{oc}")
            S.step("scalar", lambda e, oc=oc, twi=twi: e.activation(
                out=self.dlh[oc][:, twi * TC:(twi + 1) * TC], in_=self.sgt[:], func=AF.Ln,
                bias=self.csm[0:1, 3:4].broadcast_to([128, 1]) if False else self.zcol[:, 0:1], scale=1.0),
                reads=["sgt", "zcol"], writes=[f"dlh{oc}"], tag=f"dln{s}_{c}_{oc}")
            S.step("vector", lambda v, oc=oc, twi=twi: v.tensor_tensor(
                out=self.uh[oc][:, twi * TC:(twi + 1) * TC],
                in0=self.dlh[oc][:, twi * TC:(twi + 1) * TC], in1=self.xcb[oc][:], op=ALU.mult),
                reads=[f"dlh{oc}", f"xcb{oc}"], writes=[f"uh{oc}"], tag=f"du{s}_{c}_{oc}")
        def f_b(e):
            for k in range(DC):
                e.matmul(out=self.pstat[0:ST, 0:TC], lhsT=self.WbTb[:, k * ST:(k + 1) * ST],
                         rhs=self.xcb[k][:], start=(k == 0), stop=(k == DC - 1))
            last = None
            for k in range(DC):
                last = e.matmul(out=self.pbcA[0:ST, 0:TC], lhsT=self.WcTb[:, k * ST:(k + 1) * ST],
                                rhs=self.xcb[k][:], start=(k == 0), stop=(k == DC - 1))
            return last
        S.step("tensor", f_b, reads=xc_reads + ["WbTb", "WcTb"], writes=["pstat", "pbcA"], tag=f"bc{s}_{c}")
        S.step("scalar", lambda e, twi=twi: e.copy(out=self.Brow[:, twi * TC:(twi + 1) * TC],
               in_=self.pstat[0:ST, 0:TC]), reads=["pstat"], writes=["Brow"], tag=f"bev{s}_{c}")
        S.step("scalar", lambda e, twi=twi: e.copy(out=self.Crow[:, twi * TC:(twi + 1) * TC],
               in_=self.pbcA[0:ST, 0:TC]), reads=["pbcA"], writes=["Crow"], tag=f"cev{s}_{c}")

    def emit_ssm_scan(self, s, half):
        S = self.sched
        TH = 1024
        for n in range(ST):
            br, brn = self.Brep[n % 2], f"Brep{n % 2}"
            cr, crn = self.Crep[n % 2], f"Crep{n % 2}"

            S.step("sync", lambda e, n=n: e.dma_start(out=self.brow0[:], in_=self.Brow[n:n + 1, :]),
                   reads=["Brow"], writes=["brow0"], tag=f"bdma{s}_{half}_{n}")
            S.step("sync", lambda e, n=n: e.dma_start(out=self.crow0[:], in_=self.Crow[n:n + 1, :]),
                   reads=["Crow"], writes=["crow0"], tag=f"cdma{s}_{half}_{n}")

            def f_brep(e, n=n):
                e.matmul(out=self.pmA[:, 0:TC], lhsT=self.onesrow_b[0:1, 0:128],
                         rhs=self.brow0[0:1, 0:TC], start=True, stop=True)
                e.matmul(out=self.pmB[:, 0:TC], lhsT=self.onesrow_b[0:1, 0:128],
                         rhs=self.brow0[0:1, TC:2 * TC], start=True, stop=True)
                e.matmul(out=self.paux1[:, 0:TC], lhsT=self.onesrow_b[0:1, 0:128],
                         rhs=self.crow0[0:1, 0:TC], start=True, stop=True)
                return e.matmul(out=self.paux2[:, 0:TC], lhsT=self.onesrow_b[0:1, 0:128],
                                rhs=self.crow0[0:1, TC:2 * TC], start=True, stop=True)
            S.step("tensor", f_brep, reads=["brow0", "crow0", "onesrow_b"],
                   writes=["pmA", "pmB", "paux1", "paux2"], tag=f"nrep{s}_{half}_{n}")
            S.step("scalar", lambda e, br=br: e.copy(out=br[:, 0:TC], in_=self.pmA[:, 0:TC]),
                   reads=["pmA"], writes=[brn], tag=f"bre1{s}_{half}_{n}")
            S.step("scalar", lambda e, br=br: e.copy(out=br[:, TC:2 * TC], in_=self.pmB[:, 0:TC]),
                   reads=["pmB"], writes=[brn], tag=f"bre2{s}_{half}_{n}")
            S.step("scalar", lambda e, cr=cr: e.copy(out=cr[:, 0:TC], in_=self.paux1[:, 0:TC]),
                   reads=["paux1"], writes=[crn], tag=f"cre1{s}_{half}_{n}")
            S.step("scalar", lambda e, cr=cr: e.copy(out=cr[:, TC:2 * TC], in_=self.paux2[:, 0:TC]),
                   reads=["paux2"], writes=[crn], tag=f"cre2{s}_{half}_{n}")
            for dc in range(DC):
                par = dc % 2
                ceng = "vector" if dc < 2 else "gpsimd"
                yeng = ceng
                dec, decn = self.decay[par], f"decay{par}"
                ws, wsn = self.wsc[par], f"wsc{par}"
                hs, hsn = self.hsc[par], f"hsc{par}"
                hc, hcn = self.hcsc[par], f"hcsc{par}"
                hnm = f"hcar{par}"
                col = dc * ST + n
                S.step("scalar", lambda e, dc=dc, n=n, dec=dec: e.activation(
                    out=dec[:], in_=self.dlh[dc][:], func=AF.Exp,
                    bias=self.zcol[:, 0:1], scale=self.Acol[:, dc * ST + n:dc * ST + n + 1]),
                    reads=[f"dlh{dc}", "Acol", "zcol"], writes=[decn], tag=f"dk{s}_{half}_{n}_{dc}")
                S.step(ceng, lambda v, dc=dc, br=br, ws=ws: v.tensor_tensor(
                    out=ws[:], in0=self.uh[dc][:], in1=br[:], op=ALU.mult),
                    reads=[f"uh{dc}", brn], writes=[wsn], tag=f"wm{s}_{half}_{n}_{dc}")
                S.step("vector", lambda v, dec=dec, ws=ws, hs=hs, col=col: v.tensor_tensor_scan(
                    out=hs[:], data0=dec[:], data1=ws[:], initial=self.hcar[:, col:col + 1],
                    op0=ALU.mult, op1=ALU.add),
                    reads=[decn, wsn, hnm], writes=[hsn], tag=f"sc{s}_{half}_{n}_{dc}")
                S.step("vector", lambda v, hs=hs, col=col: v.tensor_copy(
                    out=self.hcar[:, col:col + 1], in_=hs[:, TH - 1:TH]),
                    reads=[hsn], writes=[hnm], tag=f"hc{s}_{half}_{n}_{dc}")
                S.step(ceng, lambda v, hs=hs, cr=cr, hc=hc: v.tensor_tensor(
                    out=hc[:], in0=hs[:], in1=cr[:], op=ALU.mult),
                    reads=[hsn, crn], writes=[hcn], tag=f"hcm{s}_{half}_{n}_{dc}")
                if n == 0:
                    S.step(yeng, lambda g, dc=dc, hc=hc: g.tensor_copy(
                        out=self.yh[dc][:], in_=hc[:]),
                        reads=[hcn], writes=[f"yh{dc}"], tag=f"ya{s}_{half}_{n}_{dc}")
                else:
                    S.step(yeng, lambda g, dc=dc, hc=hc: g.tensor_tensor(
                        out=self.yh[dc][:], in0=hc[:], in1=self.yh[dc][:], op=ALU.add),
                        reads=[hcn, f"yh{dc}"], writes=[f"yh{dc}"], tag=f"ya{s}_{half}_{n}_{dc}")

    def emit_ssm_suffix(self, s, c, half, twi, src, dst, dst_tm):
        S = self.sched
        dram = self.dram
        rot = 0
        xfm = self.xfm[rot]
        rsrc = "xres_sc" if src == "x" else src
        S.step("sync", lambda e: e.dma_start(out=self.wpb[:], in_=dram["wg_bf"]),
               reads=["wg_bf"], writes=["wpb"], tag=f"swg{s}_{c}")
        xg_reads = [f"xg{k}" for k in range(DC)]
        for oc in range(DC):
            pm, pmname = (self.pmA, "pmA") if oc % 2 == 0 else (self.pmB, "pmB")

            def f_g(e, oc=oc, pm=pm, twi=twi):
                last = None
                for k in range(DC):
                    last = e.matmul(out=pm[:, 0:TC],
                                    lhsT=self.wpb[:, k * 1024 + oc * 128:k * 1024 + (oc + 1) * 128],
                                    rhs=self.xg[k][:, 3 + twi * TC:3 + (twi + 1) * TC],
                                    start=(k == 0), stop=(k == DC - 1))
                return last
            S.step("tensor", f_g, reads=xg_reads + ["wpb"], writes=[pmname], tag=f"gmm{s}_{c}_{oc}")
            S.step("scalar", lambda e, oc=oc, pm=pm: e.activation(
                out=self.sgt[:], in_=pm[:, 0:TC], func=AF.Sigmoid,
                bias=self.bgcol[:, oc:oc + 1], scale=1.0),
                reads=[pmname, "bgcol"], writes=["sgt"], tag=f"gev{s}_{c}_{oc}")
            S.step("vector", lambda v, oc=oc, twi=twi: v.tensor_tensor(
                out=self.ygb[oc][:], in0=self.yh[oc][:, twi * TC:(twi + 1) * TC],
                in1=self.sgt[:], op=ALU.mult),
                reads=[f"yh{oc}", "sgt"], writes=[f"ygb{oc}"], tag=f"yg{s}_{c}_{oc}")
        for dc in range(DC):
            S.step("sync", lambda e, s=s, c=c, dc=dc, rsrc=rsrc: e.dma_start(
                out=xfm[dc][:], in_=dram[rsrc][s, dc, :, c * TC:(c + 1) * TC]),
                reads=[rsrc], writes=[f"xfm{rot}_{dc}"], tag=f"srl{s}_{c}_{dc}")
        yg_reads = [f"ygb{k}" for k in range(DC)]
        S.step("sync", lambda e: e.dma_start(out=self.wpb[:], in_=dram["wob_bf"]),
               reads=["wob_bf"], writes=["wpb"], tag=f"swo{s}_{c}")
        for oc in range(DC):
            pm, pmname = (self.pmA, "pmA") if oc % 2 == 0 else (self.pmB, "pmB")

            def f_o(e, oc=oc, pm=pm):
                last = None
                for k in range(DC):
                    last = e.matmul(out=pm[:, 0:TC],
                                    lhsT=self.wpb[:, k * 1024 + oc * 128:k * 1024 + (oc + 1) * 128],
                                    rhs=self.ygb[k][:], start=(k == 0), stop=(k == DC - 1))
                return last
            S.step("tensor", f_o, reads=yg_reads + ["wpb"], writes=[pmname], tag=f"omm{s}_{c}_{oc}")
            S.step("vector", lambda v, oc=oc, pm=pm: v.scalar_tensor_tensor(
                out=xfm[oc][:], in0=pm[:, 0:TC], scalar=self.bobcol[:, oc:oc + 1],
                in1=xfm[oc][:], op0=ALU.add, op1=ALU.add),
                reads=[pmname, "bobcol", f"xfm{rot}_{oc}"], writes=[f"xfm{rot}_{oc}"], tag=f"oev{s}_{c}_{oc}")
        self.emit_chunk_out(s, c, dst, dst_tm, rot)

    def emit_attn_seq(self, s, src, src_tm, dst, dst_tm):
        S = self.sched
        for c in range(NCH):
            rot = 0
            self.emit_chunk_in(s, c, src, src_tm, rot)
            self.emit_ln_stats(s, c, rot)
            self.emit_attn_xn2(s, c, rot)
            self.emit_attn_qkv(s, c, rot)
            self.emit_attn_bounds(s, c)
            self.emit_attn_core(s, c)
            self.emit_attn_out(s, c, rot)
            self.emit_chunk_out(s, c, dst, dst_tm, rot)

    def emit_attn_xn2(self, s, c, rot):
        S = self.sched
        S.step("scalar", lambda sc: sc.activation(out=self.rstd, in_=self.rvar, func=AF.Ln,
               bias=self.csm[0:1, 4:5], scale=1.0 + 1e-6),
               reads=["rvar", "csm"], writes=["rstd"], tag=f"a_ln{s}_{c}")
        S.step("scalar", lambda sc: sc.activation(out=self.rstd, in_=self.rstd, func=AF.Exp,
               bias=self.csm[0:1, 3:4], scale=-0.5),
               reads=["rstd", "csm"], writes=["rstd"], tag=f"a_ex{s}_{c}")
        def f_bc(e):
            e.matmul(out=self.pbcA[:, 0:TC], lhsT=self.onesrow[:], rhs=self.rmean, start=True, stop=True)
            return e.matmul(out=self.pbcB[:, 0:TC], lhsT=self.onesrow[:], rhs=self.rstd, start=True, stop=True)
        S.step("tensor", f_bc, reads=["rmean", "rstd", "onesrow"], writes=["pbcA", "pbcB"], tag=f"a_bc{s}_{c}")
        xfm, xnb = self.xfm[rot], self.xnb[rot]
        for dc in range(DC):
            t1 = self.t1[dc % 2]
            S.step("vector", lambda v, dc=dc, t1=t1: v.tensor_tensor(
                out=t1[:], in0=xfm[dc][:], in1=self.pbcA[:, 0:TC], op=ALU.subtract),
                reads=[f"xfm{rot}_{dc}", "pbcA"], writes=["t1_s"], tag=f"a_x1{s}_{c}_{dc}")
            S.step("vector", lambda v, dc=dc, t1=t1: v.scalar_tensor_tensor(
                out=self.xn2f[dc][:], in0=t1[:], scalar=self.nwcol[:, dc:dc + 1],
                in1=self.pbcB[:, 0:TC], op0=ALU.mult, op1=ALU.mult),
                reads=["t1_s", "nwcol", "pbcB"], writes=[f"xn2f{dc}"], tag=f"a_x2{s}_{c}_{dc}")
            S.step("scalar", lambda sc, dc=dc: sc.copy(out=xnb[dc][:], in_=self.xn2f[dc][:]),
                reads=[f"xn2f{dc}"], writes=[f"xnb{rot}_{dc}"], tag=f"a_x3{s}_{c}_{dc}")

    def emit_attn_qkv(self, s, c, rot):
        S = self.sched
        dram = self.dram
        xnb = self.xnb[rot]
        xn_reads = [f"xnb{rot}_{k}" for k in range(DC)]

        S.step("sync", lambda e: e.dma_start(out=self.wpb[:], in_=dram["wq_bf"]),
               reads=["wq_bf"], writes=["wpb"], tag=f"wq{s}_{c}")
        for oc in range(DC):
            pm, pmname = (self.pmA, "pmA") if oc % 2 == 0 else (self.pmB, "pmB")
            pm, pmname = (self.paux2, "paux2") if oc % 4 >= 2 else (pm, pmname)

            def f_q(e, oc=oc, pm=pm):
                for k in range(DC):
                    e.matmul(out=pm[:, 0:TC],
                             lhsT=self.wpb[:, k * 1024 + oc * 128:k * 1024 + (oc + 1) * 128],
                             rhs=xnb[k][:], start=(k == 0), stop=False)
                return e.matmul(out=pm[:, 0:TC], lhsT=self.bqrb[0:1, oc * 128:(oc + 1) * 128],
                                rhs=self.onesrow_b[:], start=False, stop=True)
            S.step("tensor", f_q, reads=xn_reads + ["wpb", "bqrb", "onesrow_b"], writes=[pmname],
                   tag=f"qmm{s}_{c}_{oc}")
            S.step("scalar", lambda e, oc=oc, pm=pm: e.copy(out=self.qb[oc][:], in_=pm[:, 0:TC]),
                   reads=[pmname], writes=[f"qb{oc}"], tag=f"qev{s}_{c}_{oc}")
        S.step("sync", lambda e: e.dma_start(out=self.wpb[:], in_=dram["wk_bf"]),
               reads=["wk_bf"], writes=["wpb"], tag=f"wk{s}_{c}")
        for oc in range(DC):
            pm, pmname = (self.pmA, "pmA") if oc % 2 == 0 else (self.pmB, "pmB")

            def f_k(e, oc=oc, pm=pm):
                for k in range(DC):
                    e.matmul(out=pm[:, 0:TC],
                             lhsT=self.wpb[:, k * 1024 + oc * 128:k * 1024 + (oc + 1) * 128],
                             rhs=xnb[k][:], start=(k == 0), stop=False)
                return e.matmul(out=pm[:, 0:TC], lhsT=self.bkrb[0:1, oc * 128:(oc + 1) * 128],
                                rhs=self.onesrow_b[:], start=False, stop=True)
            S.step("tensor", f_k, reads=xn_reads + ["wpb", "bkrb", "onesrow_b"], writes=[pmname],
                   tag=f"kmm{s}_{c}_{oc}")
            S.step("scalar", lambda e, oc=oc, pm=pm, c=c: e.copy(
                out=self.kcache[oc][:, c * TC:(c + 1) * TC], in_=pm[:, 0:TC]),
                reads=[pmname], writes=[f"kc{oc}"], tag=f"kev{s}_{c}_{oc}")
        S.step("sync", lambda e: e.dma_start(out=self.wpb[:], in_=dram["wv_bf"]),
               reads=["wv_bf"], writes=["wpb"], tag=f"wv{s}_{c}")
        vc = self.vcache
        for tsub in range(4):
            gts = c * 4 + tsub
            for nv in range(2):
                pm, pmname = (self.pmA, "pmA") if nv == 0 else (self.pmB, "pmB")

                def f_v(e, tsub=tsub, nv=nv, pm=pm):
                    for k in range(DC):
                        e.matmul(out=pm[:, 0:TC], lhsT=xnb[k][:, tsub * 128:(tsub + 1) * 128],
                                 rhs=self.wpb[:, k * 1024 + nv * 512:k * 1024 + (nv + 1) * 512],
                                 start=(k == 0), stop=False)
                    return e.matmul(out=pm[:, 0:TC], lhsT=self.onesrow_b[0:1, 0:128],
                                    rhs=self.bvrb[0:1, nv * 512:(nv + 1) * 512], start=False, stop=True)
                S.step("tensor", f_v, reads=xn_reads + ["wpb", "bvrb", "onesrow_b"], writes=[pmname],
                       tag=f"vmm{s}_{c}_{tsub}_{nv}")

                def f_vev(e, tsub=tsub, nv=nv, pm=pm, gts=gts):
                    out = vc[:].rearrange("p (g h e) -> p g h e", g=16, h=H)[
                        :, gts, nv * 8:(nv + 1) * 8, 0:64]
                    return e.copy(out=out, in_=pm[:, 0:TC].rearrange("p (h e) -> p h e", h=8))
                S.step("scalar", f_vev, reads=[pmname], writes=["vcache"], tag=f"vev{s}_{c}_{tsub}_{nv}")

    def emit_attn_bounds(self, s, c):
        S = self.sched
        for j in range(DC):
            x2 = self.x2t[j % 2]
            S.step("scalar", lambda e, j=j, x2=x2: e.square(out=x2[:], in_=self.qb[j][:]),
                   reads=[f"qb{j}"], writes=["x2t_s"], tag=f"bq2{s}_{c}_{j}")
            S.step("tensor", lambda e, j=j, x2=x2: e.matmul(
                out=self.pstat[0:16, 0:TC], lhsT=self.bdsel[:, j * 16:(j + 1) * 16], rhs=x2[:],
                start=(j == 0), stop=(j == DC - 1)),
                reads=["x2t_s", "bdsel"], writes=["pstat"], tag=f"bqm{s}_{c}_{j}")
        for j in range(DC):
            x2 = self.x2t[j % 2]
            S.step("scalar", lambda e, j=j, x2=x2, c=c: e.square(
                out=x2[:], in_=self.kcache[j][:, c * TC:(c + 1) * TC]),
                reads=[f"kc{j}"], writes=["x2t_s"], tag=f"bk2{s}_{c}_{j}")
            S.step("tensor", lambda e, j=j, x2=x2: e.matmul(
                out=self.pbcA[0:16, 0:TC], lhsT=self.bdsel[:, j * 16:(j + 1) * 16], rhs=x2[:],
                start=(j == 0), stop=(j == DC - 1)),
                reads=["x2t_s", "bdsel"], writes=["pbcA"], tag=f"bkm{s}_{c}_{j}")
        S.step("scalar", lambda e: e.activation(out=self.n16a[:], in_=self.pstat[0:16, 0:TC],
               func=AF.Ln, bias=self.c6col[0:16, 0:1], scale=1.0),
               reads=["pstat", "c6col"], writes=["n16a"], tag=f"bqn{s}_{c}")
        S.step("scalar", lambda e: e.activation(out=self.n16a[:], in_=self.n16a[:],
               func=AF.Exp, bias=self.zcol[0:16, 0:1], scale=0.5),
               reads=["n16a", "zcol"], writes=["n16a"], tag=f"bqe{s}_{c}")
        S.step("scalar", lambda e: e.activation(out=self.n16b[:], in_=self.pbcA[0:16, 0:TC],
               func=AF.Ln, bias=self.c6col[0:16, 0:1], scale=1.0),
               reads=["pbcA", "c6col"], writes=["n16b"], tag=f"bkn{s}_{c}")
        S.step("scalar", lambda e: e.activation(out=self.n16b[:], in_=self.n16b[:],
               func=AF.Exp, bias=self.zcol[0:16, 0:1], scale=0.5),
               reads=["n16b", "zcol"], writes=["n16b"], tag=f"bke{s}_{c}")
        S.step("vector", lambda v: v.tensor_reduce(out=self.kntmp[:], in_=self.n16b[:],
               axis=AX.X, op=ALU.max), reads=["n16b"], writes=["kntmp"], tag=f"bkx{s}_{c}")
        if c == 0:
            S.step("vector", lambda v: v.tensor_copy(out=self.knmax[:], in_=self.kntmp[:]),
                   reads=["kntmp"], writes=["knmax"], tag=f"bki{s}_{c}")
        else:
            S.step("vector", lambda v: v.tensor_tensor(out=self.knmax[:], in0=self.knmax[:],
                   in1=self.kntmp[:], op=ALU.max), reads=["kntmp", "knmax"], writes=["knmax"], tag=f"bki{s}_{c}")
        S.step("vector", lambda v: v.tensor_scalar(out=self.n16a[:], in0=self.n16a[:],
               scalar1=self.knmax[:, 0:1], scalar2=None, op0=ALU.mult),
               reads=["n16a", "knmax"], writes=["n16a"], tag=f"bb{s}_{c}")
        S.step("vector", lambda v: v.tensor_reduce(out=self.bstar[:], in_=self.n16a[:],
               axis=AX.X, op=ALU.max), reads=["n16a"], writes=["bstar"], tag=f"bs{s}_{c}")
        S.step("vector", lambda v: v.tensor_scalar(out=self.bstar[:], in0=self.bstar[:],
               scalar1=-0.125, scalar2=None, op0=ALU.mult), reads=["bstar"], writes=["bstar"], tag=f"bn{s}_{c}")
        S.step("tensor", lambda e: e.transpose(out=self.pbT[0:1, 0:16], in_=self.bstar[:],
               identity=self.identf[0:16, 0:16]), reads=["bstar", "identf"], writes=["pbT"], tag=f"bt{s}_{c}")
        S.step("vector", lambda v: v.tensor_copy(out=self.nb8row[:], in_=self.pbT[0:1, 0:16]),
               reads=["pbT"], writes=["nb8row"], tag=f"btv{s}_{c}")
        S.step("tensor", lambda e: e.matmul(out=self.pbcB[:, 0:16], lhsT=self.onesrow[:],
               rhs=self.nb8row[:], start=True, stop=True),
               reads=["nb8row", "onesrow"], writes=["pbcB"], tag=f"btb{s}_{c}")
        S.step("scalar", lambda e: e.copy(out=self.nbB[:], in_=self.pbcB[:, 0:16]),
               reads=["pbcB"], writes=["nbB"], tag=f"bte{s}_{c}")

    def emit_attn_core(self, s, c):
        for wave in range(2):
            self.emit_attn_wave(s, c, wave)
        for wave in range(2):
            self.emit_attn_wo_partial(s, c, wave)

    def emit_attn_wave(self, s, c, wave):
        S = self.sched
        for h in range(wave * 8, wave * 8 + 8):
            j, hp = h // 2, h % 2
            for kc in range(4 * c + 4):
                dd = kc - 4 * c
                q0 = max(dd, 0) * 128
                pt = self.ptile[kc % 3]
                ptname = f"ptile{kc % 3}"
                pm, pmname = [(self.pmA, "pmA"), (self.pmB, "pmB"), (self.paux2, "paux2")][kc % 3]

                def f_sc(e, j=j, hp=hp, kc=kc, q0=q0, pm=pm):
                    return e.matmul(out=pm[:, q0:TC],
                                    lhsT=self.kcache[j][hp * 64:(hp + 1) * 64, kc * 128:(kc + 1) * 128],
                                    rhs=self.qb[j][hp * 64:(hp + 1) * 64, q0:TC],
                                    start=True, stop=True)
                S.step("tensor", f_sc, reads=[f"kc{j}", f"qb{j}"], writes=[pmname], tag=f"sc{s}_{c}_{h}_{kc}")
                if q0 > 0:
                    S.step("vector", lambda v, pt=pt, q0=q0: v.memset(pt[:, 0:q0], 0.0),
                           writes=[ptname], tag=f"pz{s}_{c}_{h}_{kc}")
                S.step("scalar", lambda e, pt=pt, h=h, q0=q0, pm=pm: e.activation(
                    out=pt[:, q0:TC], in_=pm[:, q0:TC], func=AF.Exp,
                    bias=self.nbB[:, h:h + 1], scale=0.125),
                    reads=[pmname, "nbB"], writes=[ptname], tag=f"px{s}_{c}_{h}_{kc}")
                if dd >= 0:
                    S.step("vector", lambda v, pt=pt, q0=q0: v.tensor_tensor(
                        out=pt[:, q0:q0 + 128], in0=pt[:, q0:q0 + 128], in1=self.tri01[:], op=ALU.mult),
                        reads=[ptname, "tri01"], writes=[ptname], tag=f"pm{s}_{c}_{h}_{kc}")

                def f_av(e, h=h, kc=kc, pt=pt, c=c):
                    off = (kc * H + h) * 65
                    return e.matmul(out=self.paux1[0:65, 0:TC], lhsT=self.vcache[:, off:off + 65],
                                    rhs=pt[:, 0:TC], start=(kc == 0), stop=(kc == 4 * c + 3))
                S.step("tensor", f_av, reads=[ptname, "vcache", "vcache_ones"], writes=["paux1"],
                       tag=f"av{s}_{c}_{h}_{kc}")
            o64 = self.o64[0]
            S.step("scalar", lambda e, h=h: e.copy(out=self.osb[:, (h % 8) * TC:(h % 8 + 1) * TC],
                   in_=self.paux1[0:64, 0:TC]), reads=["paux1"], writes=["osb"], tag=f"oev{s}_{c}_{h}")
            S.step("scalar", lambda e, h=h, o64=o64: e.copy(out=o64[64:65, :],
                   in_=self.paux1[64:65, 0:TC]), reads=["paux1"], writes=["rowpool"], tag=f"oevs{s}_{c}_{h}")
            rs, rsn = (self.rcpr[0], "rowpoolA") if h % 2 == 0 else (self.n16a[0:1, :], "n16a")
            S.step("sync", lambda e, rs=rs, o64=o64: e.dma_start(out=rs, in_=o64[64:65, :]),
                   reads=["rowpool"], writes=[rsn], tag=f"sdma{s}_{c}_{h}")
            rr, rrn = self.rcpq[h % 2], f"rcpq{h % 2}"
            rbank, rbname = (self.pbcB, "pbcB") if h % 2 == 0 else (self.pbT, "pbT")
            S.step("vector", lambda v, rs=rs, rr=rr: v.reciprocal(out=rr[:], in_=rs),
                   reads=[rsn], writes=[rrn], tag=f"rcph{s}_{c}_{h}")
            S.step("tensor", lambda e, h=h, rr=rr, rbank=rbank: e.matmul(out=rbank[0:64, 0:TC],
                   lhsT=self.onesrow[0:1, 0:64], rhs=rr[:], start=True, stop=True),
                   reads=[rrn, "onesrow"], writes=[rbname], tag=f"rb{s}_{c}_{h}")
            S.step("vector", lambda v, h=h, rbank=rbank: v.tensor_tensor(out=self.attnO[:, h * TC:(h + 1) * TC],
                   in0=self.osb[0:64, (h % 8) * TC:(h % 8 + 1) * TC], in1=rbank[0:64, 0:TC], op=ALU.mult),
                   reads=["osb", rbname], writes=["attnO"], tag=f"onrm{s}_{c}_{h}")

    def emit_attn_wo_partial(self, s, c, wave):
        S = self.sched
        dram = self.dram
        banks = [self.pbT, self.pstat, self.pbcA, self.pbcB, self.pmA, self.pmB, self.paux1, self.paux2]
        bnames = ["pbT", "pstat", "pbcA", "pbcB", "pmA", "pmB", "paux1", "paux2"]
        for h in range(wave * 8, wave * 8 + 8):
            wob = self.wob[h % 2]
            S.step("sync", lambda e, h=h, wob=wob: e.dma_start(
                out=wob[:], in_=dram["wo_bf"][:, h * 1024:(h + 1) * 1024]),
                reads=["wo_bf"], writes=[f"wob{h % 2}"], tag=f"wol{s}_{c}_{h}")

            def f_wo(e, h=h, wob=wob):
                last = None
                for oc in range(DC):
                    last = e.matmul(out=banks[oc][:, 0:TC], lhsT=wob[:, oc * 128:(oc + 1) * 128],
                                    rhs=self.attnO[:, h * TC:(h + 1) * TC],
                                    start=(h == 0), stop=False)
                return last
            S.step("tensor", f_wo, reads=["attnO", f"wob{h % 2}"], writes=bnames,
                   tag=f"wo{s}_{c}_{h}")

    def emit_attn_out(self, s, c, rot):
        S = self.sched
        dram = self.dram
        xfm = self.xfm[rot]
        banks = [self.pbT, self.pstat, self.pbcA, self.pbcB, self.pmA, self.pmB, self.paux1, self.paux2]
        bnames = ["pbT", "pstat", "pbcA", "pbcB", "pmA", "pmB", "paux1", "paux2"]

        def f_aug(e):
            last = None
            for oc in range(DC):
                last = e.matmul(out=banks[oc][:, 0:TC], lhsT=self.borb[0:1, oc * 128:(oc + 1) * 128],
                                rhs=self.onesrow_b[:], start=False, stop=True)
            return last
        S.step("tensor", f_aug, reads=["borb", "onesrow_b"], writes=bnames, tag=f"woaug{s}_{c}")

        for oc in range(DC):
            t1 = self.t1[oc % 2]
            S.step("vector", lambda v, oc=oc, t1=t1: v.scalar_tensor_tensor(
                out=t1[:], in0=banks[oc][:, 0:TC], scalar=1.0, in1=self.xn2f[oc][:], op0=ALU.mult, op1=ALU.add),
                reads=[bnames[oc], f"xn2f{oc}"], writes=["t1_s"], tag=f"woe{s}_{c}_{oc}")
            S.step("vector", lambda v, oc=oc, t1=t1: v.tensor_tensor(
                out=xfm[oc][:], in0=t1[:], in1=xfm[oc][:], op=ALU.add),
                reads=["t1_s", f"xfm{rot}_{oc}"], writes=[f"xfm{rot}_{oc}"], tag=f"wor{s}_{c}_{oc}")

    def emit_chunk_in(self, s, c, src, src_tm, rot):
        S = self.sched
        xfm = self.xfm[rot]
        srcap = self.dram[src]
        if src_tm:
            for half in range(2):
                for i in range(2):
                    ts = half * 2 + i
                    xt = self.xtm[i]
                    S.step("sync", lambda e, xt=xt, s=s, c=c, ts=ts: e.dma_start(
                        out=xt[:], in_=srcap[s, c * TC + ts * 128: c * TC + (ts + 1) * 128, :]),
                        writes=[f"xtm{ts % 2}"], tag=f"in_tm{s}_{c}_{ts}")
                for dc in range(DC):
                    pb, pbname = (self.pbT, "pbT") if dc % 2 == 0 else (self.pstat, "pstat")
                    def f_tr(e, dc=dc, pb=pb):
                        e.transpose(out=pb[:, 0:128],
                                    in_=self.xtm[0][:, dc * 128:(dc + 1) * 128],
                                    identity=self.identf[:])
                        return e.transpose(out=pb[:, 128:256],
                                           in_=self.xtm[1][:, dc * 128:(dc + 1) * 128],
                                           identity=self.identf[:])
                    S.step("tensor", f_tr, reads=["xtm0", "xtm1", "identf"],
                           writes=[pbname], tag=f"in_tr{s}_{c}_{half}_{dc}")
                    S.step("scalar", lambda e, dc=dc, half=half, pb=pb: e.copy(
                        out=xfm[dc][:, half * 256:(half + 1) * 256], in_=pb[:, 0:256]),
                        reads=[pbname], writes=[f"xfm{rot}_{dc}"], tag=f"in_ev{s}_{c}_{half}_{dc}")
        else:
            for dc in range(DC):
                S.step("sync", lambda e, s=s, c=c, dc=dc: e.dma_start(
                    out=xfm[dc][:], in_=srcap[s, dc, :, c * TC:(c + 1) * TC]),
                    writes=[f"xfm{rot}_{dc}"], tag=f"in_fm{s}_{c}_{dc}")

    def emit_ln_stats(self, s, c, rot):
        S = self.sched
        xfm, csm = self.xfm[rot], self.csm
        for dc in range(DC):
            x2 = self.x2t[dc % 2]
            S.step("scalar", lambda e, dc=dc, x2=x2: e.square(out=x2[:], in_=xfm[dc][:]),
                   reads=[f"xfm{rot}_{dc}"], writes=["x2t_s"], tag=f"sq{s}_{c}_{dc}")

            def f_mm(e, dc=dc, x2=x2):
                e.matmul(out=self.pstat[0:1, 0:TC], lhsT=self.onescol[:], rhs=xfm[dc][:],
                         start=(dc == 0), stop=(dc == DC - 1))
                return e.matmul(out=self.pbcA[0:1, 0:TC], lhsT=self.onescol[:], rhs=x2[:],
                                start=(dc == 0), stop=(dc == DC - 1))
            S.step("tensor", f_mm, reads=[f"xfm{rot}_{dc}", "x2t_s", "onescol"],
                   writes=["pstat", "pbcA"], tag=f"stmm{s}_{c}_{dc}")

        S.step("vector", lambda v: v.tensor_scalar(out=self.rmean, in0=self.pstat[0:1, 0:TC],
               scalar1=1.0 / D, scalar2=None, op0=ALU.mult),
               reads=["pstat"], writes=["rmean"], tag=f"stm{s}_{c}")
        S.step("vector", lambda v: v.tensor_scalar(out=self.rvar, in0=self.pbcA[0:1, 0:TC],
               scalar1=1.0 / D, scalar2=None, op0=ALU.mult),
               reads=["pbcA"], writes=["rvar"], tag=f"ste{s}_{c}")
        S.step("vector", lambda v: v.tensor_tensor(out=self.rtmp, in0=self.rmean, in1=self.rmean, op=ALU.mult),
               reads=["rmean"], writes=["rtmp"], tag=f"stm2{s}_{c}")
        S.step("vector", lambda v: v.tensor_tensor(out=self.rvar, in0=self.rvar, in1=self.rtmp, op=ALU.subtract),
               reads=["rtmp", "rvar"], writes=["rvar"], tag=f"stv{s}_{c}")

        S.step("scalar", lambda sc: sc.activation(out=self.rstd, in_=self.rvar, func=AF.Ln,
               bias=csm[0:1, 0:1], scale=1.0),
               reads=["rvar", "csm"], writes=["rstd"], tag=f"strln{s}_{c}")
        S.step("scalar", lambda sc: sc.activation(out=self.rstd, in_=self.rstd, func=AF.Exp,
               bias=csm[0:1, 3:4], scale=-0.5),
               reads=["rstd", "csm"], writes=["rstd"], tag=f"strex{s}_{c}")

        def f_bc(e):
            e.matmul(out=self.pbcA[:, 0:TC], lhsT=self.onesrow[:], rhs=self.rmean, start=True, stop=True)
            return e.matmul(out=self.pbcB[:, 0:TC], lhsT=self.onesrow[:], rhs=self.rstd, start=True, stop=True)
        S.step("tensor", f_bc, reads=["rmean", "rstd", "onesrow"], writes=["pbcA", "pbcB"], tag=f"stbc{s}_{c}")

    def emit_xn(self, s, c, rot):
        S = self.sched
        xfm, xnb = self.xfm[rot], self.xnb[rot]
        for dc in range(DC):
            t1 = self.t1[dc % 2]

            def f1(v, dc=dc, t1=t1):
                return v.tensor_tensor(out=t1[:], in0=xfm[dc][:], in1=self.pbcA[:, 0:TC], op=ALU.subtract)
            S.step("vector", f1, reads=[f"xfm{rot}_{dc}", "pbcA"], writes=["t1_s"],
                   tag=f"xn1_{s}_{c}_{dc}")

            def f2(v, dc=dc, t1=t1):
                return v.tensor_tensor(out=xnb[dc][:], in0=t1[:], in1=self.pbcB[:, 0:TC], op=ALU.mult)
            S.step("vector", f2, reads=["t1_s", "pbcB"], writes=[f"xnb{rot}_{dc}"],
                   tag=f"xn2_{s}_{c}_{dc}")

    def emit_ffn_seq(self, s, src, src_tm, dst, dst_tm):
        for c in range(NCH):
            rot = 0
            self.emit_chunk_in(s, c, src, src_tm, rot)
            dbg = getattr(self, "dbg", False) and s == 0 and c == 0
            if dbg:
                self.dump("xfm0", self.xfm[rot][0][:], [f"xfm{rot}_0"], [128, TC])
            self.emit_ln_stats(s, c, rot)
            if dbg:
                self.dump("rmean", self.rmean, ["rmean"], [1, TC])
                self.dump("rstd", self.rstd, ["rstd"], [1, TC])
            self.emit_xn(s, c, rot)
            if dbg:
                self.dump("xnb0", self.xnb[rot][0][:], [f"xnb{rot}_0"], [128, TC], BF16)
            self.emit_ffn_chunk(s, c, rot)
            if dbg:
                self.dump("xout0", self.xfm[rot][0][:], [f"xfm{rot}_0"], [128, TC])
            self.emit_chunk_out(s, c, dst, dst_tm, rot)

    def emit_ffn_chunk(self, s, c, rot):
        S = self.sched
        xnb, xfm = self.xnb[rot], self.xfm[rot]
        xn_reads = [f"xnb{rot}_{k}" for k in range(DC)]

        def f_pre(e):
            for k in range(DC):
                e.matmul(out=self.paux1[0:RK, 0:TC], lhsT=self.sATb[:, k * RK:(k + 1) * RK],
                         rhs=xnb[k][:], start=(k == 0), stop=(k == DC - 1))
            for k in range(DC):
                e.matmul(out=self.pstat[0:E, 0:TC], lhsT=self.rWTb[:, k * E:(k + 1) * E],
                         rhs=xnb[k][:], start=(k == 0), stop=(k == DC - 1))
            last = None
            for k in range(DC):
                last = e.matmul(out=self.paux2[0:E * RK, 0:TC],
                                lhsT=self.rATb[:, k * E * RK:(k + 1) * E * RK],
                                rhs=xnb[k][:], start=(k == 0), stop=(k == DC - 1))
            return last
        S.step("tensor", f_pre, reads=xn_reads + ["sATb", "rWTb", "rATb"],
               writes=["paux1", "pstat", "paux2"], tag=f"ffnpre{s}_{c}")

        S.step("scalar", lambda e: e.copy(out=self.shb[:], in_=self.paux1[0:RK, 0:TC]),
               reads=["paux1"], writes=["shb"], tag=f"shev{s}_{c}")
        if getattr(self, "dbg", False) and s == 0 and c == 0:
            self.dump("shb", self.shb[:], ["shb"], [RK, TC], BF16)

        def f_lg(v):
            return v.tensor_scalar(out=self.lrow[:], in0=self.pstat[0:E, 0:TC],
                                   scalar1=self.rbcol[:, 0:1], scalar2=None, op0=ALU.add)
        S.step("vector", f_lg, reads=["pstat", "rbcol"], writes=["lrow"], tag=f"lg{s}_{c}")
        if getattr(self, "dbg", False) and s == 0 and c == 0:
            self.dump("lrow", self.lrow[:], ["lrow"], [E, TC])

        def f_ltr(e):
            last = None
            for ts in range(4):
                last = e.transpose(out=self.pbT[:, ts * 8: ts * 8 + 6],
                                   in_=self.lrow[:, ts * 128:(ts + 1) * 128],
                                   identity=self.identf[0:6, 0:6])
            return last
        S.step("tensor", f_ltr, reads=["lrow", "identf"], writes=["pbT"], tag=f"ltr{s}_{c}")

        ltm, t2s, swtm = self.ltm, self.t2s, self.swtm

        def f_t2cp(v):
            last = None
            for ts in range(4):
                last = v.tensor_copy(out=ltm[:, ts * 8:ts * 8 + 6], in_=self.pbT[:, ts * 8:ts * 8 + 6])
            return last
        S.step("vector", f_t2cp, reads=["pbT"], writes=["ltm"], tag=f"t2cp{s}_{c}")

        def f_t2mx(v):
            last = None
            for ts in range(4):
                last = v.tensor_reduce(out=t2s[:, ts * 8: ts * 8 + 1], in_=ltm[:, ts * 8:ts * 8 + 6],
                                       axis=AX.X, op=ALU.max)
            return last
        S.step("vector", f_t2mx, reads=["ltm"], writes=["t2s"], tag=f"t2mx{s}_{c}")

        def f_t2ng(v):
            last = None
            for ts in range(4):
                m6 = t2s[:, ts * 8: ts * 8 + 1]
                last = v.tensor_scalar(out=m6, in0=m6, scalar1=-1.0, scalar2=None, op0=ALU.mult)
            return last
        S.step("vector", f_t2ng, reads=["t2s"], writes=["t2s"], tag=f"t2ng{s}_{c}")
        if getattr(self, "dbg", False) and s == 0 and c == 0:
            self.dump("ltm_a", ltm[:], ["ltm"], [128, 32])
            self.dump("t2s_a", t2s[:], ["t2s"], [128, 32])

        def f_t2e(sc):
            last = None
            for ts in range(4):
                lt = ltm[:, ts * 8:ts * 8 + 6]
                last = sc.activation(out=lt, in_=lt, func=AF.Exp,
                                     bias=t2s[:, ts * 8: ts * 8 + 1], scale=1.0)
            return last
        S.step("scalar", f_t2e, reads=["ltm", "t2s"], writes=["ltm"], tag=f"t2e{s}_{c}")
        if getattr(self, "dbg", False) and s == 0 and c == 0:
            self.dump("ltm_e", ltm[:], ["ltm"], [128, 32])

        def batch4(name, fn, reads, writes):
            def f(v):
                last = None
                for ts in range(4):
                    last = fn(v, ts)
                return last
            S.step("vector", f, reads=reads, writes=writes, tag=f"{name}{s}_{c}")

        batch4("t2sum", lambda v, ts: v.tensor_reduce(
            out=t2s[:, ts * 8 + 2: ts * 8 + 3], in_=ltm[:, ts * 8:ts * 8 + 6], axis=AX.X, op=ALU.add),
            ["ltm"], ["t2s"])
        batch4("t2rcp", lambda v, ts: v.reciprocal(
            out=t2s[:, ts * 8 + 3: ts * 8 + 4], in_=t2s[:, ts * 8 + 2: ts * 8 + 3]),
            ["t2s"], ["t2s"])
        batch4("t2gt", lambda v, ts: v.tensor_scalar(
            out=ltm[:, ts * 8:ts * 8 + 6], in0=ltm[:, ts * 8:ts * 8 + 6],
            scalar1=t2s[:, ts * 8 + 3: ts * 8 + 4], scalar2=None, op0=ALU.mult),
            ["ltm", "t2s"], ["ltm"])
        batch4("t2m1", lambda v, ts: v.tensor_reduce(
            out=t2s[:, ts * 8 + 4: ts * 8 + 5], in_=ltm[:, ts * 8:ts * 8 + 6], axis=AX.X, op=ALU.max),
            ["ltm"], ["t2s"])
        batch4("t2eq", lambda v, ts: v.tensor_scalar(
            out=swtm[:, ts * 8:ts * 8 + 6], in0=ltm[:, ts * 8:ts * 8 + 6],
            scalar1=t2s[:, ts * 8 + 4: ts * 8 + 5], scalar2=None, op0=ALU.is_equal),
            ["ltm", "t2s"], ["swtm"])
        batch4("t2g2", lambda v, ts: v.scalar_tensor_tensor(
            out=swtm[:, ts * 8:ts * 8 + 6], in0=swtm[:, ts * 8:ts * 8 + 6], scalar=-2.0,
            in1=ltm[:, ts * 8:ts * 8 + 6], op0=ALU.mult, op1=ALU.add),
            ["swtm", "ltm"], ["swtm"])
        batch4("t2m2", lambda v, ts: v.tensor_reduce(
            out=t2s[:, ts * 8 + 5: ts * 8 + 6], in_=swtm[:, ts * 8:ts * 8 + 6], axis=AX.X, op=ALU.max),
            ["swtm"], ["t2s"])
        batch4("t2kp", lambda v, ts: v.tensor_scalar(
            out=swtm[:, ts * 8:ts * 8 + 6], in0=ltm[:, ts * 8:ts * 8 + 6],
            scalar1=t2s[:, ts * 8 + 5: ts * 8 + 6], scalar2=None, op0=ALU.is_ge),
            ["ltm", "t2s"], ["swtm"])
        batch4("t2dn", lambda v, ts: v.tensor_tensor(
            out=t2s[:, ts * 8 + 6: ts * 8 + 7], in0=t2s[:, ts * 8 + 4: ts * 8 + 5],
            in1=t2s[:, ts * 8 + 5: ts * 8 + 6], op=ALU.add),
            ["t2s"], ["t2s"])
        batch4("t2dn2", lambda v, ts: v.tensor_scalar(
            out=t2s[:, ts * 8 + 6: ts * 8 + 7], in0=t2s[:, ts * 8 + 6: ts * 8 + 7],
            scalar1=1e-6, scalar2=None, op0=ALU.add),
            ["t2s"], ["t2s"])
        batch4("t2rd", lambda v, ts: v.reciprocal(
            out=t2s[:, ts * 8 + 7: ts * 8 + 8], in_=t2s[:, ts * 8 + 6: ts * 8 + 7]),
            ["t2s"], ["t2s"])
        batch4("t2sw", lambda v, ts: v.tensor_tensor(
            out=swtm[:, ts * 8:ts * 8 + 6], in0=swtm[:, ts * 8:ts * 8 + 6],
            in1=ltm[:, ts * 8:ts * 8 + 6], op=ALU.mult),
            ["swtm", "ltm"], ["swtm"])
        batch4("t2sc", lambda v, ts: v.tensor_scalar(
            out=swtm[:, ts * 8:ts * 8 + 6], in0=swtm[:, ts * 8:ts * 8 + 6],
            scalar1=t2s[:, ts * 8 + 7: ts * 8 + 8], scalar2=None, op0=ALU.mult),
            ["swtm", "t2s"], ["swtm"])
        if getattr(self, "dbg", False) and s == 0 and c == 0:
            self.dump("swtm", self.swtm[:], ["swtm"], [128, 32])

        def f_str(e):
            last = None
            for ts in range(4):
                last = e.transpose(out=self.pbT[0:6, ts * 128:(ts + 1) * 128],
                                   in_=self.swtm[:, ts * 8:ts * 8 + 6],
                                   identity=self.identf[:])
            return last
        S.step("tensor", f_str, reads=["swtm", "identf"], writes=["pbT"], tag=f"swtr{s}_{c}")

        S.step("scalar", lambda e: e.copy(out=self.swfb[:], in_=self.pbT[0:6, 0:TC]),
               reads=["pbT"], writes=["swfb"], tag=f"swev{s}_{c}")

        def f_rep(e):
            return e.matmul(out=self.paux1[0:E * RK, 0:TC], lhsT=self.rep16[:], rhs=self.swfb[:],
                            start=True, stop=True)
        S.step("tensor", f_rep, reads=["swfb", "rep16"], writes=["paux1"], tag=f"swrep{s}_{c}")

        S.step("scalar", lambda e: e.copy(out=self.swrb[:], in_=self.paux1[0:E * RK, 0:TC]),
               reads=["paux1"], writes=["swrb"], tag=f"swrev{s}_{c}")

        def f_hw(v):
            return v.tensor_tensor(out=self.hwb[:], in0=self.paux2[0:E * RK, 0:TC],
                                   in1=self.swrb[:], op=ALU.mult)
        S.step("vector", f_hw, reads=["paux2", "swrb"], writes=["hwb"], tag=f"hw{s}_{c}")
        if getattr(self, "dbg", False) and s == 0 and c == 0:
            self.dump("hwb", self.hwb[:], ["hwb"], [E * RK, TC], BF16)
            self.dump("swrb", self.swrb[:], ["swrb"], [E * RK, TC])

        for oc in range(DC):
            pm = self.pmA if oc % 2 == 0 else self.pmB
            pmname = "pmA" if oc % 2 == 0 else "pmB"

            def f_main(e, oc=oc, pm=pm):
                for k in range(DC):
                    e.matmul(out=pm[:, 0:TC], lhsT=self.baseWTb[k][:, oc * 128:(oc + 1) * 128],
                             rhs=xnb[k][:], start=(k == 0), stop=False)
                e.matmul(out=pm[:, 0:TC], lhsT=self.sBTb[:, oc * 128:(oc + 1) * 128],
                         rhs=self.shb[:], start=False, stop=False)
                return e.matmul(out=pm[:, 0:TC], lhsT=self.rBfb[:, oc * 128:(oc + 1) * 128],
                                rhs=self.hwb[:], start=False, stop=True)
            S.step("tensor", f_main,
                   reads=xn_reads + ["shb", "hwb"] + [f"baseWTb{k}" for k in range(DC)] + ["sBTb", "rBfb"],
                   writes=[pmname], tag=f"main{s}_{c}_{oc}")

            def f_ev(v, oc=oc, pm=pm):
                return v.scalar_tensor_tensor(out=xfm[oc][:], in0=pm[:, 0:TC],
                                              scalar=self.bbcol[:, oc:oc + 1], in1=xfm[oc][:],
                                              op0=ALU.add, op1=ALU.add)
            S.step("vector", f_ev, reads=[pmname, "bbcol", f"xfm{rot}_{oc}"],
                   writes=[f"xfm{rot}_{oc}"], tag=f"mainev{s}_{c}_{oc}")

    def emit_chunk_out(self, s, c, dst, dst_tm, rot):
        S = self.sched
        xfm = self.xfm[rot]
        dstap = self.dram[dst]
        if dst_tm:
            for ts in range(4):
                for hf in range(2):
                    pm, pmname = (self.pmA, "pmA") if hf == 0 else (self.pmB, "pmB")

                    def f_tr(e, ts=ts, hf=hf, pm=pm):
                        last = None
                        for i in range(4):
                            dc = hf * 4 + i
                            last = e.transpose(out=pm[:, i * 128:(i + 1) * 128],
                                               in_=xfm[dc][:, ts * 128:(ts + 1) * 128],
                                               identity=self.identf[:])
                        return last
                    S.step("tensor", f_tr, reads=[f"xfm{rot}_{k}" for k in range(DC)] + ["identf"],
                           writes=[pmname], tag=f"otr{s}_{c}_{ts}_{hf}")
                    ot = self.otm[0]
                    S.step("scalar", lambda sc, ot=ot, pm=pm: sc.copy(out=ot[:], in_=pm[:, 0:512]),
                           reads=[pmname], writes=["otm0"], tag=f"oev{s}_{c}_{ts}_{hf}")
                    S.step("sync", lambda e, s=s, c=c, ts=ts, hf=hf, ot=ot: e.dma_start(
                        out=dstap[s, c * TC + ts * 128: c * TC + (ts + 1) * 128,
                                  hf * 512:(hf + 1) * 512], in_=ot[:]),
                        reads=["otm0"], tag=f"odma{s}_{c}_{ts}_{hf}")
        else:
            for dc in range(DC):
                S.step("sync", lambda e, s=s, c=c, dc=dc: e.dma_start(
                    out=dstap[s, dc, :, c * TC:(c + 1) * TC], in_=xfm[dc][:]),
                    reads=[f"xfm{rot}_{dc}"], tag=f"odma{s}_{c}_{dc}")


def run_dev(x, weights, stages, dbg=False):
    mk = MK(stages=stages)
    mk.dbg = dbg
    nc = mk.build()
    in_map = {"x": np.ascontiguousarray(x.astype(np.float32))}
    in_map.update(prep_weights(weights))
    res = run_bass_kernel_spmd(nc, [in_map], core_ids=[0])
    return res.results[0]


def np_layernorm(xx, eps=1e-5):
    m = xx.mean(-1, keepdims=True)
    v = xx.var(-1, keepdims=True)
    return (xx - m) / np.sqrt(v + eps)




def kernel(**inputs):
    x_full = np.ascontiguousarray(np.asarray(inputs["L0_attn_a_x"], np.float32))
    wprep = prep_weights(inputs)
    mk = MK(stages=("attn", "ffn1", "ssm", "ffn2"), nseq=NSEQ)
    nc = mk.build()
    n_cores = 8
    in_maps = []
    for c in range(n_cores):
        m = {"x": x_full[c * NSEQ:(c + 1) * NSEQ]}
        m.update(wprep)
        in_maps.append(m)
    res = run_bass_kernel_spmd(nc, in_maps, core_ids=list(range(n_cores)))
    out = np.concatenate([np.asarray(r["y"]) for r in res.results], axis=0)
    return out.astype(np.float32)


# version 28
# speedup vs baseline: 1.0244x; 1.0244x over previous
import numpy as np
import ml_dtypes
from contextlib import ExitStack
import concourse.bass as bass
import concourse.mybir as mybir
from concourse.bass_utils import run_bass_kernel_spmd

F32 = mybir.dt.float32
BF16 = mybir.dt.bfloat16
I32 = mybir.dt.int32
AF = mybir.ActivationFunctionType
ALU = mybir.AluOpType
AX = mybir.AxisListType

D, T, TC = 1024, 2048, 512
NCH = T // TC
NSEQ = 2
DC = D // 128
H, HD, ST = 16, 64, 16
E, RK = 6, 16
LN_EPS = 1e-5
RMS_EPS = 1e-6


class Step:
    __slots__ = ("eng", "fn", "deps", "ordinal", "tag", "idx")

    def __init__(self, eng, fn, deps, tag, idx):
        self.eng, self.fn, self.deps, self.tag, self.idx = eng, fn, deps, tag, idx
        self.ordinal = 0


class Sched:
    def __init__(self):
        self.steps = []
        self.eng_count = {}
        self.writer = {}
        self.readers = {}
        self.floor = []

    def step(self, eng, fn, reads=(), writes=(), tag=""):
        deps = set(self.floor)
        for b in reads:
            w = self.writer.get(b)
            if w is not None:
                deps.add(w)
        for b in writes:
            w = self.writer.get(b)
            if w is not None:
                deps.add(w)
            for r in self.readers.get(b, ()):
                deps.add(r)
        s = Step(eng, fn, deps, tag, len(self.steps))
        s.ordinal = self.eng_count.get(eng, 0) + 1
        self.eng_count[eng] = s.ordinal
        self.steps.append(s)
        for b in reads:
            self.readers.setdefault(b, []).append(s)
        for b in writes:
            self.writer[b] = s
            self.readers[b] = []
        return s

    def barrier(self):
        engs = ("sync", "tensor", "vector", "scalar", "gpsimd")
        lasts = {}
        for st in self.steps:
            lasts[st.eng] = st
        newfloor = []
        for en in engs:
            deps = set(v for k, v in lasts.items() if k != en)
            stp = Step(en, (lambda e: e.nop()), deps, "barrier", len(self.steps))
            stp.ordinal = self.eng_count.get(en, 0) + 1
            self.eng_count[en] = stp.ordinal
            self.steps.append(stp)
            newfloor.append(stp)
        self.floor = newfloor
        self.writer = {}
        self.readers = {}


    _COST = {
        "tensor": {"qmm": 2300, "kmm": 2300, "vmm": 2300, "dmm": 2300, "gmm": 2300, "omm": 2300,
                   "main": 2600, "ffnpre": 6000, "wo": 2000, "sc": 450, "av": 450,
                   "stmm": 700, "bqm": 700, "bkm": 700, "in_tr": 700, "otr": 1200,
                   "nrep": 1800, "bc": 4000, "stbc": 600, "a_bc": 600, "rb": 600,
                   "ltr": 400, "swtr": 400, "swrep": 500, "bt": 300, "btb": 300},
        "vector": {"sc": 4000, "wm": 2150, "hcm": 2150, "cv": 1200,
                   "xn1": 700, "xn2": 700, "a_x1": 700, "a_x2": 1200, "sx1": 700, "sx2": 700,
                   "woe": 1200, "wor": 700, "mainev": 1200, "onrm": 700, "yg": 700,
                   "dp1": 400, "du": 450, "hc": 150, "rcph": 1500, "rcp": 4500, "pz": 300, "pm": 300},
        "scalar": {"dk": 1250, "px": 650, "dex": 650, "dln": 650, "qev": 700, "kev": 700,
                   "vev": 800, "oev": 700, "oevs": 350, "in_ev": 600, "sq": 700,
                   "cvs": 650, "cvx": 650, "gev": 650, "a_x3": 650, "shev": 500,
                   "swev": 400, "swrev": 500, "bre": 550, "cre": 550, "cast": 900},
        "sync": {"sdma": 2500, "wq": 6000, "wk": 6000, "wv": 6000, "swd": 6000, "swg": 6000, "swo": 6000,
                 "in_tm": 3000, "in_fm": 1800, "odma": 1800, "srl": 1800, "xsc": 1800,
                 "sdma": 1500, "rdma": 1500, "bdma": 1500, "cdma": 1500, "wol": 1500,
                 "ld_": 2500, "sld": 2500, "st_": 4000},
        "gpsimd": {"ya": 2400, "hcm": 2000},
    }
    _DEFC = {"tensor": 500, "vector": 800, "scalar": 700, "sync": 2000, "gpsimd": 1200}

    def _cost(self, st):
        table = self._COST.get(st.eng, {})
        for pref, c in table.items():
            if st.tag.startswith(pref):
                return c
        return self._DEFC.get(st.eng, 800)

    def reorder(self, only_segments=None, freeze=()):
        segs = []
        cur = []
        i = 0
        n = len(self.steps)
        while i < n:
            st = self.steps[i]
            if st.tag == "barrier":
                segs.append(cur)
                segs.append(self.steps[i:i + 5])
                cur = []
                i += 5
            else:
                cur.append(st)
                i += 1
        segs.append(cur)
        out = []
        segidx = -1
        for seg in segs:
            if not seg or seg[0].tag == "barrier":
                out.extend(seg)
                continue
            segidx += 1
            if only_segments is not None and segidx not in only_segments:
                out.extend(seg)
                continue
            segset = set(id(x) for x in seg)
            cost = {id(x): self._cost(x) for x in seg}
            succ = {id(x): [] for x in seg}
            indeg = {id(x): 0 for x in seg}
            extra = {}
            prev_by_eng = {}
            for x in seg:
                if x.eng in freeze:
                    p = prev_by_eng.get(x.eng)
                    if p is not None:
                        extra[id(x)] = p
                    prev_by_eng[x.eng] = x
            for x in seg:
                for d in x.deps:
                    if id(d) in segset:
                        succ[id(d)].append(x)
                        indeg[id(x)] += 1
                p = extra.get(id(x))
                if p is not None:
                    succ[id(p)].append(x)
                    indeg[id(x)] += 1
            cp = {}
            for x in reversed(seg):
                cp[id(x)] = cost[id(x)] + max((cp[id(t)] for t in succ[id(x)]), default=0)
            import heapq
            ready = []
            seqn = 0
            for x in seg:
                if indeg[id(x)] == 0:
                    heapq.heappush(ready, (-cp[id(x)], seqn, x))
                    seqn += 1
            eng_free = {}
            fin = {}
            order = []
            while ready:
                _, _, x = heapq.heappop(ready)
                dep_t = max((fin.get(id(d), 0) for d in x.deps), default=0)
                stt = max(eng_free.get(x.eng, 0), dep_t)
                fin[id(x)] = stt + cost[id(x)]
                eng_free[x.eng] = fin[id(x)]
                order.append(x)
                for t in succ[id(x)]:
                    indeg[id(t)] -= 1
                    if indeg[id(t)] == 0:
                        heapq.heappush(ready, (-cp[id(t)], seqn, t))
                        seqn += 1
            assert len(order) == len(seg), "scheduler lost steps"
            out.extend(order)
        assert len(out) == n
        self.steps = out
        cnt = {}
        for x in self.steps:
            cnt[x.eng] = cnt.get(x.eng, 0) + 1
            x.ordinal = cnt[x.eng]
        self.eng_count = cnt

    def emit_one(self, ename, eng, sems):
        unit = {"sync": 16}
        seen = {}
        for s in self.steps:
            if s.eng != ename:
                continue
            need = {}
            for d in s.deps:
                need[d.eng] = max(need.get(d.eng, 0), d.ordinal)
            for de, o in sorted(need.items()):
                v = o * unit.get(de, 1)
                if seen.get(de, 0) < v:
                    eng.wait_ge(sems[de], v)
                    seen[de] = v
            last = s.fn(eng)
            assert last is not None, f"step {s.tag} returned no instruction"
            last.then_inc(sems[ename], unit.get(ename, 1))
        if ename == "sync":
            eng.wait_ge(sems["sync"], self.eng_count.get("sync", 0) * 16)


def prep_weights(w):
    t = lambda a: np.ascontiguousarray(np.asarray(a).T)
    out = {
        "WqT": t(w["L0_attn_a_Wq"]), "WkT": t(w["L0_attn_a_Wk"]),
        "WvT": t(w["L0_attn_a_Wv"]), "WoT": t(w["L0_attn_a_Wo"]),
        "bq_row": np.asarray(w["L0_attn_a_bq"]).reshape(1, D).copy(),
        "bk_row": np.asarray(w["L0_attn_a_bk"]).reshape(1, D).copy(),
        "bv_row": np.asarray(w["L0_attn_a_bv"]).reshape(1, D).copy(),
        "bo_row": np.asarray(w["L0_attn_a_bo"]).reshape(1, D).copy(),
        "norm_w": np.asarray(w["L0_attn_a_norm_w"]).reshape(DC, 128).T.copy(),
        "baseWT": t(w["L0_ffn_base_W"]),
        "bb": np.asarray(w["L0_ffn_base_b"]).reshape(DC, 128).T.copy(),
        "sAT": t(w["L0_ffn_shared_A"]),
        "sBT": t(w["L0_ffn_shared_B"]),
        "rAT": t(np.asarray(w["L0_ffn_routed_A"]).reshape(E * RK, D)),
        "rBf": np.ascontiguousarray(
            np.asarray(w["L0_ffn_routed_B"]).transpose(0, 2, 1).reshape(E * RK, D)),
        "rWT": t(w["L0_ffn_router_W"]),
        "rb": np.asarray(w["L0_ffn_router_b"]).reshape(E, 1).copy(),
        "convw": np.asarray(w["L0_attn_b_conv_w"]).reshape(D, 4).copy(),
        "convb": np.asarray(w["L0_attn_b_conv_b"]).reshape(DC, 128).T.copy(),
        "WdT": t(w["L0_attn_b_Wd"]),
        "bd": np.asarray(w["L0_attn_b_bd"]).reshape(DC, 128).T.copy(),
        "WbT": t(w["L0_attn_b_Wb"]),
        "WcT": t(w["L0_attn_b_Wc"]),
        "log_a": np.asarray(w["L0_attn_b_log_a"]).copy(),
        "WgT": t(w["L0_attn_b_Wg"]),
        "bg": np.asarray(w["L0_attn_b_bg"]).reshape(DC, 128).T.copy(),
        "WobT": t(w["L0_attn_b_Wo"]),
        "bob": np.asarray(w["L0_attn_b_bo"]).reshape(DC, 128).T.copy(),
    }
    return {k: np.ascontiguousarray(v.astype(np.float32)) for k, v in out.items()}


WSHAPES = {
    "WqT": [D, D], "WkT": [D, D], "WvT": [D, D], "WoT": [D, D],
    "bq_row": [1, D], "bk_row": [1, D], "bv_row": [1, D], "bo_row": [1, D],
    "norm_w": [128, DC],
    "baseWT": [D, D], "bb": [128, DC], "sAT": [D, RK], "sBT": [RK, D],
    "rAT": [D, E * RK], "rBf": [E * RK, D], "rWT": [D, E], "rb": [E, 1],
    "convw": [D, 4], "convb": [128, DC], "WdT": [D, D], "bd": [128, DC],
    "WbT": [D, ST], "WcT": [D, ST], "log_a": [D, ST], "WgT": [D, D],
    "bg": [128, DC], "WobT": [D, D], "bob": [128, DC],
}


class Arena:
    def __init__(self, mk, bytes_per_part):
        self.cols4 = bytes_per_part // 4
        self.t = mk.sb("arena", [128, self.cols4], F32)
        self.off4 = 0

    def reset(self):
        self.off4 = 0

    def alloc(self, shape, dt=F32):
        rows, cols = shape
        esz = 2 if dt == BF16 else 4
        n4 = (cols * esz + 63) // 64 * 16
        assert self.off4 + n4 <= self.cols4, f"arena overflow at {shape} {dt}: {self.off4*4+n4*4} > {self.cols4*4}"
        self.last_start = self.off4
        base = self.t[0:rows, self.off4:self.off4 + n4]
        self.off4 += n4
        if dt == F32:
            return base[:, 0:cols]
        return base.bitcast(dt)[:, 0:cols]


class MK:
    def __init__(self, stages=("ffn1",), nseq=NSEQ):
        self.stages = stages
        self.nseq = nseq
        self.nc = bass.Bass("TRN2", target_bir_lowering=False)
        self.sched = Sched()
        self.st = ExitStack()
        self.dram = {}

    def sb(self, name, shape, dt=F32):
        return self.st.enter_context(self.nc.sbuf_tensor(name, shape, dt))

    def asb(self, name, shape, dt=F32):
        ap = self.ar.alloc(shape, dt)
        key = (self.ar.last_start, tuple(shape), str(dt))
        if not hasattr(self, "_stable"):
            self._stable = {}
        prev = self._stable.get(name)
        assert prev is None or prev == key, f"arena attr {name} moved: {prev} -> {key}"
        self._stable[name] = key
        return ap

    def ps(self, name, shape, dt=F32):
        return self.st.enter_context(self.nc.psum_tensor(name, shape, dt))

    def din(self, name, shape, dt=F32):
        a = self.nc.dram_tensor(name, shape, dt, kind="ExternalInput").ap()
        self.dram[name] = a
        return a

    def dout(self, name, shape, dt=F32):
        a = self.nc.dram_tensor(name, shape, dt, kind="ExternalOutput").ap()
        self.dram[name] = a
        return a

    def dump(self, name, ap, readbufs, shape, dt=F32):
        d = self.dout("dbg_" + name, shape, dt)
        self.sched.step("sync", lambda e, ap=ap, d=d: e.dma_start(out=d, in_=ap),
                        reads=list(readbufs), tag="dump_" + name)

    def dint(self, name, shape, dt=F32):
        a = self.nc.dram_tensor(name, shape, dt, kind="Internal").ap()
        self.dram[name] = a
        return a

    def build(self):
        nc, S, ns = self.nc, self.sched, self.nseq
        self.din("x", [ns, T, D])
        self.dout("y", [ns, T, D])
        for n, sh in WSHAPES.items():
            self.din(n, sh)
        import os as _os
        n_bounce = len(self.stages) - 1
        for i in range(n_bounce):
            self.dout(f"xs{i}", [ns, DC, 128, T])

        if "ssm" in self.stages and self.stages[0] == "ssm":
            self.dout("xres_sc", [ns, DC, 128, T])

        self.identf = self.sb("identf", [128, 128], F32)
        self.iotab = self.sb("iotab", [128, 128], I32)
        self.onescol = self.sb("onescol", [128, 1], F32)
        self.onesrow = self.sb("onesrow", [1, 128], F32)
        self.onesrow_b = self.sb("onesrow_b", [1, TC], BF16)
        self.csm = self.sb("csm", [1, 8], F32)
        self.c6col = self.sb("c6col", [128, 1], F32)
        self.zcol = self.sb("zcol", [128, 1], F32)
        self.wl = [self.sb(f"wl{i}", [128, 512], F32) for i in range(1)]
        self.emit_consts()

        need_ssm = "ssm" in self.stages
        self.xtm = [self.sb(f"xtm{i}", [128, 1024], F32) for i in range(2)]
        self.xfm = [[self.sb(f"xfm{r}_{dc}", [128, TC], F32) for dc in range(DC)] for r in range(1)]
        t1s = self.sb("t1_s", [128, TC], F32); self.t1 = [t1s, t1s]
        x2s = self.sb("x2t_s", [128, TC], F32); self.x2t = [x2s, x2s]
        statrow = self.sb("statrow", [1, 4 * TC], F32)
        self.rmean = statrow[:, 0:TC]
        self.rvar = statrow[:, TC:2 * TC]
        self.rtmp = statrow[:, 2 * TC:3 * TC]
        self.rstd = statrow[:, 3 * TC:4 * TC]
        self.otm = [self.sb(f"otm{i}", [128, 512], F32) for i in range(1)]

        self.pbT = self.ps("pbT", [128, 512], F32)
        self.pstat = self.ps("pstat", [128, 512], F32)
        self.pbcA = self.ps("pbcA", [128, 512], F32)
        self.pbcB = self.ps("pbcB", [128, 512], F32)
        self.pmA = self.ps("pmA", [128, 512], F32)
        self.pmB = self.ps("pmB", [128, 512], F32)
        self.paux1 = self.ps("paux1", [128, 512], F32)
        self.paux2 = self.ps("paux2", [128, 512], F32)

        if "attn" in self.stages:
            self.dout("wq_bf", [128, 8192], BF16)
            self.dout("wk_bf", [128, 8192], BF16)
            self.dout("wv_bf", [128, 8192], BF16)
            self.dout("wo_bf", [64, 16384], BF16)
        if "ssm" in self.stages:
            self.dout("wd_bf", [128, 8192], BF16)
            self.dout("wg_bf", [128, 8192], BF16)
            self.dout("wob_bf", [128, 8192], BF16)

        rem = self.nc.sbuf_bytes_remaining - 256
        self.ar = Arena(self, rem)

        import os
        srcs = ["x"] + [f"xs{i}" for i in range(n_bounce)]
        dsts = [f"xs{i}" for i in range(n_bounce)] + ["y"]
        first_attn = True
        for si, stname in enumerate(self.stages):
            src, dst = srcs[si], dsts[si]
            if si > 0:
                self.sched.barrier()
            self.ar.reset()
            if stname.startswith("ffn"):
                self.alloc_ffn()
                self.load_ffn_weights()
            elif stname == "attn":
                self.alloc_attn()
                self.load_attn_weights(cast_hbm=first_attn)
                first_attn = False
            elif stname == "ssm":
                self.alloc_ssm()
                self.load_ssm_weights()
            for s in range(ns):
                if stname.startswith("ffn"):
                    self.emit_ffn_seq(s, src, src == "x", dst, dst == "y")
                elif stname == "attn":
                    self.emit_attn_seq(s, src, src == "x", dst, dst == "y")
                elif stname == "ssm":
                    self.emit_ssm_seq(s, src, src == "x", dst, dst == "y")
                else:
                    raise NotImplementedError(stname)

        self.sched.reorder(freeze=("sync",))
        sems = {}
        for en in ("sync", "tensor", "vector", "scalar", "gpsimd"):
            sems[en] = self.st.enter_context(nc.semaphore(f"sem_{en}"))
        block = self.st.enter_context(nc.Block())
        plan = self.sched

        @block.sync
        def _(sync):
            plan.emit_one("sync", sync, sems)

        @block.tensor
        def _(tensor):
            plan.emit_one("tensor", tensor, sems)

        @block.vector
        def _(vector):
            plan.emit_one("vector", vector, sems)

        @block.scalar
        def _(scalar):
            plan.emit_one("scalar", scalar, sems)

        @block.gpsimd
        def _(gpsimd):
            plan.emit_one("gpsimd", gpsimd, sems)

        self.st.close()
        return self.nc

    def emit_consts(self):
        S = self.sched
        idf, iot = self.identf, self.iotab

        S.step("gpsimd", lambda g: g.iota(iot[:], pattern=[[1, 128]], base=0, channel_multiplier=-1),
               writes=["iotab"], tag="iota")
        S.step("gpsimd", lambda g: g.tensor_scalar(out=idf[:], in0=iot[:], scalar1=0.0, scalar2=None, op0=ALU.is_equal),
               reads=["iotab"], writes=["identf"], tag="identf")


        oc, orow, orb, csm = self.onescol, self.onesrow, self.onesrow_b, self.csm

        def f_ones(v):
            v.memset(oc[:], 1.0)
            v.memset(orow[:], 1.0)
            v.memset(orb[:], 1.0)
            v.memset(csm[0:1, 0:1], LN_EPS)
            v.memset(csm[0:1, 1:2], RMS_EPS)
            v.memset(csm[0:1, 2:3], 1e-6)
            v.memset(csm[0:1, 3:4], 0.0)
            v.memset(csm[0:1, 4:5], 1e-11)
            v.memset(self.c6col[:], 1e-6)
            return v.memset(self.zcol[:], 0.0)
        S.step("vector", f_ones, writes=["onescol", "onesrow", "onesrow_b", "csm", "c6col", "zcol"], tag="ones")

    def alloc_ffn(self):
        self.xnb = [[self.asb(f"xnb{r}_{dc}", [128, TC], BF16) for dc in range(DC)] for r in range(1)]
        self.baseWTb = [self.asb(f"baseWTb{k}", [128, 1024], BF16) for k in range(DC)]
        self.sATb = self.asb("sATb", [128, RK * DC], BF16)
        self.sBTb = self.asb("sBTb", [RK, 1024], BF16)
        self.rATb = self.asb("rATb", [128, E * RK * DC], BF16)
        self.rBfb = self.asb("rBfb", [E * RK, 1024], BF16)
        self.rWTb = self.asb("rWTb", [128, E * DC], BF16)
        self.bbcol = self.asb("bbcol", [128, DC], F32)
        self.rbcol = self.asb("rbcol", [E, 1], F32)
        self.rep16 = self.asb("rep16", [E, E * RK], BF16)
        self.shb = self.asb("shb", [RK, TC], BF16)
        self.lrow = self.asb("lrow", [E, TC], F32)
        self.ltm = self.asb("ltm", [128, 32], F32)
        self.t2s = self.asb("t2s", [128, 32], F32)
        self.swtm = self.asb("swtm", [128, 32], F32)
        self.swfb = self.asb("swfb", [E, TC], BF16)
        self.swrb = self.asb("swrb", [E * RK, TC], F32)
        self.hwb = self.asb("hwb", [E * RK, TC], BF16)

    def load_ffn_weights(self):
        S, dram = self.sched, self.dram

        def ld(dst_ap, dstname, src_ap, rows, cols, scratch):
            wname = f"wl{scratch}"
            wl = self.wl[scratch]
            for c0 in range(0, cols, 512):
                cw = min(512, cols - c0)
                S.step("sync", lambda e, wl=wl, src=src_ap, rows=rows, c0=c0, cw=cw:
                       e.dma_start(out=wl[0:rows, 0:cw], in_=src[:, c0:c0 + cw]),
                       writes=[wname], tag=f"ld_{dstname}_{c0}")
                S.step("scalar", lambda e, wl=wl, dst=dst_ap, rows=rows, c0=c0, cw=cw:
                       e.copy(out=dst[:, c0:c0 + cw], in_=wl[0:rows, 0:cw]),
                       reads=[wname], writes=[dstname], tag=f"cast_{dstname}_{c0}")

        for k in range(DC):
            ld(self.baseWTb[k][:], f"baseWTb{k}", dram["baseWT"][k * 128:(k + 1) * 128, :], 128, 1024, 0)
        for k in range(DC):
            ld(self.sATb[:, k * RK:(k + 1) * RK], "sATb", dram["sAT"][k * 128:(k + 1) * 128, :], 128, RK, 0)
            ld(self.rATb[:, k * E * RK:(k + 1) * E * RK], "rATb", dram["rAT"][k * 128:(k + 1) * 128, :], 128, E * RK, 0)
            ld(self.rWTb[:, k * E:(k + 1) * E], "rWTb", dram["rWT"][k * 128:(k + 1) * 128, :], 128, E, 0)
        ld(self.sBTb[:], "sBTb", dram["sBT"], RK, 1024, 0)
        ld(self.rBfb[:], "rBfb", dram["rBf"], E * RK, 1024, 0)
        S.step("sync", lambda e: e.dma_start(out=self.bbcol[:], in_=dram["bb"]), writes=["bbcol"], tag="ld_bb")
        S.step("sync", lambda e: e.dma_start(out=self.rbcol[:], in_=dram["rb"]), writes=["rbcol"], tag="ld_rb")

        rep, iot = self.rep16, self.iotab

        S.step("gpsimd", lambda g: g.iota(iot[0:E, 0:E * RK].rearrange("p (e r) -> p e r", r=RK),
               pattern=[[1, E], [0, RK]], base=0, channel_multiplier=-1),
               writes=["iotab"], tag="rep16i")
        S.step("gpsimd", lambda g: g.tensor_scalar(out=rep[:], in0=iot[0:E, 0:E * RK], scalar1=0.0,
               scalar2=None, op0=ALU.is_equal),
               reads=["iotab"], writes=["rep16"], tag="rep16c")

    def alloc_attn(self):
        self.xnb = [[self.asb(f"xnb{r}_{dc}", [128, TC], BF16) for dc in range(DC)] for r in range(1)]
        self.wpb = self.asb("wpb", [128, 8192], BF16)
        self.wob = [self.asb(f"wob{i}", [64, 1024], BF16) for i in range(2)]
        biasrow = self.asb("biasrow", [1, 4 * D], BF16)
        self.bqrb = biasrow[:, 0:D]
        self.bkrb = biasrow[:, D:2 * D]
        self.bvrb = biasrow[:, 2 * D:3 * D]
        self.borb = biasrow[:, 3 * D:4 * D]
        self.nwcol = self.asb("nwcol", [128, DC], F32)
        self.kcache = [self.asb(f"kcache{j}", [128, T], BF16) for j in range(DC)]
        self.vcache = self.asb("vcache", [128, 16 * H * 65], BF16)
        self.qb = [self.asb(f"qb{j}", [128, TC], BF16) for j in range(DC)]
        self.xn2f = [self.asb(f"xn2f{dc}", [128, TC], F32) for dc in range(DC)]
        self.ptile = [self.asb(f"ptile{i}", [128, TC], BF16) for i in range(3)]
        self.osb = self.asb("osb", [64, 8 * TC], BF16)
        rowpool = self.asb("rowpool", [128, TC], F32)
        self.o64 = [rowpool]
        self.rcpr = [rowpool[0:1, :]]
        self.rcpq = [self.asb(f"rcpq{i}", [1, TC], F32) for i in range(2)]
        self.attnO = self.asb("attnO", [64, H * TC], BF16)
        self.n16a = self.asb("n16a", [16, TC], F32)
        self.n16b = self.asb("n16b", [16, TC], F32)
        self.knmax = self.asb("knmax", [16, 1], F32)
        self.kntmp = self.asb("kntmp", [16, 1], F32)
        self.bstar = self.asb("bstar", [16, 1], F32)
        self.nb8row = self.asb("nb8row", [1, 16], F32)
        self.nbB = self.asb("nbB", [128, 16], F32)

        self.tri01 = self.asb("tri01", [128, 128], BF16)
        self.bdsel = self.asb("bdsel", [128, 16 * DC], F32)

    def load_attn_weights(self, cast_hbm=True):
        S, dram = self.sched, self.dram

        def ld(dst_ap, dstname, src_ap, rows, cols, scratch):
            wname = f"wl{scratch}"
            wl = self.wl[scratch]
            for c0 in range(0, cols, 512):
                cw = min(512, cols - c0)
                S.step("sync", lambda e, wl=wl, src=src_ap, rows=rows, c0=c0, cw=cw:
                       e.dma_start(out=wl[0:rows, 0:cw], in_=src[:, c0:c0 + cw]),
                       writes=[wname], tag=f"ld_{dstname}_{c0}")
                S.step("scalar", lambda e, wl=wl, dst=dst_ap, rows=rows, c0=c0, cw=cw:
                       e.copy(out=dst[:, c0:c0 + cw], in_=wl[0:rows, 0:cw]),
                       reads=[wname], writes=[dstname], tag=f"cast_{dstname}_{c0}")

        if cast_hbm:
            for src, dstd in [("WqT", "wq_bf"), ("WkT", "wk_bf"), ("WvT", "wv_bf")]:
                for k in range(DC):
                    ld(self.wpb[:, k * 1024:(k + 1) * 1024], "wpb",
                       dram[src][k * 128:(k + 1) * 128, :], 128, 1024, 0)
                S.step("sync", lambda e, dstd=dstd: e.dma_start(out=dram[dstd], in_=self.wpb[:]),
                       reads=["wpb"], writes=[dstd], tag=f"st_{dstd}")
            for h in range(H):
                wob = self.wob[h % 2]
                ld(wob[:], f"wob{h % 2}", dram["WoT"][h * 64:(h + 1) * 64, :], 64, 1024, 0)
                S.step("sync", lambda e, h=h, wob=wob: e.dma_start(
                    out=dram["wo_bf"][:, h * 1024:(h + 1) * 1024], in_=wob[:]),
                    reads=[f"wob{h % 2}"], writes=["wo_bf"], tag=f"st_wo{h}")
        for nm, dst, dn in [("bq_row", self.bqrb, "bqrb"), ("bk_row", self.bkrb, "bkrb"),
                            ("bv_row", self.bvrb, "bvrb"), ("bo_row", self.borb, "borb")]:
            ld(dst[:], dn, dram[nm], 1, D, 0)
        S.step("sync", lambda e: e.dma_start(out=self.nwcol[:], in_=dram["norm_w"]),
               writes=["nwcol"], tag="ld_nw")
        iot = self.iotab
        S.step("gpsimd", lambda g: g.iota(iot[:], pattern=[[1, 128]], base=0, channel_multiplier=-1),
               writes=["iotab"], tag="tri_i")
        S.step("gpsimd", lambda g: g.tensor_scalar(out=self.tri01[:], in0=iot[:], scalar1=0.0,
               scalar2=None, op0=ALU.is_ge), reads=["iotab"], writes=["tri01"], tag="tri_c")
        S.step("vector", lambda v: v.memset(self.bdsel[:], 0.0), writes=["bdsel"], tag="bd0")
        for j in range(DC):
            S.step("vector", lambda v, j=j: v.memset(self.bdsel[0:64, j * 16 + 2 * j:j * 16 + 2 * j + 1], 1.0),
                   reads=["bdsel"], writes=["bdsel"], tag=f"bd1_{j}")
            S.step("vector", lambda v, j=j: v.memset(self.bdsel[64:128, j * 16 + 2 * j + 1:j * 16 + 2 * j + 2], 1.0),
                   reads=["bdsel"], writes=["bdsel"], tag=f"bd2_{j}")
        vc = self.vcache[:].rearrange("p (g h e) -> p g h e", g=16, h=H)
        S.step("vector", lambda v: v.memset(vc[:, :, :, 64:65], 1.0), writes=["vcache_ones"], tag="vc1")

    def alloc_ssm(self):
        TH = 1024
        self.ar.alloc([128, 4096], BF16)
        self.wpb = self.asb("wpb", [128, 8192], BF16)
        self.WbTb = self.asb("WbTb", [128, ST * DC], BF16)
        self.WcTb = self.asb("WcTb", [128, ST * DC], BF16)
        self.cwcol = self.asb("cwcol", [128, 4 * DC], F32)
        self.cbcol = self.asb("cbcol", [128, DC], F32)
        self.bdcol = self.asb("bdcol", [128, DC], F32)
        self.bgcol = self.asb("bgcol", [128, DC], F32)
        self.bobcol = self.asb("bobcol", [128, DC], F32)
        self.Acol = self.asb("Acol", [128, ST * DC], F32)
        self.xg = [self.asb(f"xg{dc}", [128, TH + 3], BF16) for dc in range(DC)]
        self.xcb = [self.asb(f"xcb{dc}", [128, TC], BF16) for dc in range(DC)]
        self.dlh = [self.asb(f"dlh{dc}", [128, TH], BF16) for dc in range(DC)]
        self.uh = [self.asb(f"uh{dc}", [128, TH], BF16) for dc in range(DC)]
        self.yh = [self.asb(f"yh{dc}", [128, TH], F32) for dc in range(DC)]
        self.ygb = [self.asb(f"ygb{dc}", [128, TC], BF16) for dc in range(DC)]
        self.Brow = self.asb("Brow", [ST, TH], BF16)
        self.Crow = self.asb("Crow", [ST, TH], BF16)
        self.Brep = [self.asb(f"Brep{i}", [128, TH], BF16) for i in range(2)]
        self.brow0 = self.asb("brow0", [1, TH], BF16)
        self.crow0 = self.asb("crow0", [1, TH], BF16)
        self.Crep = [self.asb(f"Crep{i}", [128, TH], BF16) for i in range(2)]
        self.decay = [self.asb(f"decay{i}", [128, TH], F32) for i in range(2)]
        w0 = self.asb("wsc0", [128, TH], BF16); self.wsc = [w0, w0]
        self.hsc = [self.asb(f"hsc{i}", [128, TH], BF16) for i in range(2)]
        self.hcsc = [self.asb(f"hcsc{i}", [128, TH], BF16) for i in range(2)]
        self.hcar = self.asb("hcar", [128, ST * DC], F32)
        self.cacc = [self.asb(f"cacc{i}", [128, TC], F32) for i in range(2)]
        self.sgt = self.asb("sgt", [128, TC], F32)

    def load_ssm_weights(self):
        S, dram = self.sched, self.dram

        def ld(dst_ap, dstname, src_ap, rows, cols, scratch=0):
            wname = f"wl{scratch}"
            wl = self.wl[scratch]
            for c0 in range(0, cols, 512):
                cw = min(512, cols - c0)
                S.step("sync", lambda e, wl=wl, src=src_ap, rows=rows, c0=c0, cw=cw:
                       e.dma_start(out=wl[0:rows, 0:cw], in_=src[:, c0:c0 + cw]),
                       writes=[wname], tag=f"sld_{dstname}_{c0}")
                S.step("scalar", lambda e, wl=wl, dst=dst_ap, rows=rows, c0=c0, cw=cw:
                       e.copy(out=dst[:, c0:c0 + cw], in_=wl[0:rows, 0:cw]),
                       reads=[wname], writes=[dstname], tag=f"scast_{dstname}_{c0}")

        for src, dstd in [("WdT", "wd_bf"), ("WgT", "wg_bf"), ("WobT", "wob_bf")]:
            for k in range(DC):
                ld(self.wpb[:, k * 1024:(k + 1) * 1024], "wpb",
                   dram[src][k * 128:(k + 1) * 128, :], 128, 1024)
            S.step("sync", lambda e, dstd=dstd: e.dma_start(out=dram[dstd], in_=self.wpb[:]),
                   reads=["wpb"], writes=[dstd], tag=f"st_{dstd}")
        for k in range(DC):
            ld(self.WbTb[:, k * ST:(k + 1) * ST], "WbTb", dram["WbT"][k * 128:(k + 1) * 128, :], 128, ST)
            ld(self.WcTb[:, k * ST:(k + 1) * ST], "WcTb", dram["WcT"][k * 128:(k + 1) * 128, :], 128, ST)
        for nm, dst, dn in [("convw", self.cwcol, "cwcol"), ("convb", self.cbcol, "cbcol"),
                            ("bd", self.bdcol, "bdcol"), ("bg", self.bgcol, "bgcol"),
                            ("bob", self.bobcol, "bobcol")]:
            if nm == "convw":
                for dc in range(DC):
                    S.step("sync", lambda e, dc=dc: e.dma_start(
                        out=self.cwcol[:, dc * 4:(dc + 1) * 4],
                        in_=self.dram["convw"][dc * 128:(dc + 1) * 128, :]),
                        writes=["cwcol"], tag=f"ld_cw{dc}")
            else:
                S.step("sync", lambda e, nm=nm, dst=dst: e.dma_start(out=dst[:], in_=self.dram[nm]),
                       writes=[dn], tag=f"ld_{nm}")
        for dc in range(DC):
            S.step("sync", lambda e, dc=dc: e.dma_start(
                out=self.Acol[:, dc * ST:(dc + 1) * ST],
                in_=self.dram["log_a"][dc * 128:(dc + 1) * 128, :]),
                writes=["Acol"], tag=f"ld_A{dc}")
        S.step("scalar", lambda e: e.activation(out=self.Acol[:], in_=self.Acol[:], func=AF.Exp,
               bias=self.zcol[:, 0:1], scale=1.0), reads=["Acol", "zcol"], writes=["Acol"], tag="A_exp")
        S.step("vector", lambda v: v.tensor_scalar(out=self.Acol[:], in0=self.Acol[:],
               scalar1=-1.0, scalar2=None, op0=ALU.mult), reads=["Acol"], writes=["Acol"], tag="A_neg")

    def emit_ssm_seq(self, s, src, src_tm, dst, dst_tm):
        S = self.sched
        TH = 1024
        for dc in range(DC):
            S.step("vector", lambda v, dc=dc: v.memset(self.xg[dc][:, 0:3], 0.0),
                   writes=[f"xg{dc}"], tag=f"zxg{s}_{dc}")
        S.step("vector", lambda v: v.memset(self.hcar[:], 0.0), writes=["hcar"], tag=f"zh{s}")
        for half in range(2):
            for twi in range(2):
                c = half * 2 + twi
                self.emit_ssm_prefix(s, c, half, twi, src, src_tm)
            self.emit_ssm_scan(s, half)
            for twi in range(2):
                c = half * 2 + twi
                self.emit_ssm_suffix(s, c, half, twi, src, dst, dst_tm)
            if half == 0:
                for dc in range(DC):
                    S.step("vector", lambda v, dc=dc: v.tensor_copy(
                        out=self.xg[dc][:, 0:3], in_=self.xg[dc][:, TH:TH + 3]),
                        reads=[f"xg{dc}"], writes=[f"xg{dc}"], tag=f"cxg{s}_{half}_{dc}")

    def emit_ssm_prefix(self, s, c, half, twi, src, src_tm):
        S = self.sched
        dram = self.dram
        TH = 1024
        rot = 0
        self.emit_chunk_in(s, c, src, src_tm, rot)
        xfm = self.xfm[rot]
        if src_tm:
            for dc in range(DC):
                S.step("sync", lambda e, s=s, c=c, dc=dc: e.dma_start(
                    out=dram["xres_sc"][s, dc, :, c * TC:(c + 1) * TC], in_=xfm[dc][:]),
                    reads=[f"xfm{rot}_{dc}"], writes=["xres_sc"], tag=f"xsc{s}_{c}_{dc}")
        self.emit_ln_stats(s, c, rot)
        for dc in range(DC):
            t1 = self.t1[0]
            S.step("vector", lambda v, dc=dc, t1=t1: v.tensor_tensor(
                out=t1[:], in0=xfm[dc][:], in1=self.pbcA[:, 0:TC], op=ALU.subtract),
                reads=[f"xfm{rot}_{dc}", "pbcA"], writes=["t1_s"], tag=f"sx1{s}_{c}_{dc}")
            S.step("vector", lambda v, dc=dc, t1=t1, twi=twi: v.tensor_tensor(
                out=self.xg[dc][:, 3 + twi * TC:3 + (twi + 1) * TC], in0=t1[:],
                in1=self.pbcB[:, 0:TC], op=ALU.mult),
                reads=["t1_s", "pbcB"], writes=[f"xg{dc}"], tag=f"sx2{s}_{c}_{dc}")
        for dc in range(DC):
            ca = self.cacc[dc % 2]
            can = f"cacc_{dc % 2}"
            base = twi * TC
            S.step("vector", lambda v, dc=dc, ca=ca, base=base: v.tensor_scalar(
                out=ca[:], in0=self.xg[dc][:, base:base + TC],
                scalar1=self.cwcol[:, dc * 4:dc * 4 + 1], scalar2=None, op0=ALU.mult),
                reads=[f"xg{dc}", "cwcol"], writes=[can], tag=f"cv0{s}_{c}_{dc}")
            for j in (1, 2, 3):
                S.step("vector", lambda v, dc=dc, ca=ca, base=base, j=j: v.scalar_tensor_tensor(
                    out=ca[:], in0=self.xg[dc][:, base + j:base + j + TC],
                    scalar=self.cwcol[:, dc * 4 + j:dc * 4 + j + 1], in1=ca[:],
                    op0=ALU.mult, op1=ALU.add),
                    reads=[f"xg{dc}", "cwcol", can], writes=[can], tag=f"cv{j}{s}_{c}_{dc}")
            S.step("scalar", lambda e, dc=dc, ca=ca: e.activation(
                out=self.sgt[:], in_=ca[:], func=AF.Sigmoid,
                bias=self.cbcol[:, dc:dc + 1], scale=1.0),
                reads=[can, "cbcol"], writes=["sgt"], tag=f"cvs{s}_{c}_{dc}")
            S.step("vector", lambda v, dc=dc, ca=ca: v.scalar_tensor_tensor(
                out=self.xcb[dc][:], in0=ca[:], scalar=self.cbcol[:, dc:dc + 1],
                in1=self.sgt[:], op0=ALU.add, op1=ALU.mult),
                reads=[can, "cbcol", "sgt"], writes=[f"xcb{dc}"], tag=f"cvx{s}_{c}_{dc}")
        xc_reads = [f"xcb{k}" for k in range(DC)]
        S.step("sync", lambda e: e.dma_start(out=self.wpb[:], in_=dram["wd_bf"]),
               reads=["wd_bf"], writes=["wpb"], tag=f"swd{s}_{c}")
        for oc in range(DC):
            pm, pmname = (self.pmA, "pmA") if oc % 2 == 0 else (self.pmB, "pmB")

            def f_d(e, oc=oc, pm=pm):
                last = None
                for k in range(DC):
                    last = e.matmul(out=pm[:, 0:TC],
                                    lhsT=self.wpb[:, k * 1024 + oc * 128:k * 1024 + (oc + 1) * 128],
                                    rhs=self.xcb[k][:], start=(k == 0), stop=(k == DC - 1))
                return last
            S.step("tensor", f_d, reads=xc_reads + ["wpb"], writes=[pmname], tag=f"dmm{s}_{c}_{oc}")
            S.step("scalar", lambda e, oc=oc, pm=pm: e.activation(
                out=self.sgt[:], in_=pm[:, 0:TC], func=AF.Exp,
                bias=self.bdcol[:, oc:oc + 1], scale=1.0),
                reads=[pmname, "bdcol"], writes=["sgt"], tag=f"dex{s}_{c}_{oc}")
            S.step("vector", lambda v: v.tensor_scalar(
                out=self.sgt[:], in0=self.sgt[:], scalar1=1.0, scalar2=None, op0=ALU.add),
                reads=["sgt"], writes=["sgt"], tag=f"dp1{s}_{c}_{oc}")
            S.step("scalar", lambda e, oc=oc, twi=twi: e.activation(
                out=self.dlh[oc][:, twi * TC:(twi + 1) * TC], in_=self.sgt[:], func=AF.Ln,
                bias=self.csm[0:1, 3:4].broadcast_to([128, 1]) if False else self.zcol[:, 0:1], scale=1.0),
                reads=["sgt", "zcol"], writes=[f"dlh{oc}"], tag=f"dln{s}_{c}_{oc}")
            S.step("vector", lambda v, oc=oc, twi=twi: v.tensor_tensor(
                out=self.uh[oc][:, twi * TC:(twi + 1) * TC],
                in0=self.dlh[oc][:, twi * TC:(twi + 1) * TC], in1=self.xcb[oc][:], op=ALU.mult),
                reads=[f"dlh{oc}", f"xcb{oc}"], writes=[f"uh{oc}"], tag=f"du{s}_{c}_{oc}")
        def f_b(e):
            for k in range(DC):
                e.matmul(out=self.pstat[0:ST, 0:TC], lhsT=self.WbTb[:, k * ST:(k + 1) * ST],
                         rhs=self.xcb[k][:], start=(k == 0), stop=(k == DC - 1))
            last = None
            for k in range(DC):
                last = e.matmul(out=self.pbcA[0:ST, 0:TC], lhsT=self.WcTb[:, k * ST:(k + 1) * ST],
                                rhs=self.xcb[k][:], start=(k == 0), stop=(k == DC - 1))
            return last
        S.step("tensor", f_b, reads=xc_reads + ["WbTb", "WcTb"], writes=["pstat", "pbcA"], tag=f"bc{s}_{c}")
        S.step("scalar", lambda e, twi=twi: e.copy(out=self.Brow[:, twi * TC:(twi + 1) * TC],
               in_=self.pstat[0:ST, 0:TC]), reads=["pstat"], writes=["Brow"], tag=f"bev{s}_{c}")
        S.step("scalar", lambda e, twi=twi: e.copy(out=self.Crow[:, twi * TC:(twi + 1) * TC],
               in_=self.pbcA[0:ST, 0:TC]), reads=["pbcA"], writes=["Crow"], tag=f"cev{s}_{c}")

    def emit_ssm_scan(self, s, half):
        S = self.sched
        TH = 1024
        for n in range(ST):
            br, brn = self.Brep[n % 2], f"Brep{n % 2}"
            cr, crn = self.Crep[n % 2], f"Crep{n % 2}"

            S.step("sync", lambda e, n=n: e.dma_start(out=self.brow0[:], in_=self.Brow[n:n + 1, :]),
                   reads=["Brow"], writes=["brow0"], tag=f"bdma{s}_{half}_{n}")
            S.step("sync", lambda e, n=n: e.dma_start(out=self.crow0[:], in_=self.Crow[n:n + 1, :]),
                   reads=["Crow"], writes=["crow0"], tag=f"cdma{s}_{half}_{n}")

            def f_brep(e, n=n):
                e.matmul(out=self.pmA[:, 0:TC], lhsT=self.onesrow_b[0:1, 0:128],
                         rhs=self.brow0[0:1, 0:TC], start=True, stop=True)
                e.matmul(out=self.pmB[:, 0:TC], lhsT=self.onesrow_b[0:1, 0:128],
                         rhs=self.brow0[0:1, TC:2 * TC], start=True, stop=True)
                e.matmul(out=self.paux1[:, 0:TC], lhsT=self.onesrow_b[0:1, 0:128],
                         rhs=self.crow0[0:1, 0:TC], start=True, stop=True)
                return e.matmul(out=self.paux2[:, 0:TC], lhsT=self.onesrow_b[0:1, 0:128],
                                rhs=self.crow0[0:1, TC:2 * TC], start=True, stop=True)
            S.step("tensor", f_brep, reads=["brow0", "crow0", "onesrow_b"],
                   writes=["pmA", "pmB", "paux1", "paux2"], tag=f"nrep{s}_{half}_{n}")
            S.step("scalar", lambda e, br=br: e.copy(out=br[:, 0:TC], in_=self.pmA[:, 0:TC]),
                   reads=["pmA"], writes=[brn], tag=f"bre1{s}_{half}_{n}")
            S.step("scalar", lambda e, br=br: e.copy(out=br[:, TC:2 * TC], in_=self.pmB[:, 0:TC]),
                   reads=["pmB"], writes=[brn], tag=f"bre2{s}_{half}_{n}")
            S.step("scalar", lambda e, cr=cr: e.copy(out=cr[:, 0:TC], in_=self.paux1[:, 0:TC]),
                   reads=["paux1"], writes=[crn], tag=f"cre1{s}_{half}_{n}")
            S.step("scalar", lambda e, cr=cr: e.copy(out=cr[:, TC:2 * TC], in_=self.paux2[:, 0:TC]),
                   reads=["paux2"], writes=[crn], tag=f"cre2{s}_{half}_{n}")
            for dc in range(DC):
                dec, decn = self.decay[dc % 2], f"decay{dc % 2}"
                ws, wsn = self.wsc[0], "wsc0"
                hs, hsn = self.hsc[dc % 2], f"hsc{dc % 2}"
                hc, hcn = self.hcsc[dc % 2], f"hcsc{dc % 2}"
                col = dc * ST + n
                S.step("scalar", lambda e, dc=dc, n=n, dec=dec: e.activation(
                    out=dec[:], in_=self.dlh[dc][:], func=AF.Exp,
                    bias=self.zcol[:, 0:1], scale=self.Acol[:, dc * ST + n:dc * ST + n + 1]),
                    reads=[f"dlh{dc}", "Acol", "zcol"], writes=[decn], tag=f"dk{s}_{half}_{n}_{dc}")
                S.step("vector", lambda v, dc=dc, br=br, ws=ws: v.tensor_tensor(
                    out=ws[:], in0=self.uh[dc][:], in1=br[:], op=ALU.mult),
                    reads=[f"uh{dc}", brn], writes=[wsn], tag=f"wm{s}_{half}_{n}_{dc}")
                S.step("vector", lambda v, dec=dec, ws=ws, hs=hs, col=col: v.tensor_tensor_scan(
                    out=hs[:], data0=dec[:], data1=ws[:], initial=self.hcar[:, col:col + 1],
                    op0=ALU.mult, op1=ALU.add),
                    reads=[decn, wsn, "hcar"], writes=[hsn], tag=f"sc{s}_{half}_{n}_{dc}")
                S.step("vector", lambda v, hs=hs, col=col: v.tensor_copy(
                    out=self.hcar[:, col:col + 1], in_=hs[:, TH - 1:TH]),
                    reads=[hsn], writes=["hcar"], tag=f"hc{s}_{half}_{n}_{dc}")
                S.step("gpsimd", lambda v, hs=hs, cr=cr, hc=hc: v.tensor_tensor(
                    out=hc[:], in0=hs[:], in1=cr[:], op=ALU.mult),
                    reads=[hsn, crn], writes=[hcn], tag=f"hcm{s}_{half}_{n}_{dc}")
                if n == 0:
                    S.step("gpsimd", lambda g, dc=dc, hc=hc: g.tensor_copy(
                        out=self.yh[dc][:], in_=hc[:]),
                        reads=[hcn], writes=[f"yh{dc}"], tag=f"ya{s}_{half}_{n}_{dc}")
                else:
                    S.step("gpsimd", lambda g, dc=dc, hc=hc: g.tensor_tensor(
                        out=self.yh[dc][:], in0=hc[:], in1=self.yh[dc][:], op=ALU.add),
                        reads=[hcn, f"yh{dc}"], writes=[f"yh{dc}"], tag=f"ya{s}_{half}_{n}_{dc}")

    def emit_ssm_suffix(self, s, c, half, twi, src, dst, dst_tm):
        S = self.sched
        dram = self.dram
        rot = 0
        xfm = self.xfm[rot]
        rsrc = "xres_sc" if src == "x" else src
        S.step("sync", lambda e: e.dma_start(out=self.wpb[:], in_=dram["wg_bf"]),
               reads=["wg_bf"], writes=["wpb"], tag=f"swg{s}_{c}")
        xg_reads = [f"xg{k}" for k in range(DC)]
        for oc in range(DC):
            pm, pmname = (self.pmA, "pmA") if oc % 2 == 0 else (self.pmB, "pmB")

            def f_g(e, oc=oc, pm=pm, twi=twi):
                last = None
                for k in range(DC):
                    last = e.matmul(out=pm[:, 0:TC],
                                    lhsT=self.wpb[:, k * 1024 + oc * 128:k * 1024 + (oc + 1) * 128],
                                    rhs=self.xg[k][:, 3 + twi * TC:3 + (twi + 1) * TC],
                                    start=(k == 0), stop=(k == DC - 1))
                return last
            S.step("tensor", f_g, reads=xg_reads + ["wpb"], writes=[pmname], tag=f"gmm{s}_{c}_{oc}")
            S.step("scalar", lambda e, oc=oc, pm=pm: e.activation(
                out=self.sgt[:], in_=pm[:, 0:TC], func=AF.Sigmoid,
                bias=self.bgcol[:, oc:oc + 1], scale=1.0),
                reads=[pmname, "bgcol"], writes=["sgt"], tag=f"gev{s}_{c}_{oc}")
            S.step("vector", lambda v, oc=oc, twi=twi: v.tensor_tensor(
                out=self.ygb[oc][:], in0=self.yh[oc][:, twi * TC:(twi + 1) * TC],
                in1=self.sgt[:], op=ALU.mult),
                reads=[f"yh{oc}", "sgt"], writes=[f"ygb{oc}"], tag=f"yg{s}_{c}_{oc}")
        for dc in range(DC):
            S.step("sync", lambda e, s=s, c=c, dc=dc, rsrc=rsrc: e.dma_start(
                out=xfm[dc][:], in_=dram[rsrc][s, dc, :, c * TC:(c + 1) * TC]),
                reads=[rsrc], writes=[f"xfm{rot}_{dc}"], tag=f"srl{s}_{c}_{dc}")
        yg_reads = [f"ygb{k}" for k in range(DC)]
        S.step("sync", lambda e: e.dma_start(out=self.wpb[:], in_=dram["wob_bf"]),
               reads=["wob_bf"], writes=["wpb"], tag=f"swo{s}_{c}")
        for oc in range(DC):
            pm, pmname = (self.pmA, "pmA") if oc % 2 == 0 else (self.pmB, "pmB")

            def f_o(e, oc=oc, pm=pm):
                last = None
                for k in range(DC):
                    last = e.matmul(out=pm[:, 0:TC],
                                    lhsT=self.wpb[:, k * 1024 + oc * 128:k * 1024 + (oc + 1) * 128],
                                    rhs=self.ygb[k][:], start=(k == 0), stop=(k == DC - 1))
                return last
            S.step("tensor", f_o, reads=yg_reads + ["wpb"], writes=[pmname], tag=f"omm{s}_{c}_{oc}")
            S.step("vector", lambda v, oc=oc, pm=pm: v.scalar_tensor_tensor(
                out=xfm[oc][:], in0=pm[:, 0:TC], scalar=self.bobcol[:, oc:oc + 1],
                in1=xfm[oc][:], op0=ALU.add, op1=ALU.add),
                reads=[pmname, "bobcol", f"xfm{rot}_{oc}"], writes=[f"xfm{rot}_{oc}"], tag=f"oev{s}_{c}_{oc}")
        self.emit_chunk_out(s, c, dst, dst_tm, rot)

    def emit_attn_seq(self, s, src, src_tm, dst, dst_tm):
        S = self.sched
        for c in range(NCH):
            rot = 0
            self.emit_chunk_in(s, c, src, src_tm, rot)
            self.emit_ln_stats(s, c, rot)
            self.emit_attn_xn2(s, c, rot)
            self.emit_attn_qkv(s, c, rot)
            self.emit_attn_bounds(s, c)
            self.emit_attn_core(s, c)
            self.emit_attn_out(s, c, rot)
            self.emit_chunk_out(s, c, dst, dst_tm, rot)

    def emit_attn_xn2(self, s, c, rot):
        S = self.sched
        S.step("scalar", lambda sc: sc.activation(out=self.rstd, in_=self.rvar, func=AF.Ln,
               bias=self.csm[0:1, 4:5], scale=1.0 + 1e-6),
               reads=["rvar", "csm"], writes=["rstd"], tag=f"a_ln{s}_{c}")
        S.step("scalar", lambda sc: sc.activation(out=self.rstd, in_=self.rstd, func=AF.Exp,
               bias=self.csm[0:1, 3:4], scale=-0.5),
               reads=["rstd", "csm"], writes=["rstd"], tag=f"a_ex{s}_{c}")
        def f_bc(e):
            e.matmul(out=self.pbcA[:, 0:TC], lhsT=self.onesrow[:], rhs=self.rmean, start=True, stop=True)
            return e.matmul(out=self.pbcB[:, 0:TC], lhsT=self.onesrow[:], rhs=self.rstd, start=True, stop=True)
        S.step("tensor", f_bc, reads=["rmean", "rstd", "onesrow"], writes=["pbcA", "pbcB"], tag=f"a_bc{s}_{c}")
        xfm, xnb = self.xfm[rot], self.xnb[rot]
        for dc in range(DC):
            t1 = self.t1[dc % 2]
            S.step("vector", lambda v, dc=dc, t1=t1: v.tensor_tensor(
                out=t1[:], in0=xfm[dc][:], in1=self.pbcA[:, 0:TC], op=ALU.subtract),
                reads=[f"xfm{rot}_{dc}", "pbcA"], writes=["t1_s"], tag=f"a_x1{s}_{c}_{dc}")
            S.step("vector", lambda v, dc=dc, t1=t1: v.scalar_tensor_tensor(
                out=self.xn2f[dc][:], in0=t1[:], scalar=self.nwcol[:, dc:dc + 1],
                in1=self.pbcB[:, 0:TC], op0=ALU.mult, op1=ALU.mult),
                reads=["t1_s", "nwcol", "pbcB"], writes=[f"xn2f{dc}"], tag=f"a_x2{s}_{c}_{dc}")
            S.step("scalar", lambda sc, dc=dc: sc.copy(out=xnb[dc][:], in_=self.xn2f[dc][:]),
                reads=[f"xn2f{dc}"], writes=[f"xnb{rot}_{dc}"], tag=f"a_x3{s}_{c}_{dc}")

    def emit_attn_qkv(self, s, c, rot):
        S = self.sched
        dram = self.dram
        xnb = self.xnb[rot]
        xn_reads = [f"xnb{rot}_{k}" for k in range(DC)]

        S.step("sync", lambda e: e.dma_start(out=self.wpb[:], in_=dram["wq_bf"]),
               reads=["wq_bf"], writes=["wpb"], tag=f"wq{s}_{c}")
        for oc in range(DC):
            pm, pmname = (self.pmA, "pmA") if oc % 2 == 0 else (self.pmB, "pmB")
            pm, pmname = (self.paux2, "paux2") if oc % 4 >= 2 else (pm, pmname)

            def f_q(e, oc=oc, pm=pm):
                for k in range(DC):
                    e.matmul(out=pm[:, 0:TC],
                             lhsT=self.wpb[:, k * 1024 + oc * 128:k * 1024 + (oc + 1) * 128],
                             rhs=xnb[k][:], start=(k == 0), stop=False)
                return e.matmul(out=pm[:, 0:TC], lhsT=self.bqrb[0:1, oc * 128:(oc + 1) * 128],
                                rhs=self.onesrow_b[:], start=False, stop=True)
            S.step("tensor", f_q, reads=xn_reads + ["wpb", "bqrb", "onesrow_b"], writes=[pmname],
                   tag=f"qmm{s}_{c}_{oc}")
            S.step("scalar", lambda e, oc=oc, pm=pm: e.copy(out=self.qb[oc][:], in_=pm[:, 0:TC]),
                   reads=[pmname], writes=[f"qb{oc}"], tag=f"qev{s}_{c}_{oc}")
        S.step("sync", lambda e: e.dma_start(out=self.wpb[:], in_=dram["wk_bf"]),
               reads=["wk_bf"], writes=["wpb"], tag=f"wk{s}_{c}")
        for oc in range(DC):
            pm, pmname = (self.pmA, "pmA") if oc % 2 == 0 else (self.pmB, "pmB")

            def f_k(e, oc=oc, pm=pm):
                for k in range(DC):
                    e.matmul(out=pm[:, 0:TC],
                             lhsT=self.wpb[:, k * 1024 + oc * 128:k * 1024 + (oc + 1) * 128],
                             rhs=xnb[k][:], start=(k == 0), stop=False)
                return e.matmul(out=pm[:, 0:TC], lhsT=self.bkrb[0:1, oc * 128:(oc + 1) * 128],
                                rhs=self.onesrow_b[:], start=False, stop=True)
            S.step("tensor", f_k, reads=xn_reads + ["wpb", "bkrb", "onesrow_b"], writes=[pmname],
                   tag=f"kmm{s}_{c}_{oc}")
            S.step("scalar", lambda e, oc=oc, pm=pm, c=c: e.copy(
                out=self.kcache[oc][:, c * TC:(c + 1) * TC], in_=pm[:, 0:TC]),
                reads=[pmname], writes=[f"kc{oc}"], tag=f"kev{s}_{c}_{oc}")
        S.step("sync", lambda e: e.dma_start(out=self.wpb[:], in_=dram["wv_bf"]),
               reads=["wv_bf"], writes=["wpb"], tag=f"wv{s}_{c}")
        vc = self.vcache
        for tsub in range(4):
            gts = c * 4 + tsub
            for nv in range(2):
                pm, pmname = (self.pmA, "pmA") if nv == 0 else (self.pmB, "pmB")

                def f_v(e, tsub=tsub, nv=nv, pm=pm):
                    for k in range(DC):
                        e.matmul(out=pm[:, 0:TC], lhsT=xnb[k][:, tsub * 128:(tsub + 1) * 128],
                                 rhs=self.wpb[:, k * 1024 + nv * 512:k * 1024 + (nv + 1) * 512],
                                 start=(k == 0), stop=False)
                    return e.matmul(out=pm[:, 0:TC], lhsT=self.onesrow_b[0:1, 0:128],
                                    rhs=self.bvrb[0:1, nv * 512:(nv + 1) * 512], start=False, stop=True)
                S.step("tensor", f_v, reads=xn_reads + ["wpb", "bvrb", "onesrow_b"], writes=[pmname],
                       tag=f"vmm{s}_{c}_{tsub}_{nv}")

                def f_vev(e, tsub=tsub, nv=nv, pm=pm, gts=gts):
                    out = vc[:].rearrange("p (g h e) -> p g h e", g=16, h=H)[
                        :, gts, nv * 8:(nv + 1) * 8, 0:64]
                    return e.copy(out=out, in_=pm[:, 0:TC].rearrange("p (h e) -> p h e", h=8))
                S.step("scalar", f_vev, reads=[pmname], writes=["vcache"], tag=f"vev{s}_{c}_{tsub}_{nv}")

    def emit_attn_bounds(self, s, c):
        S = self.sched
        for j in range(DC):
            x2 = self.x2t[j % 2]
            S.step("scalar", lambda e, j=j, x2=x2: e.square(out=x2[:], in_=self.qb[j][:]),
                   reads=[f"qb{j}"], writes=["x2t_s"], tag=f"bq2{s}_{c}_{j}")
            S.step("tensor", lambda e, j=j, x2=x2: e.matmul(
                out=self.pstat[0:16, 0:TC], lhsT=self.bdsel[:, j * 16:(j + 1) * 16], rhs=x2[:],
                start=(j == 0), stop=(j == DC - 1)),
                reads=["x2t_s", "bdsel"], writes=["pstat"], tag=f"bqm{s}_{c}_{j}")
        for j in range(DC):
            x2 = self.x2t[j % 2]
            S.step("scalar", lambda e, j=j, x2=x2, c=c: e.square(
                out=x2[:], in_=self.kcache[j][:, c * TC:(c + 1) * TC]),
                reads=[f"kc{j}"], writes=["x2t_s"], tag=f"bk2{s}_{c}_{j}")
            S.step("tensor", lambda e, j=j, x2=x2: e.matmul(
                out=self.pbcA[0:16, 0:TC], lhsT=self.bdsel[:, j * 16:(j + 1) * 16], rhs=x2[:],
                start=(j == 0), stop=(j == DC - 1)),
                reads=["x2t_s", "bdsel"], writes=["pbcA"], tag=f"bkm{s}_{c}_{j}")
        S.step("scalar", lambda e: e.activation(out=self.n16a[:], in_=self.pstat[0:16, 0:TC],
               func=AF.Ln, bias=self.c6col[0:16, 0:1], scale=1.0),
               reads=["pstat", "c6col"], writes=["n16a"], tag=f"bqn{s}_{c}")
        S.step("scalar", lambda e: e.activation(out=self.n16a[:], in_=self.n16a[:],
               func=AF.Exp, bias=self.zcol[0:16, 0:1], scale=0.5),
               reads=["n16a", "zcol"], writes=["n16a"], tag=f"bqe{s}_{c}")
        S.step("scalar", lambda e: e.activation(out=self.n16b[:], in_=self.pbcA[0:16, 0:TC],
               func=AF.Ln, bias=self.c6col[0:16, 0:1], scale=1.0),
               reads=["pbcA", "c6col"], writes=["n16b"], tag=f"bkn{s}_{c}")
        S.step("scalar", lambda e: e.activation(out=self.n16b[:], in_=self.n16b[:],
               func=AF.Exp, bias=self.zcol[0:16, 0:1], scale=0.5),
               reads=["n16b", "zcol"], writes=["n16b"], tag=f"bke{s}_{c}")
        S.step("vector", lambda v: v.tensor_reduce(out=self.kntmp[:], in_=self.n16b[:],
               axis=AX.X, op=ALU.max), reads=["n16b"], writes=["kntmp"], tag=f"bkx{s}_{c}")
        if c == 0:
            S.step("vector", lambda v: v.tensor_copy(out=self.knmax[:], in_=self.kntmp[:]),
                   reads=["kntmp"], writes=["knmax"], tag=f"bki{s}_{c}")
        else:
            S.step("vector", lambda v: v.tensor_tensor(out=self.knmax[:], in0=self.knmax[:],
                   in1=self.kntmp[:], op=ALU.max), reads=["kntmp", "knmax"], writes=["knmax"], tag=f"bki{s}_{c}")
        S.step("vector", lambda v: v.tensor_scalar(out=self.n16a[:], in0=self.n16a[:],
               scalar1=self.knmax[:, 0:1], scalar2=None, op0=ALU.mult),
               reads=["n16a", "knmax"], writes=["n16a"], tag=f"bb{s}_{c}")
        S.step("vector", lambda v: v.tensor_reduce(out=self.bstar[:], in_=self.n16a[:],
               axis=AX.X, op=ALU.max), reads=["n16a"], writes=["bstar"], tag=f"bs{s}_{c}")
        S.step("vector", lambda v: v.tensor_scalar(out=self.bstar[:], in0=self.bstar[:],
               scalar1=-0.125, scalar2=None, op0=ALU.mult), reads=["bstar"], writes=["bstar"], tag=f"bn{s}_{c}")
        S.step("tensor", lambda e: e.transpose(out=self.pbT[0:1, 0:16], in_=self.bstar[:],
               identity=self.identf[0:16, 0:16]), reads=["bstar", "identf"], writes=["pbT"], tag=f"bt{s}_{c}")
        S.step("vector", lambda v: v.tensor_copy(out=self.nb8row[:], in_=self.pbT[0:1, 0:16]),
               reads=["pbT"], writes=["nb8row"], tag=f"btv{s}_{c}")
        S.step("tensor", lambda e: e.matmul(out=self.pbcB[:, 0:16], lhsT=self.onesrow[:],
               rhs=self.nb8row[:], start=True, stop=True),
               reads=["nb8row", "onesrow"], writes=["pbcB"], tag=f"btb{s}_{c}")
        S.step("scalar", lambda e: e.copy(out=self.nbB[:], in_=self.pbcB[:, 0:16]),
               reads=["pbcB"], writes=["nbB"], tag=f"bte{s}_{c}")

    def emit_attn_core(self, s, c):
        for wave in range(2):
            self.emit_attn_wave(s, c, wave)
        for wave in range(2):
            self.emit_attn_wo_partial(s, c, wave)

    def emit_attn_wave(self, s, c, wave):
        S = self.sched
        for h in range(wave * 8, wave * 8 + 8):
            j, hp = h // 2, h % 2
            for kc in range(4 * c + 4):
                dd = kc - 4 * c
                q0 = max(dd, 0) * 128
                pt = self.ptile[kc % 3]
                ptname = f"ptile{kc % 3}"
                pm, pmname = [(self.pmA, "pmA"), (self.pmB, "pmB"), (self.paux2, "paux2")][kc % 3]

                def f_sc(e, j=j, hp=hp, kc=kc, q0=q0, pm=pm):
                    return e.matmul(out=pm[:, q0:TC],
                                    lhsT=self.kcache[j][hp * 64:(hp + 1) * 64, kc * 128:(kc + 1) * 128],
                                    rhs=self.qb[j][hp * 64:(hp + 1) * 64, q0:TC],
                                    start=True, stop=True)
                S.step("tensor", f_sc, reads=[f"kc{j}", f"qb{j}"], writes=[pmname], tag=f"sc{s}_{c}_{h}_{kc}")
                if q0 > 0:
                    S.step("vector", lambda v, pt=pt, q0=q0: v.memset(pt[:, 0:q0], 0.0),
                           writes=[ptname], tag=f"pz{s}_{c}_{h}_{kc}")
                S.step("scalar", lambda e, pt=pt, h=h, q0=q0, pm=pm: e.activation(
                    out=pt[:, q0:TC], in_=pm[:, q0:TC], func=AF.Exp,
                    bias=self.nbB[:, h:h + 1], scale=0.125),
                    reads=[pmname, "nbB"], writes=[ptname], tag=f"px{s}_{c}_{h}_{kc}")
                if dd >= 0:
                    S.step("vector", lambda v, pt=pt, q0=q0: v.tensor_tensor(
                        out=pt[:, q0:q0 + 128], in0=pt[:, q0:q0 + 128], in1=self.tri01[:], op=ALU.mult),
                        reads=[ptname, "tri01"], writes=[ptname], tag=f"pm{s}_{c}_{h}_{kc}")

                def f_av(e, h=h, kc=kc, pt=pt, c=c):
                    off = (kc * H + h) * 65
                    return e.matmul(out=self.paux1[0:65, 0:TC], lhsT=self.vcache[:, off:off + 65],
                                    rhs=pt[:, 0:TC], start=(kc == 0), stop=(kc == 4 * c + 3))
                S.step("tensor", f_av, reads=[ptname, "vcache", "vcache_ones"], writes=["paux1"],
                       tag=f"av{s}_{c}_{h}_{kc}")
            o64 = self.o64[0]
            S.step("scalar", lambda e, h=h: e.copy(out=self.osb[:, (h % 8) * TC:(h % 8 + 1) * TC],
                   in_=self.paux1[0:64, 0:TC]), reads=["paux1"], writes=["osb"], tag=f"oev{s}_{c}_{h}")
            S.step("scalar", lambda e, h=h, o64=o64: e.copy(out=o64[64:65, :],
                   in_=self.paux1[64:65, 0:TC]), reads=["paux1"], writes=["rowpool"], tag=f"oevs{s}_{c}_{h}")
            rs, rsn = (self.rcpr[0], "rowpoolA") if h % 2 == 0 else (self.n16a[0:1, :], "n16a")
            S.step("sync", lambda e, rs=rs, o64=o64: e.dma_start(out=rs, in_=o64[64:65, :]),
                   reads=["rowpool"], writes=[rsn], tag=f"sdma{s}_{c}_{h}")
            rr, rrn = self.rcpq[h % 2], f"rcpq{h % 2}"
            rbank, rbname = (self.pbcB, "pbcB") if h % 2 == 0 else (self.pbT, "pbT")
            S.step("vector", lambda v, rs=rs, rr=rr: v.reciprocal(out=rr[:], in_=rs),
                   reads=[rsn], writes=[rrn], tag=f"rcph{s}_{c}_{h}")
            S.step("tensor", lambda e, h=h, rr=rr, rbank=rbank: e.matmul(out=rbank[0:64, 0:TC],
                   lhsT=self.onesrow[0:1, 0:64], rhs=rr[:], start=True, stop=True),
                   reads=[rrn, "onesrow"], writes=[rbname], tag=f"rb{s}_{c}_{h}")
            S.step("vector", lambda v, h=h, rbank=rbank: v.tensor_tensor(out=self.attnO[:, h * TC:(h + 1) * TC],
                   in0=self.osb[0:64, (h % 8) * TC:(h % 8 + 1) * TC], in1=rbank[0:64, 0:TC], op=ALU.mult),
                   reads=["osb", rbname], writes=["attnO"], tag=f"onrm{s}_{c}_{h}")

    def emit_attn_wo_partial(self, s, c, wave):
        S = self.sched
        dram = self.dram
        banks = [self.pbT, self.pstat, self.pbcA, self.pbcB, self.pmA, self.pmB, self.paux1, self.paux2]
        bnames = ["pbT", "pstat", "pbcA", "pbcB", "pmA", "pmB", "paux1", "paux2"]
        for h in range(wave * 8, wave * 8 + 8):
            wob = self.wob[h % 2]
            S.step("sync", lambda e, h=h, wob=wob: e.dma_start(
                out=wob[:], in_=dram["wo_bf"][:, h * 1024:(h + 1) * 1024]),
                reads=["wo_bf"], writes=[f"wob{h % 2}"], tag=f"wol{s}_{c}_{h}")

            def f_wo(e, h=h, wob=wob):
                last = None
                for oc in range(DC):
                    last = e.matmul(out=banks[oc][:, 0:TC], lhsT=wob[:, oc * 128:(oc + 1) * 128],
                                    rhs=self.attnO[:, h * TC:(h + 1) * TC],
                                    start=(h == 0), stop=False)
                return last
            S.step("tensor", f_wo, reads=["attnO", f"wob{h % 2}"], writes=bnames,
                   tag=f"wo{s}_{c}_{h}")

    def emit_attn_out(self, s, c, rot):
        S = self.sched
        dram = self.dram
        xfm = self.xfm[rot]
        banks = [self.pbT, self.pstat, self.pbcA, self.pbcB, self.pmA, self.pmB, self.paux1, self.paux2]
        bnames = ["pbT", "pstat", "pbcA", "pbcB", "pmA", "pmB", "paux1", "paux2"]

        def f_aug(e):
            last = None
            for oc in range(DC):
                last = e.matmul(out=banks[oc][:, 0:TC], lhsT=self.borb[0:1, oc * 128:(oc + 1) * 128],
                                rhs=self.onesrow_b[:], start=False, stop=True)
            return last
        S.step("tensor", f_aug, reads=["borb", "onesrow_b"], writes=bnames, tag=f"woaug{s}_{c}")

        for oc in range(DC):
            t1 = self.t1[oc % 2]
            S.step("vector", lambda v, oc=oc, t1=t1: v.scalar_tensor_tensor(
                out=t1[:], in0=banks[oc][:, 0:TC], scalar=1.0, in1=self.xn2f[oc][:], op0=ALU.mult, op1=ALU.add),
                reads=[bnames[oc], f"xn2f{oc}"], writes=["t1_s"], tag=f"woe{s}_{c}_{oc}")
            S.step("vector", lambda v, oc=oc, t1=t1: v.tensor_tensor(
                out=xfm[oc][:], in0=t1[:], in1=xfm[oc][:], op=ALU.add),
                reads=["t1_s", f"xfm{rot}_{oc}"], writes=[f"xfm{rot}_{oc}"], tag=f"wor{s}_{c}_{oc}")

    def emit_chunk_in(self, s, c, src, src_tm, rot):
        S = self.sched
        xfm = self.xfm[rot]
        srcap = self.dram[src]
        if src_tm:
            for half in range(2):
                for i in range(2):
                    ts = half * 2 + i
                    xt = self.xtm[i]
                    S.step("sync", lambda e, xt=xt, s=s, c=c, ts=ts: e.dma_start(
                        out=xt[:], in_=srcap[s, c * TC + ts * 128: c * TC + (ts + 1) * 128, :]),
                        writes=[f"xtm{ts % 2}"], tag=f"in_tm{s}_{c}_{ts}")
                for dc in range(DC):
                    pb, pbname = (self.pbT, "pbT") if dc % 2 == 0 else (self.pstat, "pstat")
                    def f_tr(e, dc=dc, pb=pb):
                        e.transpose(out=pb[:, 0:128],
                                    in_=self.xtm[0][:, dc * 128:(dc + 1) * 128],
                                    identity=self.identf[:])
                        return e.transpose(out=pb[:, 128:256],
                                           in_=self.xtm[1][:, dc * 128:(dc + 1) * 128],
                                           identity=self.identf[:])
                    S.step("tensor", f_tr, reads=["xtm0", "xtm1", "identf"],
                           writes=[pbname], tag=f"in_tr{s}_{c}_{half}_{dc}")
                    S.step("scalar", lambda e, dc=dc, half=half, pb=pb: e.copy(
                        out=xfm[dc][:, half * 256:(half + 1) * 256], in_=pb[:, 0:256]),
                        reads=[pbname], writes=[f"xfm{rot}_{dc}"], tag=f"in_ev{s}_{c}_{half}_{dc}")
        else:
            for dc in range(DC):
                S.step("sync", lambda e, s=s, c=c, dc=dc: e.dma_start(
                    out=xfm[dc][:], in_=srcap[s, dc, :, c * TC:(c + 1) * TC]),
                    writes=[f"xfm{rot}_{dc}"], tag=f"in_fm{s}_{c}_{dc}")

    def emit_ln_stats(self, s, c, rot):
        S = self.sched
        xfm, csm = self.xfm[rot], self.csm
        for dc in range(DC):
            x2 = self.x2t[dc % 2]
            S.step("scalar", lambda e, dc=dc, x2=x2: e.square(out=x2[:], in_=xfm[dc][:]),
                   reads=[f"xfm{rot}_{dc}"], writes=["x2t_s"], tag=f"sq{s}_{c}_{dc}")

            def f_mm(e, dc=dc, x2=x2):
                e.matmul(out=self.pstat[0:1, 0:TC], lhsT=self.onescol[:], rhs=xfm[dc][:],
                         start=(dc == 0), stop=(dc == DC - 1))
                return e.matmul(out=self.pbcA[0:1, 0:TC], lhsT=self.onescol[:], rhs=x2[:],
                                start=(dc == 0), stop=(dc == DC - 1))
            S.step("tensor", f_mm, reads=[f"xfm{rot}_{dc}", "x2t_s", "onescol"],
                   writes=["pstat", "pbcA"], tag=f"stmm{s}_{c}_{dc}")

        S.step("vector", lambda v: v.tensor_scalar(out=self.rmean, in0=self.pstat[0:1, 0:TC],
               scalar1=1.0 / D, scalar2=None, op0=ALU.mult),
               reads=["pstat"], writes=["rmean"], tag=f"stm{s}_{c}")
        S.step("vector", lambda v: v.tensor_scalar(out=self.rvar, in0=self.pbcA[0:1, 0:TC],
               scalar1=1.0 / D, scalar2=None, op0=ALU.mult),
               reads=["pbcA"], writes=["rvar"], tag=f"ste{s}_{c}")
        S.step("vector", lambda v: v.tensor_tensor(out=self.rtmp, in0=self.rmean, in1=self.rmean, op=ALU.mult),
               reads=["rmean"], writes=["rtmp"], tag=f"stm2{s}_{c}")
        S.step("vector", lambda v: v.tensor_tensor(out=self.rvar, in0=self.rvar, in1=self.rtmp, op=ALU.subtract),
               reads=["rtmp", "rvar"], writes=["rvar"], tag=f"stv{s}_{c}")

        S.step("scalar", lambda sc: sc.activation(out=self.rstd, in_=self.rvar, func=AF.Ln,
               bias=csm[0:1, 0:1], scale=1.0),
               reads=["rvar", "csm"], writes=["rstd"], tag=f"strln{s}_{c}")
        S.step("scalar", lambda sc: sc.activation(out=self.rstd, in_=self.rstd, func=AF.Exp,
               bias=csm[0:1, 3:4], scale=-0.5),
               reads=["rstd", "csm"], writes=["rstd"], tag=f"strex{s}_{c}")

        def f_bc(e):
            e.matmul(out=self.pbcA[:, 0:TC], lhsT=self.onesrow[:], rhs=self.rmean, start=True, stop=True)
            return e.matmul(out=self.pbcB[:, 0:TC], lhsT=self.onesrow[:], rhs=self.rstd, start=True, stop=True)
        S.step("tensor", f_bc, reads=["rmean", "rstd", "onesrow"], writes=["pbcA", "pbcB"], tag=f"stbc{s}_{c}")

    def emit_xn(self, s, c, rot):
        S = self.sched
        xfm, xnb = self.xfm[rot], self.xnb[rot]
        for dc in range(DC):
            t1 = self.t1[dc % 2]

            def f1(v, dc=dc, t1=t1):
                return v.tensor_tensor(out=t1[:], in0=xfm[dc][:], in1=self.pbcA[:, 0:TC], op=ALU.subtract)
            S.step("vector", f1, reads=[f"xfm{rot}_{dc}", "pbcA"], writes=["t1_s"],
                   tag=f"xn1_{s}_{c}_{dc}")

            def f2(v, dc=dc, t1=t1):
                return v.tensor_tensor(out=xnb[dc][:], in0=t1[:], in1=self.pbcB[:, 0:TC], op=ALU.mult)
            S.step("vector", f2, reads=["t1_s", "pbcB"], writes=[f"xnb{rot}_{dc}"],
                   tag=f"xn2_{s}_{c}_{dc}")

    def emit_ffn_seq(self, s, src, src_tm, dst, dst_tm):
        for c in range(NCH):
            rot = 0
            self.emit_chunk_in(s, c, src, src_tm, rot)
            dbg = getattr(self, "dbg", False) and s == 0 and c == 0
            if dbg:
                self.dump("xfm0", self.xfm[rot][0][:], [f"xfm{rot}_0"], [128, TC])
            self.emit_ln_stats(s, c, rot)
            if dbg:
                self.dump("rmean", self.rmean, ["rmean"], [1, TC])
                self.dump("rstd", self.rstd, ["rstd"], [1, TC])
            self.emit_xn(s, c, rot)
            if dbg:
                self.dump("xnb0", self.xnb[rot][0][:], [f"xnb{rot}_0"], [128, TC], BF16)
            self.emit_ffn_chunk(s, c, rot)
            if dbg:
                self.dump("xout0", self.xfm[rot][0][:], [f"xfm{rot}_0"], [128, TC])
            self.emit_chunk_out(s, c, dst, dst_tm, rot)

    def emit_ffn_chunk(self, s, c, rot):
        S = self.sched
        xnb, xfm = self.xnb[rot], self.xfm[rot]
        xn_reads = [f"xnb{rot}_{k}" for k in range(DC)]

        def f_pre(e):
            for k in range(DC):
                e.matmul(out=self.paux1[0:RK, 0:TC], lhsT=self.sATb[:, k * RK:(k + 1) * RK],
                         rhs=xnb[k][:], start=(k == 0), stop=(k == DC - 1))
            for k in range(DC):
                e.matmul(out=self.pstat[0:E, 0:TC], lhsT=self.rWTb[:, k * E:(k + 1) * E],
                         rhs=xnb[k][:], start=(k == 0), stop=(k == DC - 1))
            last = None
            for k in range(DC):
                last = e.matmul(out=self.paux2[0:E * RK, 0:TC],
                                lhsT=self.rATb[:, k * E * RK:(k + 1) * E * RK],
                                rhs=xnb[k][:], start=(k == 0), stop=(k == DC - 1))
            return last
        S.step("tensor", f_pre, reads=xn_reads + ["sATb", "rWTb", "rATb"],
               writes=["paux1", "pstat", "paux2"], tag=f"ffnpre{s}_{c}")

        S.step("scalar", lambda e: e.copy(out=self.shb[:], in_=self.paux1[0:RK, 0:TC]),
               reads=["paux1"], writes=["shb"], tag=f"shev{s}_{c}")
        if getattr(self, "dbg", False) and s == 0 and c == 0:
            self.dump("shb", self.shb[:], ["shb"], [RK, TC], BF16)

        def f_lg(v):
            return v.tensor_scalar(out=self.lrow[:], in0=self.pstat[0:E, 0:TC],
                                   scalar1=self.rbcol[:, 0:1], scalar2=None, op0=ALU.add)
        S.step("vector", f_lg, reads=["pstat", "rbcol"], writes=["lrow"], tag=f"lg{s}_{c}")
        if getattr(self, "dbg", False) and s == 0 and c == 0:
            self.dump("lrow", self.lrow[:], ["lrow"], [E, TC])

        def f_ltr(e):
            last = None
            for ts in range(4):
                last = e.transpose(out=self.pbT[:, ts * 8: ts * 8 + 6],
                                   in_=self.lrow[:, ts * 128:(ts + 1) * 128],
                                   identity=self.identf[0:6, 0:6])
            return last
        S.step("tensor", f_ltr, reads=["lrow", "identf"], writes=["pbT"], tag=f"ltr{s}_{c}")

        ltm, t2s, swtm = self.ltm, self.t2s, self.swtm

        def f_t2cp(v):
            last = None
            for ts in range(4):
                last = v.tensor_copy(out=ltm[:, ts * 8:ts * 8 + 6], in_=self.pbT[:, ts * 8:ts * 8 + 6])
            return last
        S.step("vector", f_t2cp, reads=["pbT"], writes=["ltm"], tag=f"t2cp{s}_{c}")

        def f_t2mx(v):
            last = None
            for ts in range(4):
                last = v.tensor_reduce(out=t2s[:, ts * 8: ts * 8 + 1], in_=ltm[:, ts * 8:ts * 8 + 6],
                                       axis=AX.X, op=ALU.max)
            return last
        S.step("vector", f_t2mx, reads=["ltm"], writes=["t2s"], tag=f"t2mx{s}_{c}")

        def f_t2ng(v):
            last = None
            for ts in range(4):
                m6 = t2s[:, ts * 8: ts * 8 + 1]
                last = v.tensor_scalar(out=m6, in0=m6, scalar1=-1.0, scalar2=None, op0=ALU.mult)
            return last
        S.step("vector", f_t2ng, reads=["t2s"], writes=["t2s"], tag=f"t2ng{s}_{c}")
        if getattr(self, "dbg", False) and s == 0 and c == 0:
            self.dump("ltm_a", ltm[:], ["ltm"], [128, 32])
            self.dump("t2s_a", t2s[:], ["t2s"], [128, 32])

        def f_t2e(sc):
            last = None
            for ts in range(4):
                lt = ltm[:, ts * 8:ts * 8 + 6]
                last = sc.activation(out=lt, in_=lt, func=AF.Exp,
                                     bias=t2s[:, ts * 8: ts * 8 + 1], scale=1.0)
            return last
        S.step("scalar", f_t2e, reads=["ltm", "t2s"], writes=["ltm"], tag=f"t2e{s}_{c}")
        if getattr(self, "dbg", False) and s == 0 and c == 0:
            self.dump("ltm_e", ltm[:], ["ltm"], [128, 32])

        def batch4(name, fn, reads, writes):
            def f(v):
                last = None
                for ts in range(4):
                    last = fn(v, ts)
                return last
            S.step("vector", f, reads=reads, writes=writes, tag=f"{name}{s}_{c}")

        batch4("t2sum", lambda v, ts: v.tensor_reduce(
            out=t2s[:, ts * 8 + 2: ts * 8 + 3], in_=ltm[:, ts * 8:ts * 8 + 6], axis=AX.X, op=ALU.add),
            ["ltm"], ["t2s"])
        batch4("t2rcp", lambda v, ts: v.reciprocal(
            out=t2s[:, ts * 8 + 3: ts * 8 + 4], in_=t2s[:, ts * 8 + 2: ts * 8 + 3]),
            ["t2s"], ["t2s"])
        batch4("t2gt", lambda v, ts: v.tensor_scalar(
            out=ltm[:, ts * 8:ts * 8 + 6], in0=ltm[:, ts * 8:ts * 8 + 6],
            scalar1=t2s[:, ts * 8 + 3: ts * 8 + 4], scalar2=None, op0=ALU.mult),
            ["ltm", "t2s"], ["ltm"])
        batch4("t2m1", lambda v, ts: v.tensor_reduce(
            out=t2s[:, ts * 8 + 4: ts * 8 + 5], in_=ltm[:, ts * 8:ts * 8 + 6], axis=AX.X, op=ALU.max),
            ["ltm"], ["t2s"])
        batch4("t2eq", lambda v, ts: v.tensor_scalar(
            out=swtm[:, ts * 8:ts * 8 + 6], in0=ltm[:, ts * 8:ts * 8 + 6],
            scalar1=t2s[:, ts * 8 + 4: ts * 8 + 5], scalar2=None, op0=ALU.is_equal),
            ["ltm", "t2s"], ["swtm"])
        batch4("t2g2", lambda v, ts: v.scalar_tensor_tensor(
            out=swtm[:, ts * 8:ts * 8 + 6], in0=swtm[:, ts * 8:ts * 8 + 6], scalar=-2.0,
            in1=ltm[:, ts * 8:ts * 8 + 6], op0=ALU.mult, op1=ALU.add),
            ["swtm", "ltm"], ["swtm"])
        batch4("t2m2", lambda v, ts: v.tensor_reduce(
            out=t2s[:, ts * 8 + 5: ts * 8 + 6], in_=swtm[:, ts * 8:ts * 8 + 6], axis=AX.X, op=ALU.max),
            ["swtm"], ["t2s"])
        batch4("t2kp", lambda v, ts: v.tensor_scalar(
            out=swtm[:, ts * 8:ts * 8 + 6], in0=ltm[:, ts * 8:ts * 8 + 6],
            scalar1=t2s[:, ts * 8 + 5: ts * 8 + 6], scalar2=None, op0=ALU.is_ge),
            ["ltm", "t2s"], ["swtm"])
        batch4("t2dn", lambda v, ts: v.tensor_tensor(
            out=t2s[:, ts * 8 + 6: ts * 8 + 7], in0=t2s[:, ts * 8 + 4: ts * 8 + 5],
            in1=t2s[:, ts * 8 + 5: ts * 8 + 6], op=ALU.add),
            ["t2s"], ["t2s"])
        batch4("t2dn2", lambda v, ts: v.tensor_scalar(
            out=t2s[:, ts * 8 + 6: ts * 8 + 7], in0=t2s[:, ts * 8 + 6: ts * 8 + 7],
            scalar1=1e-6, scalar2=None, op0=ALU.add),
            ["t2s"], ["t2s"])
        batch4("t2rd", lambda v, ts: v.reciprocal(
            out=t2s[:, ts * 8 + 7: ts * 8 + 8], in_=t2s[:, ts * 8 + 6: ts * 8 + 7]),
            ["t2s"], ["t2s"])
        batch4("t2sw", lambda v, ts: v.tensor_tensor(
            out=swtm[:, ts * 8:ts * 8 + 6], in0=swtm[:, ts * 8:ts * 8 + 6],
            in1=ltm[:, ts * 8:ts * 8 + 6], op=ALU.mult),
            ["swtm", "ltm"], ["swtm"])
        batch4("t2sc", lambda v, ts: v.tensor_scalar(
            out=swtm[:, ts * 8:ts * 8 + 6], in0=swtm[:, ts * 8:ts * 8 + 6],
            scalar1=t2s[:, ts * 8 + 7: ts * 8 + 8], scalar2=None, op0=ALU.mult),
            ["swtm", "t2s"], ["swtm"])
        if getattr(self, "dbg", False) and s == 0 and c == 0:
            self.dump("swtm", self.swtm[:], ["swtm"], [128, 32])

        def f_str(e):
            last = None
            for ts in range(4):
                last = e.transpose(out=self.pbT[0:6, ts * 128:(ts + 1) * 128],
                                   in_=self.swtm[:, ts * 8:ts * 8 + 6],
                                   identity=self.identf[:])
            return last
        S.step("tensor", f_str, reads=["swtm", "identf"], writes=["pbT"], tag=f"swtr{s}_{c}")

        S.step("scalar", lambda e: e.copy(out=self.swfb[:], in_=self.pbT[0:6, 0:TC]),
               reads=["pbT"], writes=["swfb"], tag=f"swev{s}_{c}")

        def f_rep(e):
            return e.matmul(out=self.paux1[0:E * RK, 0:TC], lhsT=self.rep16[:], rhs=self.swfb[:],
                            start=True, stop=True)
        S.step("tensor", f_rep, reads=["swfb", "rep16"], writes=["paux1"], tag=f"swrep{s}_{c}")

        S.step("scalar", lambda e: e.copy(out=self.swrb[:], in_=self.paux1[0:E * RK, 0:TC]),
               reads=["paux1"], writes=["swrb"], tag=f"swrev{s}_{c}")

        def f_hw(v):
            return v.tensor_tensor(out=self.hwb[:], in0=self.paux2[0:E * RK, 0:TC],
                                   in1=self.swrb[:], op=ALU.mult)
        S.step("vector", f_hw, reads=["paux2", "swrb"], writes=["hwb"], tag=f"hw{s}_{c}")
        if getattr(self, "dbg", False) and s == 0 and c == 0:
            self.dump("hwb", self.hwb[:], ["hwb"], [E * RK, TC], BF16)
            self.dump("swrb", self.swrb[:], ["swrb"], [E * RK, TC])

        for oc in range(DC):
            pm = self.pmA if oc % 2 == 0 else self.pmB
            pmname = "pmA" if oc % 2 == 0 else "pmB"

            def f_main(e, oc=oc, pm=pm):
                for k in range(DC):
                    e.matmul(out=pm[:, 0:TC], lhsT=self.baseWTb[k][:, oc * 128:(oc + 1) * 128],
                             rhs=xnb[k][:], start=(k == 0), stop=False)
                e.matmul(out=pm[:, 0:TC], lhsT=self.sBTb[:, oc * 128:(oc + 1) * 128],
                         rhs=self.shb[:], start=False, stop=False)
                return e.matmul(out=pm[:, 0:TC], lhsT=self.rBfb[:, oc * 128:(oc + 1) * 128],
                                rhs=self.hwb[:], start=False, stop=True)
            S.step("tensor", f_main,
                   reads=xn_reads + ["shb", "hwb"] + [f"baseWTb{k}" for k in range(DC)] + ["sBTb", "rBfb"],
                   writes=[pmname], tag=f"main{s}_{c}_{oc}")

            def f_ev(v, oc=oc, pm=pm):
                return v.scalar_tensor_tensor(out=xfm[oc][:], in0=pm[:, 0:TC],
                                              scalar=self.bbcol[:, oc:oc + 1], in1=xfm[oc][:],
                                              op0=ALU.add, op1=ALU.add)
            S.step("vector", f_ev, reads=[pmname, "bbcol", f"xfm{rot}_{oc}"],
                   writes=[f"xfm{rot}_{oc}"], tag=f"mainev{s}_{c}_{oc}")

    def emit_chunk_out(self, s, c, dst, dst_tm, rot):
        S = self.sched
        xfm = self.xfm[rot]
        dstap = self.dram[dst]
        if dst_tm:
            for ts in range(4):
                for hf in range(2):
                    pm, pmname = (self.pmA, "pmA") if hf == 0 else (self.pmB, "pmB")

                    def f_tr(e, ts=ts, hf=hf, pm=pm):
                        last = None
                        for i in range(4):
                            dc = hf * 4 + i
                            last = e.transpose(out=pm[:, i * 128:(i + 1) * 128],
                                               in_=xfm[dc][:, ts * 128:(ts + 1) * 128],
                                               identity=self.identf[:])
                        return last
                    S.step("tensor", f_tr, reads=[f"xfm{rot}_{k}" for k in range(DC)] + ["identf"],
                           writes=[pmname], tag=f"otr{s}_{c}_{ts}_{hf}")
                    ot = self.otm[0]
                    S.step("scalar", lambda sc, ot=ot, pm=pm: sc.copy(out=ot[:], in_=pm[:, 0:512]),
                           reads=[pmname], writes=["otm0"], tag=f"oev{s}_{c}_{ts}_{hf}")
                    S.step("sync", lambda e, s=s, c=c, ts=ts, hf=hf, ot=ot: e.dma_start(
                        out=dstap[s, c * TC + ts * 128: c * TC + (ts + 1) * 128,
                                  hf * 512:(hf + 1) * 512], in_=ot[:]),
                        reads=["otm0"], tag=f"odma{s}_{c}_{ts}_{hf}")
        else:
            for dc in range(DC):
                S.step("sync", lambda e, s=s, c=c, dc=dc: e.dma_start(
                    out=dstap[s, dc, :, c * TC:(c + 1) * TC], in_=xfm[dc][:]),
                    reads=[f"xfm{rot}_{dc}"], tag=f"odma{s}_{c}_{dc}")


def run_dev(x, weights, stages, dbg=False):
    mk = MK(stages=stages)
    mk.dbg = dbg
    nc = mk.build()
    in_map = {"x": np.ascontiguousarray(x.astype(np.float32))}
    in_map.update(prep_weights(weights))
    res = run_bass_kernel_spmd(nc, [in_map], core_ids=[0])
    return res.results[0]


def np_layernorm(xx, eps=1e-5):
    m = xx.mean(-1, keepdims=True)
    v = xx.var(-1, keepdims=True)
    return (xx - m) / np.sqrt(v + eps)




def kernel(**inputs):
    x_full = np.ascontiguousarray(np.asarray(inputs["L0_attn_a_x"], np.float32))
    wprep = prep_weights(inputs)
    mk = MK(stages=("attn", "ffn1", "ssm", "ffn2"), nseq=NSEQ)
    nc = mk.build()
    n_cores = 8
    in_maps = []
    for c in range(n_cores):
        m = {"x": x_full[c * NSEQ:(c + 1) * NSEQ]}
        m.update(wprep)
        in_maps.append(m)
    res = run_bass_kernel_spmd(nc, in_maps, core_ids=list(range(n_cores)))
    out = np.concatenate([np.asarray(r["y"]) for r in res.results], axis=0)
    return out.astype(np.float32)


# version 29 (final)
# speedup vs baseline: 1.0898x; 1.0638x over previous
import numpy as np
import ml_dtypes
from contextlib import ExitStack
import concourse.bass as bass
import concourse.mybir as mybir
from concourse.bass_utils import run_bass_kernel_spmd

F32 = mybir.dt.float32
BF16 = mybir.dt.bfloat16
I32 = mybir.dt.int32
AF = mybir.ActivationFunctionType
ALU = mybir.AluOpType
AX = mybir.AxisListType

D, T, TC = 1024, 2048, 512
NCH = T // TC
NSEQ = 2
DC = D // 128
H, HD, ST = 16, 64, 16
E, RK = 6, 16
LN_EPS = 1e-5
RMS_EPS = 1e-6


class Step:
    __slots__ = ("eng", "fn", "deps", "ordinal", "tag", "idx")

    def __init__(self, eng, fn, deps, tag, idx):
        self.eng, self.fn, self.deps, self.tag, self.idx = eng, fn, deps, tag, idx
        self.ordinal = 0


class Sched:
    def __init__(self):
        self.steps = []
        self.eng_count = {}
        self.writer = {}
        self.readers = {}
        self.floor = []

    def step(self, eng, fn, reads=(), writes=(), tag=""):
        deps = set(self.floor)
        for b in reads:
            w = self.writer.get(b)
            if w is not None:
                deps.add(w)
        for b in writes:
            w = self.writer.get(b)
            if w is not None:
                deps.add(w)
            for r in self.readers.get(b, ()):
                deps.add(r)
        s = Step(eng, fn, deps, tag, len(self.steps))
        s.ordinal = self.eng_count.get(eng, 0) + 1
        self.eng_count[eng] = s.ordinal
        self.steps.append(s)
        for b in reads:
            self.readers.setdefault(b, []).append(s)
        for b in writes:
            self.writer[b] = s
            self.readers[b] = []
        return s

    def barrier(self):
        engs = ("sync", "tensor", "vector", "scalar", "gpsimd")
        lasts = {}
        for st in self.steps:
            lasts[st.eng] = st
        newfloor = []
        for en in engs:
            deps = set(v for k, v in lasts.items() if k != en)
            stp = Step(en, (lambda e: e.nop()), deps, "barrier", len(self.steps))
            stp.ordinal = self.eng_count.get(en, 0) + 1
            self.eng_count[en] = stp.ordinal
            self.steps.append(stp)
            newfloor.append(stp)
        self.floor = newfloor
        self.writer = {}
        self.readers = {}


    _COST = {
        "tensor": {"qmm": 2300, "kmm": 2300, "vmm": 2300, "dmm": 2300, "gmm": 2300, "omm": 2300,
                   "main": 2600, "ffnpre": 6000, "wo": 2000, "sc": 450, "av": 450,
                   "stmm": 700, "bqm": 700, "bkm": 700, "in_tr": 700, "otr": 1200,
                   "nrep": 1800, "bc": 4000, "stbc": 600, "a_bc": 600, "rb": 600,
                   "ltr": 400, "swtr": 400, "swrep": 500, "bt": 300, "btb": 300},
        "vector": {"sc": 2300, "wm": 2150, "hcm": 2150, "cv": 1200,
                   "xn1": 700, "xn2": 700, "a_x1": 700, "a_x2": 1200, "sx1": 700, "sx2": 700,
                   "woe": 1200, "wor": 700, "mainev": 1200, "onrm": 700, "yg": 700,
                   "dp1": 400, "du": 450, "hc": 150, "rcph": 1500, "rcp": 4500, "pz": 300, "pm": 300},
        "scalar": {"dk": 1250, "px": 650, "dex": 650, "dln": 650, "qev": 700, "kev": 700,
                   "vev": 800, "oev": 700, "oevs": 350, "in_ev": 600, "sq": 700,
                   "cvs": 650, "cvx": 650, "gev": 650, "a_x3": 650, "shev": 500,
                   "swev": 400, "swrev": 500, "bre": 550, "cre": 550, "cast": 900},
        "sync": {"sdma": 2500, "wq": 6000, "wk": 6000, "wv": 6000, "swd": 6000, "swg": 6000, "swo": 6000,
                 "in_tm": 3000, "in_fm": 1800, "odma": 1800, "srl": 1800, "xsc": 1800,
                 "sdma": 1500, "rdma": 1500, "bdma": 1500, "cdma": 1500, "wol": 1500,
                 "ld_": 2500, "sld": 2500, "st_": 4000},
        "gpsimd": {"ya": 2400},
    }
    _DEFC = {"tensor": 500, "vector": 800, "scalar": 700, "sync": 2000, "gpsimd": 1200}

    def _cost(self, st):
        table = self._COST.get(st.eng, {})
        for pref, c in table.items():
            if st.tag.startswith(pref):
                return c
        return self._DEFC.get(st.eng, 800)

    def reorder(self, only_segments=None, freeze=()):
        segs = []
        cur = []
        i = 0
        n = len(self.steps)
        while i < n:
            st = self.steps[i]
            if st.tag == "barrier":
                segs.append(cur)
                segs.append(self.steps[i:i + 5])
                cur = []
                i += 5
            else:
                cur.append(st)
                i += 1
        segs.append(cur)
        out = []
        segidx = -1
        for seg in segs:
            if not seg or seg[0].tag == "barrier":
                out.extend(seg)
                continue
            segidx += 1
            if only_segments is not None and segidx not in only_segments:
                out.extend(seg)
                continue
            segset = set(id(x) for x in seg)
            cost = {id(x): self._cost(x) for x in seg}
            succ = {id(x): [] for x in seg}
            indeg = {id(x): 0 for x in seg}
            extra = {}
            prev_by_eng = {}
            for x in seg:
                if x.eng in freeze:
                    p = prev_by_eng.get(x.eng)
                    if p is not None:
                        extra[id(x)] = p
                    prev_by_eng[x.eng] = x
            for x in seg:
                for d in x.deps:
                    if id(d) in segset:
                        succ[id(d)].append(x)
                        indeg[id(x)] += 1
                p = extra.get(id(x))
                if p is not None:
                    succ[id(p)].append(x)
                    indeg[id(x)] += 1
            cp = {}
            for x in reversed(seg):
                cp[id(x)] = cost[id(x)] + max((cp[id(t)] for t in succ[id(x)]), default=0)
            import heapq
            ready = []
            seqn = 0
            for x in seg:
                if indeg[id(x)] == 0:
                    heapq.heappush(ready, (-cp[id(x)], seqn, x))
                    seqn += 1
            eng_free = {}
            fin = {}
            order = []
            while ready:
                _, _, x = heapq.heappop(ready)
                dep_t = max((fin.get(id(d), 0) for d in x.deps), default=0)
                stt = max(eng_free.get(x.eng, 0), dep_t)
                fin[id(x)] = stt + cost[id(x)]
                eng_free[x.eng] = fin[id(x)]
                order.append(x)
                for t in succ[id(x)]:
                    indeg[id(t)] -= 1
                    if indeg[id(t)] == 0:
                        heapq.heappush(ready, (-cp[id(t)], seqn, t))
                        seqn += 1
            assert len(order) == len(seg), "scheduler lost steps"
            out.extend(order)
        assert len(out) == n
        self.steps = out
        cnt = {}
        for x in self.steps:
            cnt[x.eng] = cnt.get(x.eng, 0) + 1
            x.ordinal = cnt[x.eng]
        self.eng_count = cnt

    def emit_one(self, ename, eng, sems):
        unit = {"sync": 16}
        seen = {}
        for s in self.steps:
            if s.eng != ename:
                continue
            need = {}
            for d in s.deps:
                need[d.eng] = max(need.get(d.eng, 0), d.ordinal)
            for de, o in sorted(need.items()):
                v = o * unit.get(de, 1)
                if seen.get(de, 0) < v:
                    eng.wait_ge(sems[de], v)
                    seen[de] = v
            last = s.fn(eng)
            assert last is not None, f"step {s.tag} returned no instruction"
            last.then_inc(sems[ename], unit.get(ename, 1))
        if ename == "sync":
            eng.wait_ge(sems["sync"], self.eng_count.get("sync", 0) * 16)


def prep_weights(w):
    t = lambda a: np.ascontiguousarray(np.asarray(a).T)
    out = {
        "WqT": t(w["L0_attn_a_Wq"]), "WkT": t(w["L0_attn_a_Wk"]),
        "WvT": t(w["L0_attn_a_Wv"]), "WoT": t(w["L0_attn_a_Wo"]),
        "bq_row": np.asarray(w["L0_attn_a_bq"]).reshape(1, D).copy(),
        "bk_row": np.asarray(w["L0_attn_a_bk"]).reshape(1, D).copy(),
        "bv_row": np.asarray(w["L0_attn_a_bv"]).reshape(1, D).copy(),
        "bo_row": np.asarray(w["L0_attn_a_bo"]).reshape(1, D).copy(),
        "norm_w": np.asarray(w["L0_attn_a_norm_w"]).reshape(DC, 128).T.copy(),
        "baseWT": t(w["L0_ffn_base_W"]),
        "bb": np.asarray(w["L0_ffn_base_b"]).reshape(DC, 128).T.copy(),
        "sAT": t(w["L0_ffn_shared_A"]),
        "sBT": t(w["L0_ffn_shared_B"]),
        "rAT": t(np.asarray(w["L0_ffn_routed_A"]).reshape(E * RK, D)),
        "rBf": np.ascontiguousarray(
            np.asarray(w["L0_ffn_routed_B"]).transpose(0, 2, 1).reshape(E * RK, D)),
        "rWT": t(w["L0_ffn_router_W"]),
        "rb": np.asarray(w["L0_ffn_router_b"]).reshape(E, 1).copy(),
        "convw": np.asarray(w["L0_attn_b_conv_w"]).reshape(D, 4).copy(),
        "convb": np.asarray(w["L0_attn_b_conv_b"]).reshape(DC, 128).T.copy(),
        "WdT": t(w["L0_attn_b_Wd"]),
        "bd": np.asarray(w["L0_attn_b_bd"]).reshape(DC, 128).T.copy(),
        "WbT": t(w["L0_attn_b_Wb"]),
        "WcT": t(w["L0_attn_b_Wc"]),
        "log_a": np.asarray(w["L0_attn_b_log_a"]).copy(),
        "WgT": t(w["L0_attn_b_Wg"]),
        "bg": np.asarray(w["L0_attn_b_bg"]).reshape(DC, 128).T.copy(),
        "WobT": t(w["L0_attn_b_Wo"]),
        "bob": np.asarray(w["L0_attn_b_bo"]).reshape(DC, 128).T.copy(),
    }
    return {k: np.ascontiguousarray(v.astype(np.float32)) for k, v in out.items()}


WSHAPES = {
    "WqT": [D, D], "WkT": [D, D], "WvT": [D, D], "WoT": [D, D],
    "bq_row": [1, D], "bk_row": [1, D], "bv_row": [1, D], "bo_row": [1, D],
    "norm_w": [128, DC],
    "baseWT": [D, D], "bb": [128, DC], "sAT": [D, RK], "sBT": [RK, D],
    "rAT": [D, E * RK], "rBf": [E * RK, D], "rWT": [D, E], "rb": [E, 1],
    "convw": [D, 4], "convb": [128, DC], "WdT": [D, D], "bd": [128, DC],
    "WbT": [D, ST], "WcT": [D, ST], "log_a": [D, ST], "WgT": [D, D],
    "bg": [128, DC], "WobT": [D, D], "bob": [128, DC],
}


class Arena:
    def __init__(self, mk, bytes_per_part):
        self.cols4 = bytes_per_part // 4
        self.t = mk.sb("arena", [128, self.cols4], F32)
        self.off4 = 0

    def reset(self):
        self.off4 = 0

    def alloc(self, shape, dt=F32):
        rows, cols = shape
        esz = 2 if dt == BF16 else 4
        n4 = (cols * esz + 63) // 64 * 16
        assert self.off4 + n4 <= self.cols4, f"arena overflow at {shape} {dt}: {self.off4*4+n4*4} > {self.cols4*4}"
        self.last_start = self.off4
        base = self.t[0:rows, self.off4:self.off4 + n4]
        self.off4 += n4
        if dt == F32:
            return base[:, 0:cols]
        return base.bitcast(dt)[:, 0:cols]


class MK:
    def __init__(self, stages=("ffn1",), nseq=NSEQ):
        self.stages = stages
        self.nseq = nseq
        self.nc = bass.Bass("TRN2", target_bir_lowering=False)
        self.sched = Sched()
        self.st = ExitStack()
        self.dram = {}

    def sb(self, name, shape, dt=F32):
        return self.st.enter_context(self.nc.sbuf_tensor(name, shape, dt))

    def asb(self, name, shape, dt=F32):
        ap = self.ar.alloc(shape, dt)
        key = (self.ar.last_start, tuple(shape), str(dt))
        if not hasattr(self, "_stable"):
            self._stable = {}
        prev = self._stable.get(name)
        assert prev is None or prev == key, f"arena attr {name} moved: {prev} -> {key}"
        self._stable[name] = key
        return ap

    def ps(self, name, shape, dt=F32):
        return self.st.enter_context(self.nc.psum_tensor(name, shape, dt))

    def din(self, name, shape, dt=F32):
        a = self.nc.dram_tensor(name, shape, dt, kind="ExternalInput").ap()
        self.dram[name] = a
        return a

    def dout(self, name, shape, dt=F32):
        a = self.nc.dram_tensor(name, shape, dt, kind="ExternalOutput").ap()
        self.dram[name] = a
        return a

    def dump(self, name, ap, readbufs, shape, dt=F32):
        d = self.dout("dbg_" + name, shape, dt)
        self.sched.step("sync", lambda e, ap=ap, d=d: e.dma_start(out=d, in_=ap),
                        reads=list(readbufs), tag="dump_" + name)

    def dint(self, name, shape, dt=F32):
        a = self.nc.dram_tensor(name, shape, dt, kind="Internal").ap()
        self.dram[name] = a
        return a

    def build(self):
        nc, S, ns = self.nc, self.sched, self.nseq
        self.din("x", [ns, T, D])
        self.dout("y", [ns, T, D])
        for n, sh in WSHAPES.items():
            self.din(n, sh)
        import os as _os
        n_bounce = len(self.stages) - 1
        for i in range(n_bounce):
            self.dout(f"xs{i}", [ns, DC, 128, T])

        if "ssm" in self.stages and self.stages[0] == "ssm":
            self.dout("xres_sc", [ns, DC, 128, T])

        self.identf = self.sb("identf", [128, 128], F32)
        self.iotab = self.sb("iotab", [128, 128], I32)
        self.onescol = self.sb("onescol", [128, 1], F32)
        self.onesrow = self.sb("onesrow", [1, 128], F32)
        self.onesrow_b = self.sb("onesrow_b", [1, TC], BF16)
        self.csm = self.sb("csm", [1, 8], F32)
        self.c6col = self.sb("c6col", [128, 1], F32)
        self.zcol = self.sb("zcol", [128, 1], F32)
        self.wl = [self.sb(f"wl{i}", [128, 512], F32) for i in range(1)]
        self.emit_consts()

        need_ssm = "ssm" in self.stages
        self.xtm = [self.sb(f"xtm{i}", [128, 1024], F32) for i in range(2)]
        self.xfm = [[self.sb(f"xfm{r}_{dc}", [128, TC], F32) for dc in range(DC)] for r in range(1)]
        t1s = self.sb("t1_s", [128, TC], F32); self.t1 = [t1s, t1s]
        x2s = self.sb("x2t_s", [128, TC], F32); self.x2t = [x2s, x2s]
        statrow = self.sb("statrow", [1, 4 * TC], F32)
        self.rmean = statrow[:, 0:TC]
        self.rvar = statrow[:, TC:2 * TC]
        self.rtmp = statrow[:, 2 * TC:3 * TC]
        self.rstd = statrow[:, 3 * TC:4 * TC]
        self.otm = [self.sb(f"otm{i}", [128, 512], F32) for i in range(1)]

        self.pbT = self.ps("pbT", [128, 512], F32)
        self.pstat = self.ps("pstat", [128, 512], F32)
        self.pbcA = self.ps("pbcA", [128, 512], F32)
        self.pbcB = self.ps("pbcB", [128, 512], F32)
        self.pmA = self.ps("pmA", [128, 512], F32)
        self.pmB = self.ps("pmB", [128, 512], F32)
        self.paux1 = self.ps("paux1", [128, 512], F32)
        self.paux2 = self.ps("paux2", [128, 512], F32)

        if "attn" in self.stages:
            self.dout("wq_bf", [128, 8192], BF16)
            self.dout("wk_bf", [128, 8192], BF16)
            self.dout("wv_bf", [128, 8192], BF16)
            self.dout("wo_bf", [64, 16384], BF16)
        if "ssm" in self.stages:
            self.dout("wd_bf", [128, 8192], BF16)
            self.dout("wg_bf", [128, 8192], BF16)
            self.dout("wob_bf", [128, 8192], BF16)

        rem = self.nc.sbuf_bytes_remaining - 256
        self.ar = Arena(self, rem)

        import os
        srcs = ["x"] + [f"xs{i}" for i in range(n_bounce)]
        dsts = [f"xs{i}" for i in range(n_bounce)] + ["y"]
        first_attn = True
        for si, stname in enumerate(self.stages):
            src, dst = srcs[si], dsts[si]
            if si > 0:
                self.sched.barrier()
            self.ar.reset()
            if stname.startswith("ffn"):
                self.alloc_ffn()
                self.load_ffn_weights()
            elif stname == "attn":
                self.alloc_attn()
                self.load_attn_weights(cast_hbm=first_attn)
                first_attn = False
            elif stname == "ssm":
                self.alloc_ssm()
                self.load_ssm_weights()
            for s in range(ns):
                if stname.startswith("ffn"):
                    self.emit_ffn_seq(s, src, src == "x", dst, dst == "y")
                elif stname == "attn":
                    self.emit_attn_seq(s, src, src == "x", dst, dst == "y")
                elif stname == "ssm":
                    self.emit_ssm_seq(s, src, src == "x", dst, dst == "y")
                else:
                    raise NotImplementedError(stname)

        self.sched.reorder(freeze=("sync",))
        sems = {}
        for en in ("sync", "tensor", "vector", "scalar", "gpsimd"):
            sems[en] = self.st.enter_context(nc.semaphore(f"sem_{en}"))
        block = self.st.enter_context(nc.Block())
        plan = self.sched

        @block.sync
        def _(sync):
            plan.emit_one("sync", sync, sems)

        @block.tensor
        def _(tensor):
            plan.emit_one("tensor", tensor, sems)

        @block.vector
        def _(vector):
            plan.emit_one("vector", vector, sems)

        @block.scalar
        def _(scalar):
            plan.emit_one("scalar", scalar, sems)

        @block.gpsimd
        def _(gpsimd):
            plan.emit_one("gpsimd", gpsimd, sems)

        self.st.close()
        return self.nc

    def emit_consts(self):
        S = self.sched
        idf, iot = self.identf, self.iotab

        S.step("gpsimd", lambda g: g.iota(iot[:], pattern=[[1, 128]], base=0, channel_multiplier=-1),
               writes=["iotab"], tag="iota")
        S.step("gpsimd", lambda g: g.tensor_scalar(out=idf[:], in0=iot[:], scalar1=0.0, scalar2=None, op0=ALU.is_equal),
               reads=["iotab"], writes=["identf"], tag="identf")


        oc, orow, orb, csm = self.onescol, self.onesrow, self.onesrow_b, self.csm

        def f_ones(v):
            v.memset(oc[:], 1.0)
            v.memset(orow[:], 1.0)
            v.memset(orb[:], 1.0)
            v.memset(csm[0:1, 0:1], LN_EPS)
            v.memset(csm[0:1, 1:2], RMS_EPS)
            v.memset(csm[0:1, 2:3], 1e-6)
            v.memset(csm[0:1, 3:4], 0.0)
            v.memset(csm[0:1, 4:5], 1e-11)
            v.memset(self.c6col[:], 1e-6)
            return v.memset(self.zcol[:], 0.0)
        S.step("vector", f_ones, writes=["onescol", "onesrow", "onesrow_b", "csm", "c6col", "zcol"], tag="ones")

    def alloc_ffn(self):
        self.xnb = [[self.asb(f"xnb{r}_{dc}", [128, TC], BF16) for dc in range(DC)] for r in range(1)]
        self.baseWTb = [self.asb(f"baseWTb{k}", [128, 1024], BF16) for k in range(DC)]
        self.sATb = self.asb("sATb", [128, RK * DC], BF16)
        self.sBTb = self.asb("sBTb", [RK, 1024], BF16)
        self.rATb = self.asb("rATb", [128, E * RK * DC], BF16)
        self.rBfb = self.asb("rBfb", [E * RK, 1024], BF16)
        self.rWTb = self.asb("rWTb", [128, E * DC], BF16)
        self.bbcol = self.asb("bbcol", [128, DC], F32)
        self.rbcol = self.asb("rbcol", [E, 1], F32)
        self.rep16 = self.asb("rep16", [E, E * RK], BF16)
        self.shb = self.asb("shb", [RK, TC], BF16)
        self.lrow = self.asb("lrow", [E, TC], F32)
        self.ltm = self.asb("ltm", [128, 32], F32)
        self.t2s = self.asb("t2s", [128, 32], F32)
        self.swtm = self.asb("swtm", [128, 32], F32)
        self.swfb = self.asb("swfb", [E, TC], BF16)
        self.swrb = self.asb("swrb", [E * RK, TC], F32)
        self.hwb = self.asb("hwb", [E * RK, TC], BF16)

    def load_ffn_weights(self):
        S, dram = self.sched, self.dram

        def ld(dst_ap, dstname, src_ap, rows, cols, scratch):
            wname = f"wl{scratch}"
            wl = self.wl[scratch]
            for c0 in range(0, cols, 512):
                cw = min(512, cols - c0)
                S.step("sync", lambda e, wl=wl, src=src_ap, rows=rows, c0=c0, cw=cw:
                       e.dma_start(out=wl[0:rows, 0:cw], in_=src[:, c0:c0 + cw]),
                       writes=[wname], tag=f"ld_{dstname}_{c0}")
                S.step("scalar", lambda e, wl=wl, dst=dst_ap, rows=rows, c0=c0, cw=cw:
                       e.copy(out=dst[:, c0:c0 + cw], in_=wl[0:rows, 0:cw]),
                       reads=[wname], writes=[dstname], tag=f"cast_{dstname}_{c0}")

        for k in range(DC):
            ld(self.baseWTb[k][:], f"baseWTb{k}", dram["baseWT"][k * 128:(k + 1) * 128, :], 128, 1024, 0)
        for k in range(DC):
            ld(self.sATb[:, k * RK:(k + 1) * RK], "sATb", dram["sAT"][k * 128:(k + 1) * 128, :], 128, RK, 0)
            ld(self.rATb[:, k * E * RK:(k + 1) * E * RK], "rATb", dram["rAT"][k * 128:(k + 1) * 128, :], 128, E * RK, 0)
            ld(self.rWTb[:, k * E:(k + 1) * E], "rWTb", dram["rWT"][k * 128:(k + 1) * 128, :], 128, E, 0)
        ld(self.sBTb[:], "sBTb", dram["sBT"], RK, 1024, 0)
        ld(self.rBfb[:], "rBfb", dram["rBf"], E * RK, 1024, 0)
        S.step("sync", lambda e: e.dma_start(out=self.bbcol[:], in_=dram["bb"]), writes=["bbcol"], tag="ld_bb")
        S.step("sync", lambda e: e.dma_start(out=self.rbcol[:], in_=dram["rb"]), writes=["rbcol"], tag="ld_rb")

        rep, iot = self.rep16, self.iotab

        S.step("gpsimd", lambda g: g.iota(iot[0:E, 0:E * RK].rearrange("p (e r) -> p e r", r=RK),
               pattern=[[1, E], [0, RK]], base=0, channel_multiplier=-1),
               writes=["iotab"], tag="rep16i")
        S.step("gpsimd", lambda g: g.tensor_scalar(out=rep[:], in0=iot[0:E, 0:E * RK], scalar1=0.0,
               scalar2=None, op0=ALU.is_equal),
               reads=["iotab"], writes=["rep16"], tag="rep16c")

    def alloc_attn(self):
        self.xnb = [[self.asb(f"xnb{r}_{dc}", [128, TC], BF16) for dc in range(DC)] for r in range(1)]
        self.wpb = self.asb("wpb", [128, 8192], BF16)
        self.wob = [self.asb(f"wob{i}", [64, 1024], BF16) for i in range(2)]
        biasrow = self.asb("biasrow", [1, 4 * D], BF16)
        self.bqrb = biasrow[:, 0:D]
        self.bkrb = biasrow[:, D:2 * D]
        self.bvrb = biasrow[:, 2 * D:3 * D]
        self.borb = biasrow[:, 3 * D:4 * D]
        self.nwcol = self.asb("nwcol", [128, DC], F32)
        self.kcache = [self.asb(f"kcache{j}", [128, T], BF16) for j in range(DC)]
        self.vcache = self.asb("vcache", [128, 16 * H * 65], BF16)
        self.qb = [self.asb(f"qb{j}", [128, TC], BF16) for j in range(DC)]
        self.xn2f = [self.asb(f"xn2f{dc}", [128, TC], F32) for dc in range(DC)]
        self.ptile = [self.asb(f"ptile{i}", [128, TC], BF16) for i in range(3)]
        self.osb = self.asb("osb", [64, 8 * TC], BF16)
        rowpool = self.asb("rowpool", [128, TC], F32)
        self.o64 = [rowpool]
        self.rcpr = [rowpool[0:1, :]]
        self.rcpq = [self.asb(f"rcpq{i}", [1, TC], F32) for i in range(2)]
        self.attnO = self.asb("attnO", [64, H * TC], BF16)
        self.n16a = self.asb("n16a", [16, TC], F32)
        self.n16b = self.asb("n16b", [16, TC], F32)
        self.knmax = self.asb("knmax", [16, 1], F32)
        self.kntmp = self.asb("kntmp", [16, 1], F32)
        self.bstar = self.asb("bstar", [16, 1], F32)
        self.nb8row = self.asb("nb8row", [1, 16], F32)
        self.nbB = self.asb("nbB", [128, 16], F32)

        self.tri01 = self.asb("tri01", [128, 128], BF16)
        self.bdsel = self.asb("bdsel", [128, 16 * DC], F32)

    def load_attn_weights(self, cast_hbm=True):
        S, dram = self.sched, self.dram

        def ld(dst_ap, dstname, src_ap, rows, cols, scratch):
            wname = f"wl{scratch}"
            wl = self.wl[scratch]
            for c0 in range(0, cols, 512):
                cw = min(512, cols - c0)
                S.step("sync", lambda e, wl=wl, src=src_ap, rows=rows, c0=c0, cw=cw:
                       e.dma_start(out=wl[0:rows, 0:cw], in_=src[:, c0:c0 + cw]),
                       writes=[wname], tag=f"ld_{dstname}_{c0}")
                S.step("scalar", lambda e, wl=wl, dst=dst_ap, rows=rows, c0=c0, cw=cw:
                       e.copy(out=dst[:, c0:c0 + cw], in_=wl[0:rows, 0:cw]),
                       reads=[wname], writes=[dstname], tag=f"cast_{dstname}_{c0}")

        if cast_hbm:
            for src, dstd in [("WqT", "wq_bf"), ("WkT", "wk_bf"), ("WvT", "wv_bf")]:
                for k in range(DC):
                    ld(self.wpb[:, k * 1024:(k + 1) * 1024], "wpb",
                       dram[src][k * 128:(k + 1) * 128, :], 128, 1024, 0)
                S.step("sync", lambda e, dstd=dstd: e.dma_start(out=dram[dstd], in_=self.wpb[:]),
                       reads=["wpb"], writes=[dstd], tag=f"st_{dstd}")
            for h in range(H):
                wob = self.wob[h % 2]
                ld(wob[:], f"wob{h % 2}", dram["WoT"][h * 64:(h + 1) * 64, :], 64, 1024, 0)
                S.step("sync", lambda e, h=h, wob=wob: e.dma_start(
                    out=dram["wo_bf"][:, h * 1024:(h + 1) * 1024], in_=wob[:]),
                    reads=[f"wob{h % 2}"], writes=["wo_bf"], tag=f"st_wo{h}")
        for nm, dst, dn in [("bq_row", self.bqrb, "bqrb"), ("bk_row", self.bkrb, "bkrb"),
                            ("bv_row", self.bvrb, "bvrb"), ("bo_row", self.borb, "borb")]:
            ld(dst[:], dn, dram[nm], 1, D, 0)
        S.step("sync", lambda e: e.dma_start(out=self.nwcol[:], in_=dram["norm_w"]),
               writes=["nwcol"], tag="ld_nw")
        iot = self.iotab
        S.step("gpsimd", lambda g: g.iota(iot[:], pattern=[[1, 128]], base=0, channel_multiplier=-1),
               writes=["iotab"], tag="tri_i")
        S.step("gpsimd", lambda g: g.tensor_scalar(out=self.tri01[:], in0=iot[:], scalar1=0.0,
               scalar2=None, op0=ALU.is_ge), reads=["iotab"], writes=["tri01"], tag="tri_c")
        S.step("vector", lambda v: v.memset(self.bdsel[:], 0.0), writes=["bdsel"], tag="bd0")
        for j in range(DC):
            S.step("vector", lambda v, j=j: v.memset(self.bdsel[0:64, j * 16 + 2 * j:j * 16 + 2 * j + 1], 1.0),
                   reads=["bdsel"], writes=["bdsel"], tag=f"bd1_{j}")
            S.step("vector", lambda v, j=j: v.memset(self.bdsel[64:128, j * 16 + 2 * j + 1:j * 16 + 2 * j + 2], 1.0),
                   reads=["bdsel"], writes=["bdsel"], tag=f"bd2_{j}")
        vc = self.vcache[:].rearrange("p (g h e) -> p g h e", g=16, h=H)
        S.step("vector", lambda v: v.memset(vc[:, :, :, 64:65], 1.0), writes=["vcache_ones"], tag="vc1")

    def alloc_ssm(self):
        TH = 1024
        self.ar.alloc([128, 4096], BF16)
        self.wpb = self.asb("wpb", [128, 8192], BF16)
        self.WbTb = self.asb("WbTb", [128, ST * DC], BF16)
        self.WcTb = self.asb("WcTb", [128, ST * DC], BF16)
        self.cwcol = self.asb("cwcol", [128, 4 * DC], F32)
        self.cbcol = self.asb("cbcol", [128, DC], F32)
        self.bdcol = self.asb("bdcol", [128, DC], F32)
        self.bgcol = self.asb("bgcol", [128, DC], F32)
        self.bobcol = self.asb("bobcol", [128, DC], F32)
        self.Acol = self.asb("Acol", [128, ST * DC], F32)
        self.xg = [self.asb(f"xg{dc}", [128, TH + 3], BF16) for dc in range(DC)]
        self.xcb = [self.asb(f"xcb{dc}", [128, TC], BF16) for dc in range(DC)]
        self.dlh = [self.asb(f"dlh{dc}", [128, TH], BF16) for dc in range(DC)]
        self.uh = [self.asb(f"uh{dc}", [128, TH], BF16) for dc in range(DC)]
        self.yh = [self.asb(f"yh{dc}", [128, TH], F32) for dc in range(DC)]
        self.ygb = [self.asb(f"ygb{dc}", [128, TC], BF16) for dc in range(DC)]
        self.Brow = self.asb("Brow", [ST, TH], BF16)
        self.Crow = self.asb("Crow", [ST, TH], BF16)
        self.Brep = [self.asb(f"Brep{i}", [128, TH], BF16) for i in range(2)]
        self.brow0 = self.asb("brow0", [1, TH], BF16)
        self.crow0 = self.asb("crow0", [1, TH], BF16)
        self.Crep = [self.asb(f"Crep{i}", [128, TH], BF16) for i in range(2)]
        self.decay = [self.asb(f"decay{i}", [128, TH], F32) for i in range(2)]
        w0 = self.asb("wsc0", [128, TH], BF16); self.wsc = [w0, w0]
        self.hsc = [self.asb(f"hsc{i}", [128, TH], BF16) for i in range(2)]
        self.hcsc = [self.asb(f"hcsc{i}", [128, TH], BF16) for i in range(2)]
        self.hcar = self.asb("hcar", [128, ST * DC], F32)
        self.cacc = [self.asb(f"cacc{i}", [128, TC], F32) for i in range(2)]
        self.sgt = self.asb("sgt", [128, TC], F32)

    def load_ssm_weights(self):
        S, dram = self.sched, self.dram

        def ld(dst_ap, dstname, src_ap, rows, cols, scratch=0):
            wname = f"wl{scratch}"
            wl = self.wl[scratch]
            for c0 in range(0, cols, 512):
                cw = min(512, cols - c0)
                S.step("sync", lambda e, wl=wl, src=src_ap, rows=rows, c0=c0, cw=cw:
                       e.dma_start(out=wl[0:rows, 0:cw], in_=src[:, c0:c0 + cw]),
                       writes=[wname], tag=f"sld_{dstname}_{c0}")
                S.step("scalar", lambda e, wl=wl, dst=dst_ap, rows=rows, c0=c0, cw=cw:
                       e.copy(out=dst[:, c0:c0 + cw], in_=wl[0:rows, 0:cw]),
                       reads=[wname], writes=[dstname], tag=f"scast_{dstname}_{c0}")

        for src, dstd in [("WdT", "wd_bf"), ("WgT", "wg_bf"), ("WobT", "wob_bf")]:
            for k in range(DC):
                ld(self.wpb[:, k * 1024:(k + 1) * 1024], "wpb",
                   dram[src][k * 128:(k + 1) * 128, :], 128, 1024)
            S.step("sync", lambda e, dstd=dstd: e.dma_start(out=dram[dstd], in_=self.wpb[:]),
                   reads=["wpb"], writes=[dstd], tag=f"st_{dstd}")
        for k in range(DC):
            ld(self.WbTb[:, k * ST:(k + 1) * ST], "WbTb", dram["WbT"][k * 128:(k + 1) * 128, :], 128, ST)
            ld(self.WcTb[:, k * ST:(k + 1) * ST], "WcTb", dram["WcT"][k * 128:(k + 1) * 128, :], 128, ST)
        for nm, dst, dn in [("convw", self.cwcol, "cwcol"), ("convb", self.cbcol, "cbcol"),
                            ("bd", self.bdcol, "bdcol"), ("bg", self.bgcol, "bgcol"),
                            ("bob", self.bobcol, "bobcol")]:
            if nm == "convw":
                for dc in range(DC):
                    S.step("sync", lambda e, dc=dc: e.dma_start(
                        out=self.cwcol[:, dc * 4:(dc + 1) * 4],
                        in_=self.dram["convw"][dc * 128:(dc + 1) * 128, :]),
                        writes=["cwcol"], tag=f"ld_cw{dc}")
            else:
                S.step("sync", lambda e, nm=nm, dst=dst: e.dma_start(out=dst[:], in_=self.dram[nm]),
                       writes=[dn], tag=f"ld_{nm}")
        for dc in range(DC):
            S.step("sync", lambda e, dc=dc: e.dma_start(
                out=self.Acol[:, dc * ST:(dc + 1) * ST],
                in_=self.dram["log_a"][dc * 128:(dc + 1) * 128, :]),
                writes=["Acol"], tag=f"ld_A{dc}")
        S.step("scalar", lambda e: e.activation(out=self.Acol[:], in_=self.Acol[:], func=AF.Exp,
               bias=self.zcol[:, 0:1], scale=1.0), reads=["Acol", "zcol"], writes=["Acol"], tag="A_exp")
        S.step("vector", lambda v: v.tensor_scalar(out=self.Acol[:], in0=self.Acol[:],
               scalar1=-1.0, scalar2=None, op0=ALU.mult), reads=["Acol"], writes=["Acol"], tag="A_neg")

    def emit_ssm_seq(self, s, src, src_tm, dst, dst_tm):
        S = self.sched
        TH = 1024
        for dc in range(DC):
            S.step("vector", lambda v, dc=dc: v.memset(self.xg[dc][:, 0:3], 0.0),
                   writes=[f"xg{dc}"], tag=f"zxg{s}_{dc}")
        S.step("vector", lambda v: v.memset(self.hcar[:], 0.0), writes=["hcar"], tag=f"zh{s}")
        for half in range(2):
            for twi in range(2):
                c = half * 2 + twi
                self.emit_ssm_prefix(s, c, half, twi, src, src_tm)
            self.emit_ssm_scan(s, half)
            for twi in range(2):
                c = half * 2 + twi
                self.emit_ssm_suffix(s, c, half, twi, src, dst, dst_tm)
            if half == 0:
                for dc in range(DC):
                    S.step("vector", lambda v, dc=dc: v.tensor_copy(
                        out=self.xg[dc][:, 0:3], in_=self.xg[dc][:, TH:TH + 3]),
                        reads=[f"xg{dc}"], writes=[f"xg{dc}"], tag=f"cxg{s}_{half}_{dc}")

    def emit_ssm_prefix(self, s, c, half, twi, src, src_tm):
        S = self.sched
        dram = self.dram
        TH = 1024
        rot = 0
        self.emit_chunk_in(s, c, src, src_tm, rot)
        xfm = self.xfm[rot]
        if src_tm:
            for dc in range(DC):
                S.step("sync", lambda e, s=s, c=c, dc=dc: e.dma_start(
                    out=dram["xres_sc"][s, dc, :, c * TC:(c + 1) * TC], in_=xfm[dc][:]),
                    reads=[f"xfm{rot}_{dc}"], writes=["xres_sc"], tag=f"xsc{s}_{c}_{dc}")
        self.emit_ln_stats(s, c, rot)
        for dc in range(DC):
            t1 = self.t1[0]
            S.step("vector", lambda v, dc=dc, t1=t1: v.tensor_tensor(
                out=t1[:], in0=xfm[dc][:], in1=self.pbcA[:, 0:TC], op=ALU.subtract),
                reads=[f"xfm{rot}_{dc}", "pbcA"], writes=["t1_s"], tag=f"sx1{s}_{c}_{dc}")
            S.step("vector", lambda v, dc=dc, t1=t1, twi=twi: v.tensor_tensor(
                out=self.xg[dc][:, 3 + twi * TC:3 + (twi + 1) * TC], in0=t1[:],
                in1=self.pbcB[:, 0:TC], op=ALU.mult),
                reads=["t1_s", "pbcB"], writes=[f"xg{dc}"], tag=f"sx2{s}_{c}_{dc}")
        for dc in range(DC):
            ca = self.cacc[dc % 2]
            can = f"cacc_{dc % 2}"
            base = twi * TC
            S.step("vector", lambda v, dc=dc, ca=ca, base=base: v.tensor_scalar(
                out=ca[:], in0=self.xg[dc][:, base:base + TC],
                scalar1=self.cwcol[:, dc * 4:dc * 4 + 1], scalar2=None, op0=ALU.mult),
                reads=[f"xg{dc}", "cwcol"], writes=[can], tag=f"cv0{s}_{c}_{dc}")
            for j in (1, 2, 3):
                S.step("vector", lambda v, dc=dc, ca=ca, base=base, j=j: v.scalar_tensor_tensor(
                    out=ca[:], in0=self.xg[dc][:, base + j:base + j + TC],
                    scalar=self.cwcol[:, dc * 4 + j:dc * 4 + j + 1], in1=ca[:],
                    op0=ALU.mult, op1=ALU.add),
                    reads=[f"xg{dc}", "cwcol", can], writes=[can], tag=f"cv{j}{s}_{c}_{dc}")
            S.step("scalar", lambda e, dc=dc, ca=ca: e.activation(
                out=self.xcb[dc][:], in_=ca[:], func=AF.Silu,
                bias=self.cbcol[:, dc:dc + 1], scale=1.0),
                reads=[can, "cbcol"], writes=[f"xcb{dc}"], tag=f"cvs{s}_{c}_{dc}")
        xc_reads = [f"xcb{k}" for k in range(DC)]
        S.step("sync", lambda e: e.dma_start(out=self.wpb[:], in_=dram["wd_bf"]),
               reads=["wd_bf"], writes=["wpb"], tag=f"swd{s}_{c}")
        for oc in range(DC):
            pm, pmname = (self.pmA, "pmA") if oc % 2 == 0 else (self.pmB, "pmB")

            def f_d(e, oc=oc, pm=pm):
                last = None
                for k in range(DC):
                    last = e.matmul(out=pm[:, 0:TC],
                                    lhsT=self.wpb[:, k * 1024 + oc * 128:k * 1024 + (oc + 1) * 128],
                                    rhs=self.xcb[k][:], start=(k == 0), stop=(k == DC - 1))
                return last
            S.step("tensor", f_d, reads=xc_reads + ["wpb"], writes=[pmname], tag=f"dmm{s}_{c}_{oc}")
            S.step("scalar", lambda e, oc=oc, pm=pm: e.activation(
                out=self.sgt[:], in_=pm[:, 0:TC], func=AF.Exp,
                bias=self.bdcol[:, oc:oc + 1], scale=1.0),
                reads=[pmname, "bdcol"], writes=["sgt"], tag=f"dex{s}_{c}_{oc}")
            S.step("scalar", lambda e, oc=oc, twi=twi: e.activation(
                out=self.dlh[oc][:, twi * TC:(twi + 1) * TC], in_=self.sgt[:], func=AF.Ln,
                bias=self.onescol[:, 0:1], scale=1.0),
                reads=["sgt", "onescol"], writes=[f"dlh{oc}"], tag=f"dln{s}_{c}_{oc}")
            S.step("vector", lambda v, oc=oc, twi=twi: v.tensor_tensor(
                out=self.uh[oc][:, twi * TC:(twi + 1) * TC],
                in0=self.dlh[oc][:, twi * TC:(twi + 1) * TC], in1=self.xcb[oc][:], op=ALU.mult),
                reads=[f"dlh{oc}", f"xcb{oc}"], writes=[f"uh{oc}"], tag=f"du{s}_{c}_{oc}")
        def f_b(e):
            for k in range(DC):
                e.matmul(out=self.pstat[0:ST, 0:TC], lhsT=self.WbTb[:, k * ST:(k + 1) * ST],
                         rhs=self.xcb[k][:], start=(k == 0), stop=(k == DC - 1))
            last = None
            for k in range(DC):
                last = e.matmul(out=self.pbcA[0:ST, 0:TC], lhsT=self.WcTb[:, k * ST:(k + 1) * ST],
                                rhs=self.xcb[k][:], start=(k == 0), stop=(k == DC - 1))
            return last
        S.step("tensor", f_b, reads=xc_reads + ["WbTb", "WcTb"], writes=["pstat", "pbcA"], tag=f"bc{s}_{c}")
        S.step("scalar", lambda e, twi=twi: e.copy(out=self.Brow[:, twi * TC:(twi + 1) * TC],
               in_=self.pstat[0:ST, 0:TC]), reads=["pstat"], writes=["Brow"], tag=f"bev{s}_{c}")
        S.step("scalar", lambda e, twi=twi: e.copy(out=self.Crow[:, twi * TC:(twi + 1) * TC],
               in_=self.pbcA[0:ST, 0:TC]), reads=["pbcA"], writes=["Crow"], tag=f"cev{s}_{c}")

    def emit_ssm_scan(self, s, half):
        S = self.sched
        TH = 1024
        for n in range(ST):
            br, brn = self.Brep[n % 2], f"Brep{n % 2}"
            cr, crn = self.Crep[n % 2], f"Crep{n % 2}"

            S.step("sync", lambda e, n=n: e.dma_start(out=self.brow0[:], in_=self.Brow[n:n + 1, :]),
                   reads=["Brow"], writes=["brow0"], tag=f"bdma{s}_{half}_{n}")
            S.step("sync", lambda e, n=n: e.dma_start(out=self.crow0[:], in_=self.Crow[n:n + 1, :]),
                   reads=["Crow"], writes=["crow0"], tag=f"cdma{s}_{half}_{n}")

            def f_brep(e, n=n):
                e.matmul(out=self.pmA[:, 0:TC], lhsT=self.onesrow_b[0:1, 0:128],
                         rhs=self.brow0[0:1, 0:TC], start=True, stop=True)
                e.matmul(out=self.pmB[:, 0:TC], lhsT=self.onesrow_b[0:1, 0:128],
                         rhs=self.brow0[0:1, TC:2 * TC], start=True, stop=True)
                e.matmul(out=self.paux1[:, 0:TC], lhsT=self.onesrow_b[0:1, 0:128],
                         rhs=self.crow0[0:1, 0:TC], start=True, stop=True)
                return e.matmul(out=self.paux2[:, 0:TC], lhsT=self.onesrow_b[0:1, 0:128],
                                rhs=self.crow0[0:1, TC:2 * TC], start=True, stop=True)
            S.step("tensor", f_brep, reads=["brow0", "crow0", "onesrow_b"],
                   writes=["pmA", "pmB", "paux1", "paux2"], tag=f"nrep{s}_{half}_{n}")
            S.step("scalar", lambda e, br=br: e.copy(out=br[:, 0:TC], in_=self.pmA[:, 0:TC]),
                   reads=["pmA"], writes=[brn], tag=f"bre1{s}_{half}_{n}")
            S.step("scalar", lambda e, br=br: e.copy(out=br[:, TC:2 * TC], in_=self.pmB[:, 0:TC]),
                   reads=["pmB"], writes=[brn], tag=f"bre2{s}_{half}_{n}")
            S.step("scalar", lambda e, cr=cr: e.copy(out=cr[:, 0:TC], in_=self.paux1[:, 0:TC]),
                   reads=["paux1"], writes=[crn], tag=f"cre1{s}_{half}_{n}")
            S.step("scalar", lambda e, cr=cr: e.copy(out=cr[:, TC:2 * TC], in_=self.paux2[:, 0:TC]),
                   reads=["paux2"], writes=[crn], tag=f"cre2{s}_{half}_{n}")
            for dc in range(DC):
                dec, decn = self.decay[dc % 2], f"decay{dc % 2}"
                ws, wsn = self.wsc[0], "wsc0"
                hs, hsn = self.hsc[dc % 2], f"hsc{dc % 2}"
                hc, hcn = self.hcsc[dc % 2], f"hcsc{dc % 2}"
                col = dc * ST + n
                S.step("scalar", lambda e, dc=dc, n=n, dec=dec: e.activation(
                    out=dec[:], in_=self.dlh[dc][:], func=AF.Exp,
                    bias=self.zcol[:, 0:1], scale=self.Acol[:, dc * ST + n:dc * ST + n + 1]),
                    reads=[f"dlh{dc}", "Acol", "zcol"], writes=[decn], tag=f"dk{s}_{half}_{n}_{dc}")
                S.step("vector", lambda v, dc=dc, br=br, ws=ws: v.tensor_tensor(
                    out=ws[:], in0=self.uh[dc][:], in1=br[:], op=ALU.mult),
                    reads=[f"uh{dc}", brn], writes=[wsn], tag=f"wm{s}_{half}_{n}_{dc}")
                S.step("vector", lambda v, dec=dec, ws=ws, hs=hs, col=col: v.tensor_tensor_scan(
                    out=hs[:], data0=dec[:], data1=ws[:], initial=self.hcar[:, col:col + 1],
                    op0=ALU.mult, op1=ALU.add),
                    reads=[decn, wsn, "hcar"], writes=[hsn], tag=f"sc{s}_{half}_{n}_{dc}")
                S.step("vector", lambda v, hs=hs, col=col: v.tensor_copy(
                    out=self.hcar[:, col:col + 1], in_=hs[:, TH - 1:TH]),
                    reads=[hsn], writes=["hcar"], tag=f"hc{s}_{half}_{n}_{dc}")
                S.step("vector", lambda v, hs=hs, cr=cr, hc=hc: v.tensor_tensor(
                    out=hc[:], in0=hs[:], in1=cr[:], op=ALU.mult),
                    reads=[hsn, crn], writes=[hcn], tag=f"hcm{s}_{half}_{n}_{dc}")
                if n == 0:
                    S.step("gpsimd", lambda g, dc=dc, hc=hc: g.tensor_copy(
                        out=self.yh[dc][:], in_=hc[:]),
                        reads=[hcn], writes=[f"yh{dc}"], tag=f"ya{s}_{half}_{n}_{dc}")
                else:
                    S.step("gpsimd", lambda g, dc=dc, hc=hc: g.tensor_tensor(
                        out=self.yh[dc][:], in0=hc[:], in1=self.yh[dc][:], op=ALU.add),
                        reads=[hcn, f"yh{dc}"], writes=[f"yh{dc}"], tag=f"ya{s}_{half}_{n}_{dc}")

    def emit_ssm_suffix(self, s, c, half, twi, src, dst, dst_tm):
        S = self.sched
        dram = self.dram
        rot = 0
        xfm = self.xfm[rot]
        rsrc = "xres_sc" if src == "x" else src
        S.step("sync", lambda e: e.dma_start(out=self.wpb[:], in_=dram["wg_bf"]),
               reads=["wg_bf"], writes=["wpb"], tag=f"swg{s}_{c}")
        xg_reads = [f"xg{k}" for k in range(DC)]
        for oc in range(DC):
            pm, pmname = (self.pmA, "pmA") if oc % 2 == 0 else (self.pmB, "pmB")

            def f_g(e, oc=oc, pm=pm, twi=twi):
                last = None
                for k in range(DC):
                    last = e.matmul(out=pm[:, 0:TC],
                                    lhsT=self.wpb[:, k * 1024 + oc * 128:k * 1024 + (oc + 1) * 128],
                                    rhs=self.xg[k][:, 3 + twi * TC:3 + (twi + 1) * TC],
                                    start=(k == 0), stop=(k == DC - 1))
                return last
            S.step("tensor", f_g, reads=xg_reads + ["wpb"], writes=[pmname], tag=f"gmm{s}_{c}_{oc}")
            S.step("scalar", lambda e, oc=oc, pm=pm: e.activation(
                out=self.sgt[:], in_=pm[:, 0:TC], func=AF.Sigmoid,
                bias=self.bgcol[:, oc:oc + 1], scale=1.0),
                reads=[pmname, "bgcol"], writes=["sgt"], tag=f"gev{s}_{c}_{oc}")
            S.step("vector", lambda v, oc=oc, twi=twi: v.tensor_tensor(
                out=self.ygb[oc][:], in0=self.yh[oc][:, twi * TC:(twi + 1) * TC],
                in1=self.sgt[:], op=ALU.mult),
                reads=[f"yh{oc}", "sgt"], writes=[f"ygb{oc}"], tag=f"yg{s}_{c}_{oc}")
        for dc in range(DC):
            S.step("sync", lambda e, s=s, c=c, dc=dc, rsrc=rsrc: e.dma_start(
                out=xfm[dc][:], in_=dram[rsrc][s, dc, :, c * TC:(c + 1) * TC]),
                reads=[rsrc], writes=[f"xfm{rot}_{dc}"], tag=f"srl{s}_{c}_{dc}")
        yg_reads = [f"ygb{k}" for k in range(DC)]
        S.step("sync", lambda e: e.dma_start(out=self.wpb[:], in_=dram["wob_bf"]),
               reads=["wob_bf"], writes=["wpb"], tag=f"swo{s}_{c}")
        for oc in range(DC):
            pm, pmname = (self.pmA, "pmA") if oc % 2 == 0 else (self.pmB, "pmB")

            def f_o(e, oc=oc, pm=pm):
                last = None
                for k in range(DC):
                    last = e.matmul(out=pm[:, 0:TC],
                                    lhsT=self.wpb[:, k * 1024 + oc * 128:k * 1024 + (oc + 1) * 128],
                                    rhs=self.ygb[k][:], start=(k == 0), stop=(k == DC - 1))
                return last
            S.step("tensor", f_o, reads=yg_reads + ["wpb"], writes=[pmname], tag=f"omm{s}_{c}_{oc}")
            S.step("vector", lambda v, oc=oc, pm=pm: v.scalar_tensor_tensor(
                out=xfm[oc][:], in0=pm[:, 0:TC], scalar=self.bobcol[:, oc:oc + 1],
                in1=xfm[oc][:], op0=ALU.add, op1=ALU.add),
                reads=[pmname, "bobcol", f"xfm{rot}_{oc}"], writes=[f"xfm{rot}_{oc}"], tag=f"oev{s}_{c}_{oc}")
        self.emit_chunk_out(s, c, dst, dst_tm, rot)

    def emit_attn_seq(self, s, src, src_tm, dst, dst_tm):
        S = self.sched
        for c in range(NCH):
            rot = 0
            self.emit_chunk_in(s, c, src, src_tm, rot)
            self.emit_ln_stats(s, c, rot)
            self.emit_attn_xn2(s, c, rot)
            self.emit_attn_qkv(s, c, rot)
            self.emit_attn_bounds(s, c)
            self.emit_attn_core(s, c)
            self.emit_attn_out(s, c, rot)
            self.emit_chunk_out(s, c, dst, dst_tm, rot)

    def emit_attn_xn2(self, s, c, rot):
        S = self.sched
        S.step("scalar", lambda sc: sc.activation(out=self.rstd, in_=self.rvar, func=AF.Ln,
               bias=self.csm[0:1, 4:5], scale=1.0 + 1e-6),
               reads=["rvar", "csm"], writes=["rstd"], tag=f"a_ln{s}_{c}")
        S.step("scalar", lambda sc: sc.activation(out=self.rstd, in_=self.rstd, func=AF.Exp,
               bias=self.csm[0:1, 3:4], scale=-0.5),
               reads=["rstd", "csm"], writes=["rstd"], tag=f"a_ex{s}_{c}")
        def f_bc(e):
            e.matmul(out=self.pbcA[:, 0:TC], lhsT=self.onesrow[:], rhs=self.rmean, start=True, stop=True)
            return e.matmul(out=self.pbcB[:, 0:TC], lhsT=self.onesrow[:], rhs=self.rstd, start=True, stop=True)
        S.step("tensor", f_bc, reads=["rmean", "rstd", "onesrow"], writes=["pbcA", "pbcB"], tag=f"a_bc{s}_{c}")
        xfm, xnb = self.xfm[rot], self.xnb[rot]
        for dc in range(DC):
            t1 = self.t1[dc % 2]
            S.step("vector", lambda v, dc=dc, t1=t1: v.tensor_tensor(
                out=t1[:], in0=xfm[dc][:], in1=self.pbcA[:, 0:TC], op=ALU.subtract),
                reads=[f"xfm{rot}_{dc}", "pbcA"], writes=["t1_s"], tag=f"a_x1{s}_{c}_{dc}")
            S.step("vector", lambda v, dc=dc, t1=t1: v.scalar_tensor_tensor(
                out=self.xn2f[dc][:], in0=t1[:], scalar=self.nwcol[:, dc:dc + 1],
                in1=self.pbcB[:, 0:TC], op0=ALU.mult, op1=ALU.mult),
                reads=["t1_s", "nwcol", "pbcB"], writes=[f"xn2f{dc}"], tag=f"a_x2{s}_{c}_{dc}")
            S.step("scalar", lambda sc, dc=dc: sc.copy(out=xnb[dc][:], in_=self.xn2f[dc][:]),
                reads=[f"xn2f{dc}"], writes=[f"xnb{rot}_{dc}"], tag=f"a_x3{s}_{c}_{dc}")

    def emit_attn_qkv(self, s, c, rot):
        S = self.sched
        dram = self.dram
        xnb = self.xnb[rot]
        xn_reads = [f"xnb{rot}_{k}" for k in range(DC)]

        S.step("sync", lambda e: e.dma_start(out=self.wpb[:], in_=dram["wq_bf"]),
               reads=["wq_bf"], writes=["wpb"], tag=f"wq{s}_{c}")
        for oc in range(DC):
            pm, pmname = (self.pmA, "pmA") if oc % 2 == 0 else (self.pmB, "pmB")
            pm, pmname = (self.paux2, "paux2") if oc % 4 >= 2 else (pm, pmname)

            def f_q(e, oc=oc, pm=pm):
                for k in range(DC):
                    e.matmul(out=pm[:, 0:TC],
                             lhsT=self.wpb[:, k * 1024 + oc * 128:k * 1024 + (oc + 1) * 128],
                             rhs=xnb[k][:], start=(k == 0), stop=False)
                return e.matmul(out=pm[:, 0:TC], lhsT=self.bqrb[0:1, oc * 128:(oc + 1) * 128],
                                rhs=self.onesrow_b[:], start=False, stop=True)
            S.step("tensor", f_q, reads=xn_reads + ["wpb", "bqrb", "onesrow_b"], writes=[pmname],
                   tag=f"qmm{s}_{c}_{oc}")
            S.step("scalar", lambda e, oc=oc, pm=pm: e.copy(out=self.qb[oc][:], in_=pm[:, 0:TC]),
                   reads=[pmname], writes=[f"qb{oc}"], tag=f"qev{s}_{c}_{oc}")
        S.step("sync", lambda e: e.dma_start(out=self.wpb[:], in_=dram["wk_bf"]),
               reads=["wk_bf"], writes=["wpb"], tag=f"wk{s}_{c}")
        for oc in range(DC):
            pm, pmname = (self.pmA, "pmA") if oc % 2 == 0 else (self.pmB, "pmB")

            def f_k(e, oc=oc, pm=pm):
                for k in range(DC):
                    e.matmul(out=pm[:, 0:TC],
                             lhsT=self.wpb[:, k * 1024 + oc * 128:k * 1024 + (oc + 1) * 128],
                             rhs=xnb[k][:], start=(k == 0), stop=False)
                return e.matmul(out=pm[:, 0:TC], lhsT=self.bkrb[0:1, oc * 128:(oc + 1) * 128],
                                rhs=self.onesrow_b[:], start=False, stop=True)
            S.step("tensor", f_k, reads=xn_reads + ["wpb", "bkrb", "onesrow_b"], writes=[pmname],
                   tag=f"kmm{s}_{c}_{oc}")
            S.step("scalar", lambda e, oc=oc, pm=pm, c=c: e.copy(
                out=self.kcache[oc][:, c * TC:(c + 1) * TC], in_=pm[:, 0:TC]),
                reads=[pmname], writes=[f"kc{oc}"], tag=f"kev{s}_{c}_{oc}")
        S.step("sync", lambda e: e.dma_start(out=self.wpb[:], in_=dram["wv_bf"]),
               reads=["wv_bf"], writes=["wpb"], tag=f"wv{s}_{c}")
        vc = self.vcache
        for tsub in range(4):
            gts = c * 4 + tsub
            for nv in range(2):
                pm, pmname = (self.pmA, "pmA") if nv == 0 else (self.pmB, "pmB")

                def f_v(e, tsub=tsub, nv=nv, pm=pm):
                    for k in range(DC):
                        e.matmul(out=pm[:, 0:TC], lhsT=xnb[k][:, tsub * 128:(tsub + 1) * 128],
                                 rhs=self.wpb[:, k * 1024 + nv * 512:k * 1024 + (nv + 1) * 512],
                                 start=(k == 0), stop=False)
                    return e.matmul(out=pm[:, 0:TC], lhsT=self.onesrow_b[0:1, 0:128],
                                    rhs=self.bvrb[0:1, nv * 512:(nv + 1) * 512], start=False, stop=True)
                S.step("tensor", f_v, reads=xn_reads + ["wpb", "bvrb", "onesrow_b"], writes=[pmname],
                       tag=f"vmm{s}_{c}_{tsub}_{nv}")

                def f_vev(e, tsub=tsub, nv=nv, pm=pm, gts=gts):
                    out = vc[:].rearrange("p (g h e) -> p g h e", g=16, h=H)[
                        :, gts, nv * 8:(nv + 1) * 8, 0:64]
                    return e.copy(out=out, in_=pm[:, 0:TC].rearrange("p (h e) -> p h e", h=8))
                S.step("scalar", f_vev, reads=[pmname], writes=["vcache"], tag=f"vev{s}_{c}_{tsub}_{nv}")

    def emit_attn_bounds(self, s, c):
        S = self.sched
        for j in range(DC):
            x2 = self.x2t[j % 2]
            S.step("scalar", lambda e, j=j, x2=x2: e.square(out=x2[:], in_=self.qb[j][:]),
                   reads=[f"qb{j}"], writes=["x2t_s"], tag=f"bq2{s}_{c}_{j}")
            S.step("tensor", lambda e, j=j, x2=x2: e.matmul(
                out=self.pstat[0:16, 0:TC], lhsT=self.bdsel[:, j * 16:(j + 1) * 16], rhs=x2[:],
                start=(j == 0), stop=(j == DC - 1)),
                reads=["x2t_s", "bdsel"], writes=["pstat"], tag=f"bqm{s}_{c}_{j}")
        for j in range(DC):
            x2 = self.x2t[j % 2]
            S.step("scalar", lambda e, j=j, x2=x2, c=c: e.square(
                out=x2[:], in_=self.kcache[j][:, c * TC:(c + 1) * TC]),
                reads=[f"kc{j}"], writes=["x2t_s"], tag=f"bk2{s}_{c}_{j}")
            S.step("tensor", lambda e, j=j, x2=x2: e.matmul(
                out=self.pbcA[0:16, 0:TC], lhsT=self.bdsel[:, j * 16:(j + 1) * 16], rhs=x2[:],
                start=(j == 0), stop=(j == DC - 1)),
                reads=["x2t_s", "bdsel"], writes=["pbcA"], tag=f"bkm{s}_{c}_{j}")
        S.step("scalar", lambda e: e.activation(out=self.n16a[:], in_=self.pstat[0:16, 0:TC],
               func=AF.Ln, bias=self.c6col[0:16, 0:1], scale=1.0),
               reads=["pstat", "c6col"], writes=["n16a"], tag=f"bqn{s}_{c}")
        S.step("scalar", lambda e: e.activation(out=self.n16a[:], in_=self.n16a[:],
               func=AF.Exp, bias=self.zcol[0:16, 0:1], scale=0.5),
               reads=["n16a", "zcol"], writes=["n16a"], tag=f"bqe{s}_{c}")
        S.step("scalar", lambda e: e.activation(out=self.n16b[:], in_=self.pbcA[0:16, 0:TC],
               func=AF.Ln, bias=self.c6col[0:16, 0:1], scale=1.0),
               reads=["pbcA", "c6col"], writes=["n16b"], tag=f"bkn{s}_{c}")
        S.step("scalar", lambda e: e.activation(out=self.n16b[:], in_=self.n16b[:],
               func=AF.Exp, bias=self.zcol[0:16, 0:1], scale=0.5),
               reads=["n16b", "zcol"], writes=["n16b"], tag=f"bke{s}_{c}")
        S.step("vector", lambda v: v.tensor_reduce(out=self.kntmp[:], in_=self.n16b[:],
               axis=AX.X, op=ALU.max), reads=["n16b"], writes=["kntmp"], tag=f"bkx{s}_{c}")
        if c == 0:
            S.step("vector", lambda v: v.tensor_copy(out=self.knmax[:], in_=self.kntmp[:]),
                   reads=["kntmp"], writes=["knmax"], tag=f"bki{s}_{c}")
        else:
            S.step("vector", lambda v: v.tensor_tensor(out=self.knmax[:], in0=self.knmax[:],
                   in1=self.kntmp[:], op=ALU.max), reads=["kntmp", "knmax"], writes=["knmax"], tag=f"bki{s}_{c}")
        S.step("vector", lambda v: v.tensor_scalar(out=self.n16a[:], in0=self.n16a[:],
               scalar1=self.knmax[:, 0:1], scalar2=None, op0=ALU.mult),
               reads=["n16a", "knmax"], writes=["n16a"], tag=f"bb{s}_{c}")
        S.step("vector", lambda v: v.tensor_reduce(out=self.bstar[:], in_=self.n16a[:],
               axis=AX.X, op=ALU.max), reads=["n16a"], writes=["bstar"], tag=f"bs{s}_{c}")
        S.step("vector", lambda v: v.tensor_scalar(out=self.bstar[:], in0=self.bstar[:],
               scalar1=-0.125, scalar2=None, op0=ALU.mult), reads=["bstar"], writes=["bstar"], tag=f"bn{s}_{c}")
        S.step("tensor", lambda e: e.transpose(out=self.pbT[0:1, 0:16], in_=self.bstar[:],
               identity=self.identf[0:16, 0:16]), reads=["bstar", "identf"], writes=["pbT"], tag=f"bt{s}_{c}")
        S.step("vector", lambda v: v.tensor_copy(out=self.nb8row[:], in_=self.pbT[0:1, 0:16]),
               reads=["pbT"], writes=["nb8row"], tag=f"btv{s}_{c}")
        S.step("tensor", lambda e: e.matmul(out=self.pbcB[:, 0:16], lhsT=self.onesrow[:],
               rhs=self.nb8row[:], start=True, stop=True),
               reads=["nb8row", "onesrow"], writes=["pbcB"], tag=f"btb{s}_{c}")
        S.step("scalar", lambda e: e.copy(out=self.nbB[:], in_=self.pbcB[:, 0:16]),
               reads=["pbcB"], writes=["nbB"], tag=f"bte{s}_{c}")

    def emit_attn_core(self, s, c):
        for wave in range(2):
            self.emit_attn_wave(s, c, wave)
        for wave in range(2):
            self.emit_attn_wo_partial(s, c, wave)

    def emit_attn_wave(self, s, c, wave):
        S = self.sched
        for h in range(wave * 8, wave * 8 + 8):
            j, hp = h // 2, h % 2
            for kc in range(4 * c + 4):
                dd = kc - 4 * c
                q0 = max(dd, 0) * 128
                pt = self.ptile[kc % 3]
                ptname = f"ptile{kc % 3}"
                pm, pmname = [(self.pmA, "pmA"), (self.pmB, "pmB"), (self.paux2, "paux2")][kc % 3]

                def f_sc(e, j=j, hp=hp, kc=kc, q0=q0, pm=pm):
                    return e.matmul(out=pm[:, q0:TC],
                                    lhsT=self.kcache[j][hp * 64:(hp + 1) * 64, kc * 128:(kc + 1) * 128],
                                    rhs=self.qb[j][hp * 64:(hp + 1) * 64, q0:TC],
                                    start=True, stop=True)
                S.step("tensor", f_sc, reads=[f"kc{j}", f"qb{j}"], writes=[pmname], tag=f"sc{s}_{c}_{h}_{kc}")
                if q0 > 0:
                    S.step("vector", lambda v, pt=pt, q0=q0: v.memset(pt[:, 0:q0], 0.0),
                           writes=[ptname], tag=f"pz{s}_{c}_{h}_{kc}")
                S.step("scalar", lambda e, pt=pt, h=h, q0=q0, pm=pm: e.activation(
                    out=pt[:, q0:TC], in_=pm[:, q0:TC], func=AF.Exp,
                    bias=self.nbB[:, h:h + 1], scale=0.125),
                    reads=[pmname, "nbB"], writes=[ptname], tag=f"px{s}_{c}_{h}_{kc}")
                if dd >= 0:
                    S.step("vector", lambda v, pt=pt, q0=q0: v.tensor_tensor(
                        out=pt[:, q0:q0 + 128], in0=pt[:, q0:q0 + 128], in1=self.tri01[:], op=ALU.mult),
                        reads=[ptname, "tri01"], writes=[ptname], tag=f"pm{s}_{c}_{h}_{kc}")

                def f_av(e, h=h, kc=kc, pt=pt, c=c):
                    off = (kc * H + h) * 65
                    return e.matmul(out=self.paux1[0:65, 0:TC], lhsT=self.vcache[:, off:off + 65],
                                    rhs=pt[:, 0:TC], start=(kc == 0), stop=(kc == 4 * c + 3))
                S.step("tensor", f_av, reads=[ptname, "vcache", "vcache_ones"], writes=["paux1"],
                       tag=f"av{s}_{c}_{h}_{kc}")
            o64 = self.o64[0]
            S.step("scalar", lambda e, h=h: e.copy(out=self.osb[:, (h % 8) * TC:(h % 8 + 1) * TC],
                   in_=self.paux1[0:64, 0:TC]), reads=["paux1"], writes=["osb"], tag=f"oev{s}_{c}_{h}")
            S.step("scalar", lambda e, h=h, o64=o64: e.copy(out=o64[64:65, :],
                   in_=self.paux1[64:65, 0:TC]), reads=["paux1"], writes=["rowpool"], tag=f"oevs{s}_{c}_{h}")
            rs, rsn = (self.rcpr[0], "rowpoolA") if h % 2 == 0 else (self.n16a[0:1, :], "n16a")
            S.step("sync", lambda e, rs=rs, o64=o64: e.dma_start(out=rs, in_=o64[64:65, :]),
                   reads=["rowpool"], writes=[rsn], tag=f"sdma{s}_{c}_{h}")
            rr, rrn = self.rcpq[h % 2], f"rcpq{h % 2}"
            rbank, rbname = (self.pbcB, "pbcB") if h % 2 == 0 else (self.pbT, "pbT")
            S.step("vector", lambda v, rs=rs, rr=rr: v.reciprocal(out=rr[:], in_=rs),
                   reads=[rsn], writes=[rrn], tag=f"rcph{s}_{c}_{h}")
            S.step("tensor", lambda e, h=h, rr=rr, rbank=rbank: e.matmul(out=rbank[0:64, 0:TC],
                   lhsT=self.onesrow[0:1, 0:64], rhs=rr[:], start=True, stop=True),
                   reads=[rrn, "onesrow"], writes=[rbname], tag=f"rb{s}_{c}_{h}")
            S.step("vector", lambda v, h=h, rbank=rbank: v.tensor_tensor(out=self.attnO[:, h * TC:(h + 1) * TC],
                   in0=self.osb[0:64, (h % 8) * TC:(h % 8 + 1) * TC], in1=rbank[0:64, 0:TC], op=ALU.mult),
                   reads=["osb", rbname], writes=["attnO"], tag=f"onrm{s}_{c}_{h}")

    def emit_attn_wo_partial(self, s, c, wave):
        S = self.sched
        dram = self.dram
        banks = [self.pbT, self.pstat, self.pbcA, self.pbcB, self.pmA, self.pmB, self.paux1, self.paux2]
        bnames = ["pbT", "pstat", "pbcA", "pbcB", "pmA", "pmB", "paux1", "paux2"]
        for h in range(wave * 8, wave * 8 + 8):
            wob = self.wob[h % 2]
            S.step("sync", lambda e, h=h, wob=wob: e.dma_start(
                out=wob[:], in_=dram["wo_bf"][:, h * 1024:(h + 1) * 1024]),
                reads=["wo_bf"], writes=[f"wob{h % 2}"], tag=f"wol{s}_{c}_{h}")

            def f_wo(e, h=h, wob=wob):
                last = None
                for oc in range(DC):
                    last = e.matmul(out=banks[oc][:, 0:TC], lhsT=wob[:, oc * 128:(oc + 1) * 128],
                                    rhs=self.attnO[:, h * TC:(h + 1) * TC],
                                    start=(h == 0), stop=False)
                return last
            S.step("tensor", f_wo, reads=["attnO", f"wob{h % 2}"], writes=bnames,
                   tag=f"wo{s}_{c}_{h}")

    def emit_attn_out(self, s, c, rot):
        S = self.sched
        dram = self.dram
        xfm = self.xfm[rot]
        banks = [self.pbT, self.pstat, self.pbcA, self.pbcB, self.pmA, self.pmB, self.paux1, self.paux2]
        bnames = ["pbT", "pstat", "pbcA", "pbcB", "pmA", "pmB", "paux1", "paux2"]

        def f_aug(e):
            last = None
            for oc in range(DC):
                last = e.matmul(out=banks[oc][:, 0:TC], lhsT=self.borb[0:1, oc * 128:(oc + 1) * 128],
                                rhs=self.onesrow_b[:], start=False, stop=True)
            return last
        S.step("tensor", f_aug, reads=["borb", "onesrow_b"], writes=bnames, tag=f"woaug{s}_{c}")

        for oc in range(DC):
            t1 = self.t1[oc % 2]
            S.step("vector", lambda v, oc=oc, t1=t1: v.scalar_tensor_tensor(
                out=t1[:], in0=banks[oc][:, 0:TC], scalar=1.0, in1=self.xn2f[oc][:], op0=ALU.mult, op1=ALU.add),
                reads=[bnames[oc], f"xn2f{oc}"], writes=["t1_s"], tag=f"woe{s}_{c}_{oc}")
            S.step("vector", lambda v, oc=oc, t1=t1: v.tensor_tensor(
                out=xfm[oc][:], in0=t1[:], in1=xfm[oc][:], op=ALU.add),
                reads=["t1_s", f"xfm{rot}_{oc}"], writes=[f"xfm{rot}_{oc}"], tag=f"wor{s}_{c}_{oc}")

    def emit_chunk_in(self, s, c, src, src_tm, rot):
        S = self.sched
        xfm = self.xfm[rot]
        srcap = self.dram[src]
        if src_tm:
            for half in range(2):
                for i in range(2):
                    ts = half * 2 + i
                    xt = self.xtm[i]
                    S.step("sync", lambda e, xt=xt, s=s, c=c, ts=ts: e.dma_start(
                        out=xt[:], in_=srcap[s, c * TC + ts * 128: c * TC + (ts + 1) * 128, :]),
                        writes=[f"xtm{ts % 2}"], tag=f"in_tm{s}_{c}_{ts}")
                for dc in range(DC):
                    pb, pbname = (self.pbT, "pbT") if dc % 2 == 0 else (self.pstat, "pstat")
                    def f_tr(e, dc=dc, pb=pb):
                        e.transpose(out=pb[:, 0:128],
                                    in_=self.xtm[0][:, dc * 128:(dc + 1) * 128],
                                    identity=self.identf[:])
                        return e.transpose(out=pb[:, 128:256],
                                           in_=self.xtm[1][:, dc * 128:(dc + 1) * 128],
                                           identity=self.identf[:])
                    S.step("tensor", f_tr, reads=["xtm0", "xtm1", "identf"],
                           writes=[pbname], tag=f"in_tr{s}_{c}_{half}_{dc}")
                    S.step("scalar", lambda e, dc=dc, half=half, pb=pb: e.copy(
                        out=xfm[dc][:, half * 256:(half + 1) * 256], in_=pb[:, 0:256]),
                        reads=[pbname], writes=[f"xfm{rot}_{dc}"], tag=f"in_ev{s}_{c}_{half}_{dc}")
        else:
            for dc in range(DC):
                S.step("sync", lambda e, s=s, c=c, dc=dc: e.dma_start(
                    out=xfm[dc][:], in_=srcap[s, dc, :, c * TC:(c + 1) * TC]),
                    writes=[f"xfm{rot}_{dc}"], tag=f"in_fm{s}_{c}_{dc}")

    def emit_ln_stats(self, s, c, rot):
        S = self.sched
        xfm, csm = self.xfm[rot], self.csm
        for dc in range(DC):
            x2 = self.x2t[dc % 2]
            S.step("scalar", lambda e, dc=dc, x2=x2: e.square(out=x2[:], in_=xfm[dc][:]),
                   reads=[f"xfm{rot}_{dc}"], writes=["x2t_s"], tag=f"sq{s}_{c}_{dc}")

            def f_mm(e, dc=dc, x2=x2):
                e.matmul(out=self.pstat[0:1, 0:TC], lhsT=self.onescol[:], rhs=xfm[dc][:],
                         start=(dc == 0), stop=(dc == DC - 1))
                return e.matmul(out=self.pbcA[0:1, 0:TC], lhsT=self.onescol[:], rhs=x2[:],
                                start=(dc == 0), stop=(dc == DC - 1))
            S.step("tensor", f_mm, reads=[f"xfm{rot}_{dc}", "x2t_s", "onescol"],
                   writes=["pstat", "pbcA"], tag=f"stmm{s}_{c}_{dc}")

        S.step("vector", lambda v: v.tensor_scalar(out=self.rmean, in0=self.pstat[0:1, 0:TC],
               scalar1=1.0 / D, scalar2=None, op0=ALU.mult),
               reads=["pstat"], writes=["rmean"], tag=f"stm{s}_{c}")
        S.step("vector", lambda v: v.tensor_scalar(out=self.rvar, in0=self.pbcA[0:1, 0:TC],
               scalar1=1.0 / D, scalar2=None, op0=ALU.mult),
               reads=["pbcA"], writes=["rvar"], tag=f"ste{s}_{c}")
        S.step("vector", lambda v: v.tensor_tensor(out=self.rtmp, in0=self.rmean, in1=self.rmean, op=ALU.mult),
               reads=["rmean"], writes=["rtmp"], tag=f"stm2{s}_{c}")
        S.step("vector", lambda v: v.tensor_tensor(out=self.rvar, in0=self.rvar, in1=self.rtmp, op=ALU.subtract),
               reads=["rtmp", "rvar"], writes=["rvar"], tag=f"stv{s}_{c}")

        S.step("scalar", lambda sc: sc.activation(out=self.rstd, in_=self.rvar, func=AF.Ln,
               bias=csm[0:1, 0:1], scale=1.0),
               reads=["rvar", "csm"], writes=["rstd"], tag=f"strln{s}_{c}")
        S.step("scalar", lambda sc: sc.activation(out=self.rstd, in_=self.rstd, func=AF.Exp,
               bias=csm[0:1, 3:4], scale=-0.5),
               reads=["rstd", "csm"], writes=["rstd"], tag=f"strex{s}_{c}")

        def f_bc(e):
            e.matmul(out=self.pbcA[:, 0:TC], lhsT=self.onesrow[:], rhs=self.rmean, start=True, stop=True)
            return e.matmul(out=self.pbcB[:, 0:TC], lhsT=self.onesrow[:], rhs=self.rstd, start=True, stop=True)
        S.step("tensor", f_bc, reads=["rmean", "rstd", "onesrow"], writes=["pbcA", "pbcB"], tag=f"stbc{s}_{c}")

    def emit_xn(self, s, c, rot):
        S = self.sched
        xfm, xnb = self.xfm[rot], self.xnb[rot]
        for dc in range(DC):
            t1 = self.t1[dc % 2]

            def f1(v, dc=dc, t1=t1):
                return v.tensor_tensor(out=t1[:], in0=xfm[dc][:], in1=self.pbcA[:, 0:TC], op=ALU.subtract)
            S.step("vector", f1, reads=[f"xfm{rot}_{dc}", "pbcA"], writes=["t1_s"],
                   tag=f"xn1_{s}_{c}_{dc}")

            def f2(v, dc=dc, t1=t1):
                return v.tensor_tensor(out=xnb[dc][:], in0=t1[:], in1=self.pbcB[:, 0:TC], op=ALU.mult)
            S.step("vector", f2, reads=["t1_s", "pbcB"], writes=[f"xnb{rot}_{dc}"],
                   tag=f"xn2_{s}_{c}_{dc}")

    def emit_ffn_seq(self, s, src, src_tm, dst, dst_tm):
        for c in range(NCH):
            rot = 0
            self.emit_chunk_in(s, c, src, src_tm, rot)
            dbg = getattr(self, "dbg", False) and s == 0 and c == 0
            if dbg:
                self.dump("xfm0", self.xfm[rot][0][:], [f"xfm{rot}_0"], [128, TC])
            self.emit_ln_stats(s, c, rot)
            if dbg:
                self.dump("rmean", self.rmean, ["rmean"], [1, TC])
                self.dump("rstd", self.rstd, ["rstd"], [1, TC])
            self.emit_xn(s, c, rot)
            if dbg:
                self.dump("xnb0", self.xnb[rot][0][:], [f"xnb{rot}_0"], [128, TC], BF16)
            self.emit_ffn_chunk(s, c, rot)
            if dbg:
                self.dump("xout0", self.xfm[rot][0][:], [f"xfm{rot}_0"], [128, TC])
            self.emit_chunk_out(s, c, dst, dst_tm, rot)

    def emit_ffn_chunk(self, s, c, rot):
        S = self.sched
        xnb, xfm = self.xnb[rot], self.xfm[rot]
        xn_reads = [f"xnb{rot}_{k}" for k in range(DC)]

        def f_pre(e):
            for k in range(DC):
                e.matmul(out=self.paux1[0:RK, 0:TC], lhsT=self.sATb[:, k * RK:(k + 1) * RK],
                         rhs=xnb[k][:], start=(k == 0), stop=(k == DC - 1))
            for k in range(DC):
                e.matmul(out=self.pstat[0:E, 0:TC], lhsT=self.rWTb[:, k * E:(k + 1) * E],
                         rhs=xnb[k][:], start=(k == 0), stop=(k == DC - 1))
            last = None
            for k in range(DC):
                last = e.matmul(out=self.paux2[0:E * RK, 0:TC],
                                lhsT=self.rATb[:, k * E * RK:(k + 1) * E * RK],
                                rhs=xnb[k][:], start=(k == 0), stop=(k == DC - 1))
            return last
        S.step("tensor", f_pre, reads=xn_reads + ["sATb", "rWTb", "rATb"],
               writes=["paux1", "pstat", "paux2"], tag=f"ffnpre{s}_{c}")

        S.step("scalar", lambda e: e.copy(out=self.shb[:], in_=self.paux1[0:RK, 0:TC]),
               reads=["paux1"], writes=["shb"], tag=f"shev{s}_{c}")
        if getattr(self, "dbg", False) and s == 0 and c == 0:
            self.dump("shb", self.shb[:], ["shb"], [RK, TC], BF16)

        def f_lg(v):
            return v.tensor_scalar(out=self.lrow[:], in0=self.pstat[0:E, 0:TC],
                                   scalar1=self.rbcol[:, 0:1], scalar2=None, op0=ALU.add)
        S.step("vector", f_lg, reads=["pstat", "rbcol"], writes=["lrow"], tag=f"lg{s}_{c}")
        if getattr(self, "dbg", False) and s == 0 and c == 0:
            self.dump("lrow", self.lrow[:], ["lrow"], [E, TC])

        def f_ltr(e):
            last = None
            for ts in range(4):
                last = e.transpose(out=self.pbT[:, ts * 8: ts * 8 + 6],
                                   in_=self.lrow[:, ts * 128:(ts + 1) * 128],
                                   identity=self.identf[0:6, 0:6])
            return last
        S.step("tensor", f_ltr, reads=["lrow", "identf"], writes=["pbT"], tag=f"ltr{s}_{c}")

        ltm, t2s, swtm = self.ltm, self.t2s, self.swtm

        def f_t2cp(v):
            last = None
            for ts in range(4):
                last = v.tensor_copy(out=ltm[:, ts * 8:ts * 8 + 6], in_=self.pbT[:, ts * 8:ts * 8 + 6])
            return last
        S.step("vector", f_t2cp, reads=["pbT"], writes=["ltm"], tag=f"t2cp{s}_{c}")

        def f_t2mx(v):
            last = None
            for ts in range(4):
                last = v.tensor_reduce(out=t2s[:, ts * 8: ts * 8 + 1], in_=ltm[:, ts * 8:ts * 8 + 6],
                                       axis=AX.X, op=ALU.max)
            return last
        S.step("vector", f_t2mx, reads=["ltm"], writes=["t2s"], tag=f"t2mx{s}_{c}")

        def f_t2ng(v):
            last = None
            for ts in range(4):
                m6 = t2s[:, ts * 8: ts * 8 + 1]
                last = v.tensor_scalar(out=m6, in0=m6, scalar1=-1.0, scalar2=None, op0=ALU.mult)
            return last
        S.step("vector", f_t2ng, reads=["t2s"], writes=["t2s"], tag=f"t2ng{s}_{c}")
        if getattr(self, "dbg", False) and s == 0 and c == 0:
            self.dump("ltm_a", ltm[:], ["ltm"], [128, 32])
            self.dump("t2s_a", t2s[:], ["t2s"], [128, 32])

        def f_t2e(sc):
            last = None
            for ts in range(4):
                lt = ltm[:, ts * 8:ts * 8 + 6]
                last = sc.activation(out=lt, in_=lt, func=AF.Exp,
                                     bias=t2s[:, ts * 8: ts * 8 + 1], scale=1.0)
            return last
        S.step("scalar", f_t2e, reads=["ltm", "t2s"], writes=["ltm"], tag=f"t2e{s}_{c}")
        if getattr(self, "dbg", False) and s == 0 and c == 0:
            self.dump("ltm_e", ltm[:], ["ltm"], [128, 32])

        def batch4(name, fn, reads, writes):
            def f(v):
                last = None
                for ts in range(4):
                    last = fn(v, ts)
                return last
            S.step("vector", f, reads=reads, writes=writes, tag=f"{name}{s}_{c}")

        batch4("t2sum", lambda v, ts: v.tensor_reduce(
            out=t2s[:, ts * 8 + 2: ts * 8 + 3], in_=ltm[:, ts * 8:ts * 8 + 6], axis=AX.X, op=ALU.add),
            ["ltm"], ["t2s"])
        batch4("t2rcp", lambda v, ts: v.reciprocal(
            out=t2s[:, ts * 8 + 3: ts * 8 + 4], in_=t2s[:, ts * 8 + 2: ts * 8 + 3]),
            ["t2s"], ["t2s"])
        batch4("t2gt", lambda v, ts: v.tensor_scalar(
            out=ltm[:, ts * 8:ts * 8 + 6], in0=ltm[:, ts * 8:ts * 8 + 6],
            scalar1=t2s[:, ts * 8 + 3: ts * 8 + 4], scalar2=None, op0=ALU.mult),
            ["ltm", "t2s"], ["ltm"])
        batch4("t2m1", lambda v, ts: v.tensor_reduce(
            out=t2s[:, ts * 8 + 4: ts * 8 + 5], in_=ltm[:, ts * 8:ts * 8 + 6], axis=AX.X, op=ALU.max),
            ["ltm"], ["t2s"])
        batch4("t2eq", lambda v, ts: v.tensor_scalar(
            out=swtm[:, ts * 8:ts * 8 + 6], in0=ltm[:, ts * 8:ts * 8 + 6],
            scalar1=t2s[:, ts * 8 + 4: ts * 8 + 5], scalar2=None, op0=ALU.is_equal),
            ["ltm", "t2s"], ["swtm"])
        batch4("t2g2", lambda v, ts: v.scalar_tensor_tensor(
            out=swtm[:, ts * 8:ts * 8 + 6], in0=swtm[:, ts * 8:ts * 8 + 6], scalar=-2.0,
            in1=ltm[:, ts * 8:ts * 8 + 6], op0=ALU.mult, op1=ALU.add),
            ["swtm", "ltm"], ["swtm"])
        batch4("t2m2", lambda v, ts: v.tensor_reduce(
            out=t2s[:, ts * 8 + 5: ts * 8 + 6], in_=swtm[:, ts * 8:ts * 8 + 6], axis=AX.X, op=ALU.max),
            ["swtm"], ["t2s"])
        batch4("t2kp", lambda v, ts: v.tensor_scalar(
            out=swtm[:, ts * 8:ts * 8 + 6], in0=ltm[:, ts * 8:ts * 8 + 6],
            scalar1=t2s[:, ts * 8 + 5: ts * 8 + 6], scalar2=None, op0=ALU.is_ge),
            ["ltm", "t2s"], ["swtm"])
        batch4("t2dn", lambda v, ts: v.tensor_tensor(
            out=t2s[:, ts * 8 + 6: ts * 8 + 7], in0=t2s[:, ts * 8 + 4: ts * 8 + 5],
            in1=t2s[:, ts * 8 + 5: ts * 8 + 6], op=ALU.add),
            ["t2s"], ["t2s"])
        batch4("t2dn2", lambda v, ts: v.tensor_scalar(
            out=t2s[:, ts * 8 + 6: ts * 8 + 7], in0=t2s[:, ts * 8 + 6: ts * 8 + 7],
            scalar1=1e-6, scalar2=None, op0=ALU.add),
            ["t2s"], ["t2s"])
        batch4("t2rd", lambda v, ts: v.reciprocal(
            out=t2s[:, ts * 8 + 7: ts * 8 + 8], in_=t2s[:, ts * 8 + 6: ts * 8 + 7]),
            ["t2s"], ["t2s"])
        batch4("t2sw", lambda v, ts: v.tensor_tensor(
            out=swtm[:, ts * 8:ts * 8 + 6], in0=swtm[:, ts * 8:ts * 8 + 6],
            in1=ltm[:, ts * 8:ts * 8 + 6], op=ALU.mult),
            ["swtm", "ltm"], ["swtm"])
        batch4("t2sc", lambda v, ts: v.tensor_scalar(
            out=swtm[:, ts * 8:ts * 8 + 6], in0=swtm[:, ts * 8:ts * 8 + 6],
            scalar1=t2s[:, ts * 8 + 7: ts * 8 + 8], scalar2=None, op0=ALU.mult),
            ["swtm", "t2s"], ["swtm"])
        if getattr(self, "dbg", False) and s == 0 and c == 0:
            self.dump("swtm", self.swtm[:], ["swtm"], [128, 32])

        def f_str(e):
            last = None
            for ts in range(4):
                last = e.transpose(out=self.pbT[0:6, ts * 128:(ts + 1) * 128],
                                   in_=self.swtm[:, ts * 8:ts * 8 + 6],
                                   identity=self.identf[:])
            return last
        S.step("tensor", f_str, reads=["swtm", "identf"], writes=["pbT"], tag=f"swtr{s}_{c}")

        S.step("scalar", lambda e: e.copy(out=self.swfb[:], in_=self.pbT[0:6, 0:TC]),
               reads=["pbT"], writes=["swfb"], tag=f"swev{s}_{c}")

        def f_rep(e):
            return e.matmul(out=self.paux1[0:E * RK, 0:TC], lhsT=self.rep16[:], rhs=self.swfb[:],
                            start=True, stop=True)
        S.step("tensor", f_rep, reads=["swfb", "rep16"], writes=["paux1"], tag=f"swrep{s}_{c}")

        S.step("scalar", lambda e: e.copy(out=self.swrb[:], in_=self.paux1[0:E * RK, 0:TC]),
               reads=["paux1"], writes=["swrb"], tag=f"swrev{s}_{c}")

        def f_hw(v):
            return v.tensor_tensor(out=self.hwb[:], in0=self.paux2[0:E * RK, 0:TC],
                                   in1=self.swrb[:], op=ALU.mult)
        S.step("vector", f_hw, reads=["paux2", "swrb"], writes=["hwb"], tag=f"hw{s}_{c}")
        if getattr(self, "dbg", False) and s == 0 and c == 0:
            self.dump("hwb", self.hwb[:], ["hwb"], [E * RK, TC], BF16)
            self.dump("swrb", self.swrb[:], ["swrb"], [E * RK, TC])

        for oc in range(DC):
            pm = self.pmA if oc % 2 == 0 else self.pmB
            pmname = "pmA" if oc % 2 == 0 else "pmB"

            def f_main(e, oc=oc, pm=pm):
                for k in range(DC):
                    e.matmul(out=pm[:, 0:TC], lhsT=self.baseWTb[k][:, oc * 128:(oc + 1) * 128],
                             rhs=xnb[k][:], start=(k == 0), stop=False)
                e.matmul(out=pm[:, 0:TC], lhsT=self.sBTb[:, oc * 128:(oc + 1) * 128],
                         rhs=self.shb[:], start=False, stop=False)
                return e.matmul(out=pm[:, 0:TC], lhsT=self.rBfb[:, oc * 128:(oc + 1) * 128],
                                rhs=self.hwb[:], start=False, stop=True)
            S.step("tensor", f_main,
                   reads=xn_reads + ["shb", "hwb"] + [f"baseWTb{k}" for k in range(DC)] + ["sBTb", "rBfb"],
                   writes=[pmname], tag=f"main{s}_{c}_{oc}")

            def f_ev(v, oc=oc, pm=pm):
                return v.scalar_tensor_tensor(out=xfm[oc][:], in0=pm[:, 0:TC],
                                              scalar=self.bbcol[:, oc:oc + 1], in1=xfm[oc][:],
                                              op0=ALU.add, op1=ALU.add)
            S.step("vector", f_ev, reads=[pmname, "bbcol", f"xfm{rot}_{oc}"],
                   writes=[f"xfm{rot}_{oc}"], tag=f"mainev{s}_{c}_{oc}")

    def emit_chunk_out(self, s, c, dst, dst_tm, rot):
        S = self.sched
        xfm = self.xfm[rot]
        dstap = self.dram[dst]
        if dst_tm:
            for ts in range(4):
                for hf in range(2):
                    pm, pmname = (self.pmA, "pmA") if hf == 0 else (self.pmB, "pmB")

                    def f_tr(e, ts=ts, hf=hf, pm=pm):
                        last = None
                        for i in range(4):
                            dc = hf * 4 + i
                            last = e.transpose(out=pm[:, i * 128:(i + 1) * 128],
                                               in_=xfm[dc][:, ts * 128:(ts + 1) * 128],
                                               identity=self.identf[:])
                        return last
                    S.step("tensor", f_tr, reads=[f"xfm{rot}_{k}" for k in range(DC)] + ["identf"],
                           writes=[pmname], tag=f"otr{s}_{c}_{ts}_{hf}")
                    ot = self.otm[0]
                    S.step("scalar", lambda sc, ot=ot, pm=pm: sc.copy(out=ot[:], in_=pm[:, 0:512]),
                           reads=[pmname], writes=["otm0"], tag=f"oev{s}_{c}_{ts}_{hf}")
                    S.step("sync", lambda e, s=s, c=c, ts=ts, hf=hf, ot=ot: e.dma_start(
                        out=dstap[s, c * TC + ts * 128: c * TC + (ts + 1) * 128,
                                  hf * 512:(hf + 1) * 512], in_=ot[:]),
                        reads=["otm0"], tag=f"odma{s}_{c}_{ts}_{hf}")
        else:
            for dc in range(DC):
                S.step("sync", lambda e, s=s, c=c, dc=dc: e.dma_start(
                    out=dstap[s, dc, :, c * TC:(c + 1) * TC], in_=xfm[dc][:]),
                    reads=[f"xfm{rot}_{dc}"], tag=f"odma{s}_{c}_{dc}")


def run_dev(x, weights, stages, dbg=False):
    mk = MK(stages=stages)
    mk.dbg = dbg
    nc = mk.build()
    in_map = {"x": np.ascontiguousarray(x.astype(np.float32))}
    in_map.update(prep_weights(weights))
    res = run_bass_kernel_spmd(nc, [in_map], core_ids=[0])
    return res.results[0]


def np_layernorm(xx, eps=1e-5):
    m = xx.mean(-1, keepdims=True)
    v = xx.var(-1, keepdims=True)
    return (xx - m) / np.sqrt(v + eps)




def kernel(**inputs):
    x_full = np.ascontiguousarray(np.asarray(inputs["L0_attn_a_x"], np.float32))
    wprep = prep_weights(inputs)
    mk = MK(stages=("attn", "ffn1", "ssm", "ffn2"), nseq=NSEQ)
    nc = mk.build()
    n_cores = 8
    in_maps = []
    for c in range(n_cores):
        m = {"x": x_full[c * NSEQ:(c + 1) * NSEQ]}
        m.update(wprep)
        in_maps.append(m)
    res = run_bass_kernel_spmd(nc, in_maps, core_ids=list(range(n_cores)))
    out = np.concatenate([np.asarray(r["y"]) for r in res.results], axis=0)
    return out.astype(np.float32)
